# version 16
# baseline (speedup 1.0000x reference)
import numpy as np
import ml_dtypes
import concourse.bass as bass
import concourse.mybir as mybir
from concourse.alu_op_type import AluOpType
from concourse.bass_utils import run_bass_kernel_spmd
from contextlib import ExitStack

F32 = mybir.dt.float32
BF16 = mybir.dt.bfloat16
AF = mybir.ActivationFunctionType
OP = AluOpType

L, D, DI, NS, R = 2048, 1024, 2048, 16, 64
E, CAP, DFF, F1, RH = 16, 256, 2048, 4096, 128
T = 256
NCH = L // T
NLT = L // 128
LN_EPS = 1e-5
ENGINES = ("sync", "scalar", "vector", "tensor", "gpsimd")


def ts(i, n):
    return slice(i * n, (i + 1) * n)


_PSUM_NAMES = frozenset(["P0", "P1", "P2", "P3", "P4", "P5", "P6", "PB"])


def _ap_range(ap):
    if ap.__class__.__name__.endswith("TensorHandle"):
        ap = ap[:]
    name = ap.tensor.name
    if name in _PSUM_NAMES:
        return name, 0, 1 << 30
    off = ap.offset
    lo = off
    hi = off
    for step, count in ap.ap:
        if count <= 1:
            continue
        d = step * (count - 1)
        if d >= 0:
            hi += d
        else:
            lo += d
    return name, lo, hi + 1


class Prog:

    def __init__(self):
        self.ops = {e: [] for e in ENGINES}
        self.cnt = {e: 0 for e in ENGINES}
        self.clock = {e: {c: 0 for c in ENGINES} for e in ENGINES}
        self.writers = {}
        self.readers = {}
        self.pending = {}
        self.n_sync_edges = 0

    def _deps_from(self, table, name, lo, hi, deps):
        for (l2, h2, ch, val) in table.get(name, ()):
            if l2 < hi and lo < h2:
                if deps.get(ch, 0) < val:
                    deps[ch] = val

    def emit(self, eng, fn, reads=(), writes=(), dma=False, dma_fence=False):
        deps = dict(self.pending.pop(eng, ()))
        rr = [_ap_range(a) for a in reads if a is not None]
        ww = [_ap_range(a) for a in writes if a is not None]
        for (name, lo, hi) in rr:
            self._deps_from(self.writers, name, lo, hi, deps)
        for (name, lo, hi) in ww:
            self._deps_from(self.writers, name, lo, hi, deps)
            self._deps_from(self.readers, name, lo, hi, deps)
        if eng == "tensor" and not dma:
            deps.pop(eng, None)
        kl = self.clock[eng]
        waits = {}
        for ch, val in deps.items():
            if val > kl[ch]:
                waits[ch] = val
                kl[ch] = val
        inc = 16 if dma else 1
        self.cnt[eng] += inc
        self.n_sync_edges += len(waits)
        self.ops[eng].append((fn, waits, inc))
        for (name, lo, hi) in ww:
            self.writers.setdefault(name, []).append((lo, hi, eng, self.cnt[eng]))
            wl = self.writers[name]
            if len(wl) > 96:
                del wl[:32]
        for (name, lo, hi) in rr:
            self.readers.setdefault(name, []).append((lo, hi, eng, self.cnt[eng]))
            rl = self.readers[name]
            if len(rl) > 96:
                del rl[:32]

    def barrier(self):
        snap = dict(self.cnt)
        self.writers = {}
        self.readers = {}
        self.pending = {e: dict(snap) for e in ENGINES}

    def finalize(self):
        return dict(self.cnt)


def build_program(taps=()):
    nc = bass.Bass("TRN2", target_bir_lowering=False)
    prog = Prog()
    em = prog.emit

    def din(name, shape, dt=F32):
        return nc.dram_tensor(name, list(shape), dt, kind="ExternalInput").ap()

    x0 = din("x", (L, D))
    W_in = din("W_in", (2 * DI, D))
    conv_w = din("conv_w", (128, 16, 4))
    conv_b = din("conv_b", (128, 16))
    W_x = din("W_x", (R + 2 * NS, DI))
    W_dt = din("W_dt", (128, 16, R))
    b_dt = din("b_dt", (128, 16))
    A_log = din("A_log", (128, 16, NS))
    Dp = din("Dp", (128, 16))
    W_out1 = din("W_out1", (DI, 2 * DI))
    W_out2 = din("W_out2", (D, DI))
    r_w1 = din("router_w1", (128, 8, RH))
    r_w2 = din("router_w2", (RH, E))
    moe_w1 = din("moe_w1", (E, D, DFF))
    moe_w2 = din("moe_w2", (E, D, DFF))
    moe_w3 = din("moe_w3", (E, DFF, D))
    l1_lnw = din("l1_lnw", (D,))
    l1_lnb = din("l1_lnb", (D,))
    l1_ip = din("l1_ip", (2 * DI, D))
    l1_cw = din("l1_cw", (128, 16, 4))
    l1_cb = din("l1_cb", (128, 16))
    l1_xp = din("l1_xp", (R + 2 * NS, DI))
    l1_dtp = din("l1_dtp", (128, 16, R))
    l1_dtb = din("l1_dtb", (128, 16))
    l1_Alog = din("l1_Alog", (128, 16, NS))
    l1_D = din("l1_D", (128, 16))
    l1_op = din("l1_op", (D, DI))
    w_up = din("w_up", (F1, D))
    w_down = din("w_down", (D, F1))
    identf = din("identf", (128, 128))
    identb = din("identb", (128, 128), BF16)
    oh16f_in = din("oh16f", (16, 16 * 128))
    iota_cr = din("iota_cr", (128, CAP))
    iota_cc = din("iota_cc", (128, 2))
    ones1f = din("ones1f", (1, 128))
    onescb = din("onescb", (128, 1), BF16)

    y_out = nc.dram_tensor("y", [L, D], F32, kind="ExternalOutput").ap()

    def dscr(name, shape, dt=F32):
        return nc.dram_tensor(name, list(shape), dt, kind="Internal").ap()

    xres1 = dscr("xres1", (L, D))
    xres2 = dscr("xres2", (L, D))
    xres3 = dscr("xres3", (L, D))
    ycat_s = dscr("ycat_s", (32, 128, L), BF16)
    hn_s = dscr("hn_s", (32, 128, L), BF16)
    s_WinT = dscr("s_WinT", (D, 2 * DI), BF16)
    s_Wo1T = dscr("s_Wo1T", (2 * DI, DI), BF16)
    s_Wo2T = dscr("s_Wo2T", (DI, D), BF16)
    s_ipT = dscr("s_ipT", (D, 2 * DI), BF16)
    s_opT = dscr("s_opT", (DI, D), BF16)

    tap_t = {}

    def tap(name, shape, dt=F32):
        if name not in taps:
            return None
        if name not in tap_t:
            tap_t[name] = nc.dram_tensor("tap_" + name, list(shape), dt,
                                         kind="ExternalOutput").ap()
        return tap_t[name]

    stack = ExitStack()

    def sbp(name, shape, dt=F32):
        return stack.enter_context(nc.sbuf_tensor(name, list(shape), dt))

    ident_f = sbp("ident_f", (128, 128))
    ident_b = sbp("ident_b", (128, 128), BF16)
    ones1_f = sbp("ones1_f", (1, 128))
    ones_cb = sbp("ones_cb", (128, 1), BF16)
    oh16_f = sbp("oh16_f", (16, 16 * 128))
    carry = sbp("carry", (128, 16 * NS))
    stat = sbp("stat", (128, 20))
    srow = sbp("srow", (16, 192))

    psum = [stack.enter_context(nc.psum_tensor("P%d" % i, [128, 512], F32)) for i in range(7)]
    psum_b = stack.enter_context(nc.psum_tensor("PB", [128, 1024], BF16))

    def _ap(x):
        if x is not None and x.__class__.__name__.endswith("TensorHandle"):
            return x[:]
        return x

    def E_dma(eng, out, in_):
        out, in_ = _ap(out), _ap(in_)
        em(eng, (lambda e, out=out, in_=in_: e.dma_start(out=out, in_=in_)), dma=True,
           reads=[in_], writes=[out])

    def E_dmaT(out, in_):
        out, in_ = _ap(out), _ap(in_)
        em("sync", (lambda e, out=out, in_=in_: e.dma_start(out=out, in_=in_, transpose=True)),
           dma=True, reads=[in_], writes=[out])

    def E_act(out, in_, func, bias=None, scale=None, eng="scalar"):
        out, in_ = _ap(out), _ap(in_)
        kw = {}
        rds = [in_]
        if bias is not None:
            kw["bias"] = _ap(bias)
            if not isinstance(bias, float):
                rds.append(kw["bias"])
        if scale is not None:
            kw["scale"] = _ap(scale)
            if not isinstance(scale, float):
                rds.append(kw["scale"])
        em(eng, (lambda e, out=out, in_=in_, func=func, kw=kw:
                 e.activation(out=out, in_=in_, func=func, **kw)), reads=rds, writes=[out])

    def E_tt(out, in0, in1, op, eng="vector"):
        out, in0, in1 = _ap(out), _ap(in0), _ap(in1)
        em(eng, (lambda e, out=out, in0=in0, in1=in1, op=op:
                 e.tensor_tensor(out=out, in0=in0, in1=in1, op=op)),
           reads=[in0, in1], writes=[out])

    def E_ts(out, in0, s1, op0, s2=None, op1=None, accum_out=None, eng="vector"):
        out, in0, s1, s2 = _ap(out), _ap(in0), _ap(s1), _ap(s2)
        kw = {}
        rds = [in0]
        wrs = [out]
        for s in (s1, s2):
            if s is not None and not isinstance(s, float):
                rds.append(s)
        if op1 is not None:
            kw["op1"] = op1
        if accum_out is not None:
            kw["accum_out"] = _ap(accum_out)
            wrs.append(kw["accum_out"])
        em(eng, (lambda e, out=out, in0=in0, s1=s1, s2=s2, op0=op0, kw=kw:
                 e.tensor_scalar(out=out, in0=in0, scalar1=s1, scalar2=s2, op0=op0, **kw)),
           reads=rds, writes=wrs)

    def E_stt(out, in0, scalar, in1, op0, op1, eng="vector"):
        out, in0, scalar, in1 = _ap(out), _ap(in0), _ap(scalar), _ap(in1)
        rds = [in0, in1]
        if not isinstance(scalar, float):
            rds.append(scalar)
        em(eng, (lambda e, out=out, in0=in0, sc=scalar, in1=in1, op0=op0, op1=op1:
                 e.scalar_tensor_tensor(out=out, in0=in0, scalar=sc, in1=in1, op0=op0, op1=op1)),
           reads=rds, writes=[out])

    def E_red(out, in_, op, absval=False, eng="vector"):
        out, in_ = _ap(out), _ap(in_)
        kw = {"apply_absolute_value": True} if absval else {}
        em(eng, (lambda e, out=out, in_=in_, op=op, kw=kw:
                 e.tensor_reduce(out=out, in_=in_, axis=mybir.AxisListType.X, op=op, **kw)),
           reads=[in_], writes=[out])

    def E_scan(out, d0, d1, initial):
        out, d0, d1, initial = _ap(out), _ap(d0), _ap(d1), _ap(initial)
        rds = [d0, d1]
        if not isinstance(initial, float):
            rds.append(initial)
        em("vector", (lambda e, out=out, d0=d0, d1=d1, init=initial:
                      e.tensor_tensor_scan(out=out, data0=d0, data1=d1, initial=init,
                                           op0=OP.mult, op1=OP.add)),
           reads=rds, writes=[out])

    def E_mm(out, lhsT, rhs, start, stop):
        out, lhsT, rhs = _ap(out), _ap(lhsT), _ap(rhs)
        em("tensor", (lambda e, out=out, l=lhsT, r=rhs, st=start, sp=stop:
                      e.matmul(out=out, lhsT=l, rhs=r, start=st, stop=sp)),
           reads=[lhsT, rhs], writes=[out])

    def E_tr(out, in_, identity):
        out, in_, identity = _ap(out), _ap(in_), _ap(identity)
        k = in_.shape[0]
        if identity.shape[0] != k:
            identity = identity[0:k, 0:k]
        em("tensor", (lambda e, out=out, in_=in_, idn=identity:
                      e.transpose(out=out, in_=in_, identity=idn)),
           reads=[in_, identity], writes=[out])

    def E_copy(out, in_, eng="scalar"):
        out, in_ = _ap(out), _ap(in_)
        if eng == "scalar":
            E_act(out, in_, AF.Copy)
        else:
            em(eng, (lambda e, out=out, in_=in_: e.tensor_copy(out=out, in_=in_)),
               reads=[in_], writes=[out])

    def E_memset(ap, v, eng="vector"):
        ap = _ap(ap)
        em(eng, (lambda e, ap=ap, v=v: e.memset(ap, v)), writes=[ap])

    def E_recip(out, in_):
        out, in_ = _ap(out), _ap(in_)
        em("vector", (lambda e, out=out, in_=in_: e.reciprocal(out=out, in_=in_)),
           reads=[in_], writes=[out])

    def rsqrt2(out, v, t1):
        E_recip(t1, v)
        E_act(out, t1, AF.Sqrt)

    def E_bnstats(out, in_):
        out, in_ = _ap(out), _ap(in_)
        em("vector", (lambda e, out=out, in_=in_: e.bn_stats(out=out, in_=in_)),
           reads=[in_], writes=[out])

    def E_bnaggr(out, in_):
        out, in_ = _ap(out), _ap(in_)
        em("vector", (lambda e, out=out, in_=in_: e.bn_aggr(out=out, in_=in_)),
           reads=[in_], writes=[out])

    def ln_stats(x_t, sq, Dd=D, eps=LN_EPS):
        m, var, rs = stat[:, 0:1], stat[:, 1:2], stat[:, 2:3]
        t1 = stat[:, 3:4]
        nh = Dd // 512
        for h in range(nh):
            E_bnstats(stat[:, 4 + 6 * h:10 + 6 * h], x_t[:, ts(h, 512)])
        E_bnaggr(stat[:, 0:2], stat[:, 4:4 + 6 * nh])
        E_ts(t1, var, 1.0, OP.mult, eps, OP.add)
        rsqrt2(rs, t1, stat[:, 3:4]) if False else None
        E_recip(stat[:, 3:4], t1)
        E_act(rs, stat[:, 3:4], AF.Sqrt)

    def prep_T(w_ap, o_dim, i_dim, dst, st_nat, st_row):
        OSUP = 1024
        k = [0]
        for ib in range(i_dim // 128):
            rn = ib % 2
            for osb in range(o_dim // OSUP):
                src = w_ap[ts(osb, OSUP), ts(ib, 128)].rearrange("(a p) i -> p a i", p=128)
                E_dma("gpsimd", st_nat[:, rn, :, :], src)
                for a in range(OSUP // 128):
                    o0 = osb * OSUP + a * 128
                    pmt = psum[k[0] % 4][:, 0:128]
                    E_tr(pmt, st_nat[:, rn, a, :], ident_f[:])
                    E_copy(st_row[:, rn, o0:o0 + 128], pmt,
                           eng=("scalar" if k[0] % 2 else "vector"))
                    k[0] += 1
            E_dma("scalar", dst[ts(ib, 128), :], st_row[:, rn, 0:o_dim])

    E_dma("sync", ident_f[:], identf)
    with nc.sbuf_tensor("pr_nat", [128, 2, 8, 128], F32) as st_nat, \
         nc.sbuf_tensor("pr_row", [128, 2, 4096], BF16) as st_row:
        prep_T(W_in, 2 * DI, D, s_WinT, st_nat, st_row)
        prep_T(W_out1, DI, 2 * DI, s_Wo1T, st_nat, st_row)
        prep_T(W_out2, D, DI, s_Wo2T, st_nat, st_row)
        prep_T(l1_ip, 2 * DI, D, s_ipT, st_nat, st_row)
        prep_T(l1_op, D, DI, s_opT, st_nat, st_row)
    prog.barrier()

    E_dma("sync", ident_b[:], identb)
    E_dma("sync", ones1_f[:], ones1f)
    E_dma("sync", ones_cb[:], onescb)
    E_dma("sync", oh16_f[:], oh16f_in)

    def mamba_block(which):
        l0 = which == 0
        sfx = "_m%d" % which
        x_in = x0 if l0 else xres2
        winT_scr = s_WinT if l0 else s_ipT
        w_x_in = W_x if l0 else l1_xp
        w_dt_in = W_dt if l0 else l1_dtp
        cw_in = conv_w if l0 else l1_cw
        cb_in = conv_b if l0 else l1_cb
        bdt_in = b_dt if l0 else l1_dtb
        alog_in = A_log if l0 else l1_Alog
        dp_in = Dp if l0 else l1_D

        es = ExitStack()

        def s(name, shape, dt=F32):
            return es.enter_context(nc.sbuf_tensor(name + sfx, list(shape), dt))

        cwc = s("cwc", (128, 16, 4))
        cbc = s("cbc", (128, 16))
        bdtc = s("bdtc", (128, 16))
        dpc = s("dpc", (128, 16))
        ac = s("ac", (128, 16 * NS))
        wxT = s("wxT", (128, 16, 96), BF16)
        wdtT = s("wdtT", (64, DI), BF16)
        E_dma("sync", cwc[:], cw_in)
        E_dma("sync", cbc[:], cb_in)
        E_dma("sync", bdtc[:], bdt_in)
        E_dma("sync", dpc[:], dp_in)
        E_dma("sync", ac[:].rearrange("p (a n) -> p a n", n=NS), alog_in)
        E_act(ac[:], ac[:], AF.Exp)
        E_ts(ac[:], ac[:], -1.0, OP.mult)
        if l0:
            dpb = s("dpb", (128, 16), BF16)
            atb = s("atb", (16, DI), BF16)
            E_copy(dpb[:], dpc[:])
            for db in range(16):
                E_tr(psum[6][0:NS, 0:128], ac[:, ts(db, NS)], ident_f[:])
                E_copy(atb[:, ts(db, 128)], psum[6][0:NS, 0:128])
        else:
            lnw_bc = s("lnwbc", (128, D), BF16)
            lnb_bc = s("lnbbc", (128, D), BF16)
            with nc.sbuf_tensor("lnr" + sfx, [1, 2 * D], F32) as lnr:
                E_dma("sync", lnr[:, 0:D], l1_lnw.rearrange("(a d) -> a d", a=1))
                E_dma("sync", lnr[:, D:2 * D], l1_lnb.rearrange("(a d) -> a d", a=1))
                for i, dstt in ((0, lnw_bc), (1, lnb_bc)):
                    for j in range(2):
                        E_mm(psum[6][:, 0:512], ones1_f[:],
                             lnr[:, i * D + j * 512: i * D + (j + 1) * 512], start=True, stop=True)
                        E_copy(dstt[:, ts(j, 512)], psum[6][:, 0:512])
            prog.barrier()
        with nc.sbuf_tensor("smw" + sfx, [96, DI], BF16) as smw:
            E_dma("gpsimd", smw[:], w_x_in)
            for ib in range(16):
                E_tr(psum_b[:, 0:96], smw[:, ts(ib, 128)], ident_b[:])
                E_copy(wxT[:, ib, :], psum_b[:, 0:96])
        prog.barrier()
        with nc.sbuf_tensor("smd" + sfx, [128, 16, R], BF16) as smd:
            E_dma("gpsimd", smd[:], w_dt_in)
            for ib in range(16):
                E_tr(psum_b[0:R, 0:128], smd[:, ib, :], ident_b[:])
                E_copy(wdtT[:, ts(ib, 128)], psum_b[0:R, 0:128])
        prog.barrier()

        xf = s("xf", (128, D))
        sq = s("sq", (128, D))
        xT = s("xT", (128, 8, T), BF16)
        winP = s("winP", (128, 8, 512), BF16)
        xih = s("xih", (128, 16, T + 3))
        halo = s("halo", (128, 16, 3))
        u_f = s("u", (128, 16, T))
        ub = s("ub", (128, 16, T), BF16) if l0 else s("ub", (128, 2, T), BF16)
        sz = s("sz", (128, 2, 16, T), BF16)
        dtrb = s("dtrb", (64, T), BF16)
        btf = s("btf", (16, T))
        ctf = s("ctf", (16, T))
        btil = s("btil", (128, NS, T))
        ctil = s("ctil", (128, NS, T), BF16)
        dl_all = s("dl_all", (128, 16, T))
        sp_s = s("sp_s", (128, 2, T))
        dA_all = s("dA_all", (128, NS, T))
        bsc_all = s("bsc_all", (128, NS, T))
        du = s("du", (128, T))
        s_all = s("s_all", (128, NS, T))
        sC = s("sC", (128, NS, T), BF16)
        t1 = s("t1", (128, T))
        ycb = s("ycb", (128, 2, T), BF16)
        if l0:
            ctb = s("ctb", (16, T), BF16)
            sdu = btf[0:1, :]
        else:
            ytb = s("ytb", (128, 16, T), BF16)
            opt_t = s("optt", (128, 1, D), BF16)

        tp_xn = tap("xn%d" % which, (L, D))
        tp_u = tap("u%d" % which, (128, 16, T))
        tp_dl = tap("dl%d" % which, (128, 16, T))
        tp_bt = tap("bt%d" % which, (16, T))
        tp_sz = tap("sz%d" % which, (128, 16, T))
        tp_s0 = tap("s%d" % which, (128, NS, T))

        E_memset(carry[:], 0.0)

        def ln_xz(c):
            t0 = c * T
            for lt in range(2):
                row0 = t0 + lt * 128
                E_dma("sync", xf[:], x_in[row0:row0 + 128, :])
                ln_stats(xf, sq)
                if not l0:
                    var = stat[:, 1:2]
                    t1c, t2c, vt = stat[:, 10:11], stat[:, 11:12], stat[:, 12:13]
                    E_ts(t1c, var, 1.0, OP.mult, LN_EPS, OP.add)
                    E_recip(t2c, t1c)
                    E_tt(vt, var, t2c, OP.mult)
                    E_ts(vt, vt, 25.0, OP.mult, LN_EPS, OP.add)
                    rsqrt2(t2c, vt, stat[:, 13:14])
                    E_ts(t2c, t2c, 5.0, OP.mult)
                    E_tt(stat[:, 2:3], stat[:, 2:3], t2c, OP.mult)
                E_ts(sq[:, 0:D], xf, stat[:, 0:1], OP.subtract, stat[:, 2:3], OP.mult)
                if not l0:
                    E_tt(sq[:, 0:D], sq[:, 0:D], lnw_bc[:], OP.mult)
                    E_tt(sq[:, 0:D], sq[:, 0:D], lnb_bc[:], OP.add)
                if tp_xn is not None:
                    E_dma("gpsimd", tp_xn[row0:row0 + 128, :], sq[:, 0:D])
                for ib in range(8):
                    pmt = psum[4 + ib % 2][:, 0:128]
                    E_tr(pmt, sq[:, ts(ib, 128)], ident_f[:])
                    E_copy(xT[:, ib, lt * 128:(lt + 1) * 128], pmt)
            for q in range(8):
                for ib in range(8):
                    E_dma("scalar", winP[:, ib, :], winT_scr[ts(ib, 128), q * 512:(q + 1) * 512])
                for eq in range(4):
                    et = q * 4 + eq
                    pm = psum[eq % 2][:, 0:T]
                    for ib in range(8):
                        E_mm(pm, winP[:, ib, ts(eq, 128)], xT[:, ib, :],
                             start=(ib == 0), stop=(ib == 7))
                    if et < 16:
                        if c == 0:
                            E_memset(xih[:, et, 0:3], 0.0)
                        else:
                            E_copy(xih[:, et, 0:3], halo[:, et, :])
                        E_copy(xih[:, et, 3:T + 3], pm)
                    else:
                        E_act(sz[:, c % 2, et - 16, :], pm, AF.Silu)

        ln_xz(0)
        for c in range(NCH):
            t0 = c * T
            for k in range(4):
                for et in range(16):
                    tconv = dA_all[:, et, :]
                    if k == 0:
                        E_ts(tconv, xih[:, et, 0:T], cwc[:, et, 0:1], OP.mult)
                    else:
                        E_stt(tconv, xih[:, et, k:k + T], cwc[:, et, k:k + 1], tconv,
                              OP.mult, OP.add)
            for et in range(16):
                E_copy(halo[:, et, :], xih[:, et, T:T + 3])
                E_act(u_f[:, et, :], dA_all[:, et, :], AF.Silu, bias=cbc[:, et:et + 1])
                if l0:
                    E_copy(ub[:, et, :], u_f[:, et, :])
            if c == 0 and tp_u is not None:
                E_dma("gpsimd", tp_u, u_f[:])
            if c == 0 and tp_sz is not None:
                E_dma("gpsimd", tp_sz, sz[:, 0])
            for ib in range(16):
                if l0:
                    ubt = ub[:, ib, :]
                else:
                    ubt = ub[:, ib % 2, :]
                    E_copy(ubt, u_f[:, ib, :])
                E_mm(psum[0][0:64, 0:T], wxT[:, ib, 0:64], ubt, start=(ib == 0), stop=(ib == 15))
                E_mm(psum[1][0:16, 0:T], wxT[:, ib, 64:80], ubt, start=(ib == 0), stop=(ib == 15))
                E_mm(psum[2][0:16, 0:T], wxT[:, ib, 80:96], ubt, start=(ib == 0), stop=(ib == 15))
            E_copy(dtrb[:], psum[0][0:64, 0:T])
            E_copy(btf[:], psum[1][0:16, 0:T])
            E_copy(ctf[:], psum[2][0:16, 0:T])
            if l0:
                E_copy(ctb[:], psum[2][0:16, 0:T])
            if c == 0 and tp_bt is not None:
                E_dma("gpsimd", tp_bt, btf[:])
            for n in range(NS):
                pm = psum[0 + n % 2][:, 0:T]
                E_mm(pm, oh16_f[:, ts(n, 128)], btf[:], start=True, stop=True)
                E_copy(btil[:, n, :], pm)
                pm2 = psum[2 + n % 2][:, 0:T]
                E_mm(pm2, oh16_f[:, ts(n, 128)], ctf[:], start=True, stop=True)
                E_copy(ctil[:, n, :], pm2, eng="vector")
            for db in range(16):
                pm = psum[4 + db % 2][:, 0:T]
                sp = sp_s[:, db % 2, :]
                dl = dl_all[:, db, :]
                E_mm(pm, wdtT[:, ts(db, 128)], dtrb[:], start=True, stop=True)
                E_act(sp, pm, AF.Abs, bias=bdtc[:, db:db + 1])
                E_act(sp, sp, AF.Exp, scale=-1.0)
                E_act(sp, sp, AF.Ln, bias=1.0)
                E_act(dl, pm, AF.Relu, bias=bdtc[:, db:db + 1])
                E_tt(dl, dl, sp, OP.add, eng="gpsimd")
            if c == 0 and tp_dl is not None:
                E_dma("gpsimd", tp_dl, dl_all[:])
            if c + 1 < NCH:
                ln_xz(c + 1)
            def tail(db):
                r = db % 2
                py = psum[2 + r][:, 0:T]
                E_stt(t1[:], u_f[:, db, :], dpc[:, db:db + 1], py, OP.mult, OP.add)
                E_tt(ycb[:, r, :], t1[:], sz[:, c % 2, db, :], OP.mult)
                if l0:
                    E_dma("sync", ycat_s[db, :, t0:t0 + T], ycb[:, r, :])
                else:
                    E_copy(ytb[:, db, :], ycb[:, r, :], eng="vector")

            for db in range(16):
                r = db % 2
                dl = dl_all[:, db, :]
                dub = du[:]
                E_tt(dub, dl, u_f[:, db, :], OP.mult, eng="gpsimd")
                dub_b = dub.rearrange("p (a t) -> p a t", a=1).broadcast_to([128, 4, T])
                for g in range(4):
                    em("gpsimd", (lambda e, out=bsc_all[:, 4 * g:4 * g + 4, :], in0=dub_b,
                                  in1=btil[:, 4 * g:4 * g + 4, :]:
                                  e.tensor_tensor(out=out, in0=in0, in1=in1, op=OP.mult)),
                       reads=[dub, btil[:, 4 * g:4 * g + 4, :]],
                       writes=[bsc_all[:, 4 * g:4 * g + 4, :]])
                py = psum[2 + r][:, 0:T]
                for n in range(NS):
                    E_act(dA_all[:, n, :], dl, AF.Exp, scale=ac[:, db * NS + n:db * NS + n + 1])
                for n in range(NS):
                    E_scan(s_all[:, n, :], dA_all[:, n, :], bsc_all[:, n, :],
                           carry[:, db * NS + n:db * NS + n + 1])
                E_copy(carry[:, ts(db, NS)], s_all[:, :, T - 1], eng="vector")
                E_tt(sC[:], s_all[:], ctil[:], OP.mult)
                for n in range(NS):
                    E_mm(py, ident_b[:], sC[:, n, :], start=(n == 0), stop=(n == NS - 1))
                if c == 0 and db == 0 and tp_s0 is not None:
                    E_dma("gpsimd", tp_s0, s_all[:])
                if db > 0:
                    tail(db - 1)
            tail(15)
            if l0:
                psd = psum[6][0:1, 0:T]
                for db in range(16):
                    E_mm(psd, dpb[:, db:db + 1], ub[:, db, :], start=(db == 0), stop=(db == 15))
                E_copy(sdu[:], psd)
                for db2 in range(16):
                    pm = psum[2 + db2 % 2][:, 0:T]
                    E_mm(pm, atb[:, ts(db2, 128)], ctb[:], start=True, stop=False)
                    E_mm(pm, ones1_f[:], sdu[:], start=False, stop=True)
                    E_copy(ycb[:, db2 % 2, :], pm)
                    E_dma("sync", ycat_s[16 + db2, :, t0:t0 + T], ycb[:, db2 % 2, :])
            else:
                pm00 = psum[2][:, 0:512]
                pm01 = psum[3][:, 0:512]
                pm10 = psum[6][:, 0:512]
                pm11 = psum[5][:, 0:512]
                for db in range(16):
                    E_dma("sync", opt_t[:, 0, :], s_opT[ts(db, 128), :])
                    st = (db == 0)
                    sp2 = (db == 15)
                    E_mm(pm00, ytb[:, db, 0:128], opt_t[:, 0, 0:512], start=st, stop=sp2)
                    E_mm(pm01, ytb[:, db, 0:128], opt_t[:, 0, 512:1024], start=st, stop=sp2)
                    E_mm(pm10, ytb[:, db, 128:256], opt_t[:, 0, 0:512], start=st, stop=sp2)
                    E_mm(pm11, ytb[:, db, 128:256], opt_t[:, 0, 512:1024], start=st, stop=sp2)
                for lsub in range(2):
                    row0 = t0 + lsub * 128
                    for dh in range(2):
                        pm = (pm00, pm01, pm10, pm11)[lsub * 2 + dh]
                        rb = xf[:, ts(dh, 512)]
                        E_dma("sync", rb, xres2[row0:row0 + 128, ts(dh, 512)])
                        E_tt(rb, rb, pm, OP.add)
                        E_dma("sync", xres3[row0:row0 + 128, ts(dh, 512)], rb)
        es.close()
        prog.barrier()

    def l0_tail():
        es = ExitStack()

        def s(name, shape, dt=F32):
            return es.enter_context(nc.sbuf_tensor(name, list(shape), dt))

        h1T = s("t_h1T", (128, 16, L), BF16)
        wo2 = s("t_wo2", (128, 16, D), BF16)
        wo1 = s("t_wo1", (128, 32, 1024), BF16)
        yc = s("t_yc", (128, 2, 32, 256), BF16)
        xs = s("t_xs", (128, 512))
        o2 = s("t_o2", (128, 512))
        for ib in range(16):
            E_dma("sync", wo2[:, ib, :], s_Wo2T[ts(ib, 128), :])
        for fh in range(2):
            for cb in range(32):
                E_dma("sync", wo1[:, cb, :], s_Wo1T[ts(cb, 128), fh * 1024:(fh + 1) * 1024])
            for lsl in range(8):
                for cb in range(32):
                    E_dma("sync", yc[:, lsl % 2, cb, :], ycat_s[cb, :, lsl * 256:(lsl + 1) * 256])
                for ft in range(8):
                    pm = psum[ft % 2][:, 0:256]
                    for cb in range(32):
                        E_mm(pm, wo1[:, cb, ts(ft, 128)], yc[:, lsl % 2, cb, :],
                             start=(cb == 0), stop=(cb == 31))
                    E_act(h1T[:, fh * 8 + ft, lsl * 256:(lsl + 1) * 256], pm, AF.Relu)
        for lt in range(NLT):
            for dh in range(2):
                pm = psum[2 + dh][:, 0:512]
                for ft in range(16):
                    E_mm(pm, h1T[:, ft, lt * 128:(lt + 1) * 128], wo2[:, ft, ts(dh, 512)],
                         start=(ft == 0), stop=(ft == 15))
                E_dma("sync", xs[:], x0[ts(lt, 128), ts(dh, 512)])
                E_tt(o2[:], xs[:], pm, OP.add)
                E_dma("sync", xres1[ts(lt, 128), ts(dh, 512)], o2[:])
        es.close()
        prog.barrier()

    def moe_block():
        es = ExitStack()

        def s(name, shape, dt=F32):
            return es.enter_context(nc.sbuf_tensor(name, list(shape), dt))

        iotacr = s("m_iotacr", (128, CAP))
        iotacc = s("m_iotacc", (128, 2))
        E_dma("sync", iotacr[:], iota_cr)
        E_dma("sync", iotacc[:], iota_cc)
        xn_tok = s("m_xn", (128, 16, D), BF16)
        xf = s("m_xf", (128, D))
        rankm = s("m_rankm", (16, L))
        rmT = s("m_rmT", (128, 16, 16))
        pT = s("m_pT", (128, 16, 16), BF16)
        routing_es = ExitStack()
        probs = routing_es.enter_context(nc.sbuf_tensor("m_probs", [16, L], F32))
        cmp_s = routing_es.enter_context(nc.sbuf_tensor("m_cmp", [16, L], F32))
        ones_r = routing_es.enter_context(nc.sbuf_tensor("m_ones", [16, L], F32))
        lo, hi, th = srow[:, 0:1], srow[:, 1:2], srow[:, 2:3]
        cntc, pred, dnm = srow[:, 3:4], srow[:, 4:5], srow[:, 5:6]
        mx, sm = srow[:, 6:7], srow[:, 7:8]
        tp_pr = tap("probs", (16, L))
        tp_rk = tap("rankm", (16, L))
        tp_ein = tap("ein", (128, 8, CAP), BF16)
        tp_gv = tap("gv", (128, 16, CAP), BF16)

        with nc.sbuf_tensor("m_xnT", [128, 8, L], BF16) as xnT, \
             nc.sbuf_tensor("m_hrT", [128, L], BF16) as hrT, \
             nc.sbuf_tensor("m_sq", [128, D], F32) as sq, \
             nc.sbuf_tensor("m_rw1", [128, 8, RH], BF16) as rw1b, \
             nc.sbuf_tensor("m_rw2", [128, E], BF16) as rw2b:
            E_dma("gpsimd", rw1b[:], r_w1)
            E_dma("gpsimd", rw2b[:], r_w2)
            for lt in range(NLT):
                E_dma("sync", xf[:], xres1[ts(lt, 128), :])
                ln_stats(xf, sq)
                E_ts(xn_tok[:, lt, :], xf, stat[:, 0:1], OP.subtract, stat[:, 2:3], OP.mult)
                E_copy(sq[:, 0:D], xn_tok[:, lt, :])
                for ib in range(8):
                    E_tr(psum[6][:, 0:128], sq[:, ts(ib, 128)], ident_f[:])
                    E_copy(xnT[:, ib, ts(lt, 128)], psum[6][:, 0:128])
            for j in range(4):
                pm = psum[j % 2][:, 0:512]
                for ib in range(8):
                    E_mm(pm, rw1b[:, ib, :], xnT[:, ib, ts(j, 512)], start=(ib == 0), stop=(ib == 7))
                E_act(hrT[:, ts(j, 512)], pm, AF.Silu)
            for j in range(4):
                pm = psum[2 + j % 2][0:16, 0:512]
                E_mm(pm, rw2b[:], hrT[:, ts(j, 512)], start=True, stop=True)
                E_copy(probs[:, ts(j, 512)], pm)
        prog.barrier()
        E_red(mx, probs[:], OP.max)
        E_ts(mx, mx, -1.0, OP.mult)
        E_act(probs[:], probs[:], AF.Exp, bias=mx)
        E_red(sm, probs[:], OP.add)
        E_recip(sm, sm)
        E_ts(probs[:], probs[:], sm, OP.mult)
        if tp_pr is not None:
            E_dma("gpsimd", tp_pr, probs[:])
        E_memset(lo, 0.0)
        E_memset(hi, 1.1)
        tb1, tb2 = srow[:, 8:9], srow[:, 9:10]
        for _ in range(44):
            E_tt(th, lo, hi, OP.add)
            E_ts(th, th, 0.5, OP.mult)
            E_ts(cmp_s[:], probs[:], th, OP.is_ge, op1=OP.add, accum_out=cntc)
            E_ts(pred, cntc, float(CAP) - 0.5, OP.is_ge)
            E_tt(tb1, th, lo, OP.subtract)
            E_tt(tb2, hi, th, OP.subtract)
            E_stt(lo, tb1, pred, lo, OP.mult, OP.add)
            E_stt(hi, tb2, pred, th, OP.mult, OP.add)
        E_ts(cmp_s[:], probs[:], lo, OP.is_ge)
        E_memset(ones_r[:], 1.0)
        E_scan(rankm[:], ones_r[:], cmp_s[:], 0.0)
        E_tt(rankm[:], rankm[:], cmp_s[:], OP.mult)
        E_ts(cmp_s[:], rankm[:], float(CAP) + 0.5, OP.is_le)
        E_tt(rankm[:], rankm[:], cmp_s[:], OP.mult)
        if tp_rk is not None:
            E_dma("gpsimd", tp_rk, rankm[:])
        E_ts(cmp_s[:], rankm[:], 0.5, OP.is_ge)
        E_tt(cmp_s[:], cmp_s[:], probs[:], OP.mult)
        E_red(dnm, cmp_s[:], OP.add)
        E_ts(dnm, dnm, 1.0, OP.mult, 1e-9, OP.add)
        E_recip(dnm, dnm)
        for lt in range(NLT):
            E_tr(psum[6][:, 0:16], rankm[:, ts(lt, 128)], ident_f[:])
            E_copy(rmT[:, lt, :], psum[6][:, 0:16])
            E_tr(psum[6][:, 0:16], probs[:, ts(lt, 128)], ident_f[:])
            E_copy(pT[:, lt, :], psum[6][:, 0:16])
        routing_es.close()
        prog.barrier()

        wgt = s("m_wgt", (128, 16, D), BF16)
        geT = s("m_geT", (128, 16, CAP), BF16)
        einT = s("m_einT", (128, 8, CAP), BF16)
        w12 = s("m_w12", (128, 16, DFF), BF16)
        w3f = s("m_w3f", (128, 16, D), BF16)
        gvb = s("m_gv", (128, 16, CAP), BF16)
        sg = s("m_sg", (128, CAP))
        psel = sg[0:16, :]
        wcol = s("m_wcol", (128, 2 * E))
        gsc = s("m_gsc", (128, 2, 128), BF16)
        for grp in range(2):
            for ei in range(8):
                e = grp * 8 + ei
                for lt in range(NLT):
                    E_ts(geT[:, lt, :], iotacr[:], rmT[:, lt, e:e + 1], OP.is_equal)
                for dt8 in range(8):
                    pm = psum[dt8 % 2][:, 0:CAP]
                    for lt in range(NLT):
                        E_mm(pm, xn_tok[:, lt, ts(dt8, 128)], geT[:, lt, :],
                             start=(lt == 0), stop=(lt == 15))
                    E_copy(einT[:, dt8, :], pm)
                if tp_ein is not None and e == 0:
                    E_dma("gpsimd", tp_ein, einT[:])
                pm = psum[2][0:16, 0:CAP]
                for lt in range(NLT):
                    E_mm(pm, pT[:, lt, :], geT[:, lt, :], start=(lt == 0), stop=(lt == 15))
                E_copy(psel[:], pm)
                E_ts(psel[:], psel[:], dnm, OP.mult)
                for ch in range(2):
                    E_tr(psum[6][:, 0:16], psel[:, ts(ch, 128)], ident_f[:])
                    E_copy(wcol[:, 2 * e + ch:2 * e + ch + 1], psum[6][:, e:e + 1])
                E_dma("gpsimd", w12[:, 0:8, :], moe_w1[e].rearrange("(a p) f -> p a f", p=128))
                E_dma("gpsimd", w12[:, 8:16, :], moe_w2[e].rearrange("(a p) f -> p a f", p=128))
                E_dma("gpsimd", w3f[:], moe_w3[e].rearrange("(a p) d -> p a d", p=128))
                for ft in range(16):
                    pg = psum[2 + ft % 2][:, 0:CAP]
                    pv = psum[4 + ft % 2][:, 0:CAP]
                    for dt8 in range(8):
                        E_mm(pg, w12[:, dt8, ts(ft, 128)], einT[:, dt8, :],
                             start=(dt8 == 0), stop=(dt8 == 7))
                    for dt8 in range(8):
                        E_mm(pv, w12[:, 8 + dt8, ts(ft, 128)], einT[:, dt8, :],
                             start=(dt8 == 0), stop=(dt8 == 7))
                    E_act(sg[:], pg, AF.Silu)
                    E_tt(gvb[:, ft, :], sg[:], pv, OP.mult)
                if tp_gv is not None and e == 0:
                    E_dma("gpsimd", tp_gv, gvb[:])
                for ct in range(2):
                    pm0 = psum[2 + ct][:, 0:512]
                    pm1 = psum[4 + ct][:, 0:512]
                    for ft in range(16):
                        E_mm(pm0, gvb[:, ft, ts(ct, 128)], w3f[:, ft, 0:512],
                             start=(ft == 0), stop=(ft == 15))
                        E_mm(pm1, gvb[:, ft, ts(ct, 128)], w3f[:, ft, 512:1024],
                             start=(ft == 0), stop=(ft == 15))
                    E_ts(wgt[:, 2 * ei + ct, 0:512], pm0, wcol[:, 2 * e + ct:2 * e + ct + 1], OP.mult)
                    E_ts(wgt[:, 2 * ei + ct, 512:1024], pm1, wcol[:, 2 * e + ct:2 * e + ct + 1], OP.mult)
            src = xres1 if grp == 0 else xres2
            for st in range(NLT):
                pm = [psum[0][:, 0:512], psum[1][:, 0:512]]
                for ei in range(8):
                    e = grp * 8 + ei
                    prb = psum[5][:, 0:128]
                    E_mm(prb, oh16_f[:, ts(e, 128)], rankm[:, ts(st, 128)], start=True, stop=True)
                    for ch in range(2):
                        E_ts(gsc[:, ch, :], prb, iotacc[:, ch:ch + 1], OP.is_equal)
                    for ch in range(2):
                        for dh in range(2):
                            E_mm(pm[dh], gsc[:, ch, :], wgt[:, 2 * ei + ch, ts(dh, 512)],
                                 start=(ei == 0 and ch == 0), stop=(ei == 7 and ch == 1))
                E_dma("sync", xf[:], src[ts(st, 128), :])
                for dh in range(2):
                    E_tt(xf[:, ts(dh, 512)], xf[:, ts(dh, 512)], pm[dh], OP.add)
                    E_dma("sync", xres2[ts(st, 128), ts(dh, 512)], xf[:, ts(dh, 512)])
        es.close()
        prog.barrier()

    def l1_ffn():
        es = ExitStack()

        def s(name, shape, dt=F32):
            return es.enter_context(nc.sbuf_tensor(name, list(shape), dt))

        wtile = s("f_wt", (128, 2, F1))
        accw = s("f_acc", (128, 2))
        ahalf = s("f_ah", (128, 2))
        for wi, w_ap, nrow, ncol in ((0, w_up, F1, D), (1, w_down, D, F1)):
            E_memset(accw[:, wi:wi + 1], 0.0)
            for ob in range(nrow // 128):
                E_dma("sync", wtile[:, ob % 2, 0:ncol], w_ap[ts(ob, 128), :])
                E_red(stat[:, ob % 2:ob % 2 + 1], wtile[:, ob % 2, 0:ncol], OP.add, absval=True)
                E_tt(accw[:, wi:wi + 1], accw[:, wi:wi + 1], stat[:, ob % 2:ob % 2 + 1], OP.add)
        for wi in range(2):
            E_tr(psum[6][0:1, 0:128], accw[:, wi:wi + 1], ident_f[:])
            E_copy(srow[0:1, 0:128], psum[6][0:1, 0:128])
            E_red(srow[0:1, 128:129], srow[0:1, 0:128], OP.add)
            E_ts(srow[0:1, 129:130], srow[0:1, 128:129], 0.5 / float(F1 * D), OP.mult)
            E_mm(psum[6][:, 0:1], ones1_f[:], srow[0:1, 129:130], start=True, stop=True)
            E_copy(ahalf[:, wi:wi + 1], psum[6][:, 0:1])

        sgn = s("f_sgn", (128, 2, 128))
        msk = s("f_msk", (128, 2, 128))
        natf = s("f_natf", (128, 4, 128))

        def ternarize_T(w_ap, wi, nrow, ncol, dstT):
            kk = [0]
            for ob in range(nrow // 128):
                E_dma("sync", wtile[:, ob % 2, 0:ncol], w_ap[ts(ob, 128), :])
                for ib in range(ncol // 128):
                    r2 = ib % 2
                    E_act(sgn[:, r2, :], wtile[:, ob % 2, ts(ib, 128)], AF.Sign)
                    E_act(msk[:, r2, :], wtile[:, ob % 2, ts(ib, 128)], AF.Abs)
                    E_ts(msk[:, r2, :], msk[:, r2, :], ahalf[:, wi:wi + 1], OP.is_gt)
                    E_tt(natf[:, ib % 4, :], sgn[:, r2, :], msk[:, r2, :], OP.mult)
                    pmt = psum[kk[0] % 4][:, 0:128]
                    kk[0] += 1
                    E_tr(pmt, natf[:, ib % 4, :], ident_f[:])
                    E_copy(dstT[:, ib, ts(ob, 128)], pmt)

        xn4T = s("f_xn4T", (128, 8, L), BF16)
        xf = s("f_xf", (128, D))
        sq = s("f_sq", (128, D))
        for lt in range(NLT):
            E_dma("sync", xf[:], xres3[ts(lt, 128), :])
            ln_stats(xf, sq)
            var = stat[:, 1:2]
            t1c, t2c, vt = stat[:, 10:11], stat[:, 11:12], stat[:, 12:13]
            E_ts(t1c, var, 1.0, OP.mult, LN_EPS, OP.add)
            E_recip(t2c, t1c)
            E_tt(vt, var, t2c, OP.mult)
            E_recip(t1c, vt)
            E_act(t1c, t1c, AF.Sqrt)
            E_recip(t2c, t1c)
            E_ts(t2c, t2c, 1.0, OP.mult, 1e-8, OP.add)
            E_recip(t1c, t2c)
            E_tt(stat[:, 2:3], stat[:, 2:3], t1c, OP.mult)
            E_ts(sq[:, 0:D], xf, stat[:, 0:1], OP.subtract, stat[:, 2:3], OP.mult)
            for ib in range(8):
                E_tr(psum[6][:, 0:128], sq[:, ts(ib, 128)], ident_f[:])
                E_copy(xn4T[:, ib, ts(lt, 128)], psum[6][:, 0:128])
        LS = 256
        with nc.sbuf_tensor("f_upT", [128, 8, F1], BF16) as w_upT, \
             nc.sbuf_tensor("f_hg", [128, 32, LS], F32) as hg, \
             nc.sbuf_tensor("f_h2b", [128, 2, LS], BF16) as h2b, \
             nc.sbuf_tensor("f_hsc", [128, 2, LS], F32) as hsc, \
             nc.sbuf_tensor("f_hnb", [128, LS], BF16) as hnb, \
             nc.sbuf_tensor("f_r2r", [1, LS], F32) as r2r, \
             nc.sbuf_tensor("f_srt", [1, 3 * LS], F32) as srt:
            ternarize_T(w_up, 0, F1, D, w_upT)
            for lsl in range(L // LS):
                ph2 = psum[4][0:1, 0:LS]
                for ft in range(32):
                    pm = psum[ft % 2][:, 0:LS]
                    for ib in range(8):
                        E_mm(pm, w_upT[:, ib, ts(ft, 128)], xn4T[:, ib, ts(lsl, LS)],
                             start=(ib == 0), stop=(ib == 7))
                    E_act(hg[:, ft, :], pm, AF.Gelu)
                    E_tt(hsc[:, ft % 2, :], hg[:, ft, :], hg[:, ft, :], OP.mult)
                    E_copy(h2b[:, ft % 2, :], hsc[:, ft % 2, :])
                    E_mm(ph2, ones_cb[:], h2b[:, ft % 2, :], start=(ft == 0), stop=(ft == 31))
                E_copy(r2r[:], ph2)
                E_ts(r2r[:], r2r[:], 1.0 / float(F1), OP.mult)
                E_act(srt[:, 0:LS], r2r[:], AF.Sqrt)
                E_recip(srt[:, LS:2 * LS], srt[:, 0:LS])
                E_tt(srt[:, 2 * LS:3 * LS], srt[:, LS:2 * LS], r2r[:], OP.mult)
                E_tt(srt[:, 0:LS], srt[:, 0:LS], srt[:, 2 * LS:3 * LS], OP.add)
                E_ts(srt[:, 0:LS], srt[:, 0:LS], 0.5, OP.mult, 1e-8, OP.add)
                E_recip(r2r[:], srt[:, 0:LS])
                pmb = psum[5][:, 0:LS]
                E_mm(pmb, ones1_f[:], r2r[:], start=True, stop=True)
                for ft in range(32):
                    E_tt(hnb[:], hg[:, ft, :], pmb, OP.mult)
                    E_dma("sync", hn_s[ft, :, ts(lsl, LS)], hnb[:])
        prog.barrier()
        with nc.sbuf_tensor("f_dnT", [128, 32, D], BF16) as w_dnT, \
             nc.sbuf_tensor("f_hn", [128, 2, 32, 128], BF16) as hnt, \
             nc.sbuf_tensor("f_o", [128, 512], F32) as of:
            ternarize_T(w_down, 1, D, F1, w_dnT)
            for lt in range(NLT):
                for ft in range(32):
                    E_dma("sync", hnt[:, lt % 2, ft, :], hn_s[ft, :, ts(lt, 128)])
                for dh in range(2):
                    pm = psum[2 + dh][:, 0:512]
                    for ft in range(32):
                        E_mm(pm, hnt[:, lt % 2, ft, :], w_dnT[:, ft, ts(dh, 512)],
                             start=(ft == 0), stop=(ft == 31))
                    E_dma("sync", xf[:, 0:512], xres3[ts(lt, 128), ts(dh, 512)])
                    E_tt(of[:], xf[:, 0:512], pm, OP.add)
                    E_dma("sync", y_out[ts(lt, 128), ts(dh, 512)], of[:])
        es.close()
        prog.barrier()

    mamba_block(0)
    l0_tail()
    moe_block()
    mamba_block(1)
    l1_ffn()

    for nm, src_t, shape, dt in (("x1", xres1, (L, D), F32), ("x2", xres2, (L, D), F32),
                                 ("x3", xres3, (L, D), F32),
                                 ("ycat", ycat_s, (32, 128, L), BF16)):
        tt = tap(nm, shape, dt)
        if tt is not None:
            E_dma("gpsimd", tt, src_t)

    final_cnt = prog.finalize()

    sem_stack = ExitStack()
    sems = {e: sem_stack.enter_context(nc.semaphore("sem_" + e)) for e in ENGINES}
    with nc.Block() as block:
        def runner(eng_name):
            def body(eng):
                for fn, waits, inc in prog.ops[eng_name]:
                    for ch, val in waits.items():
                        eng.wait_ge(sems[ch], val)
                    ins = fn(eng)
                    ins.then_inc(sems[eng_name], inc)
                if eng_name == "sync":
                    for ch in ENGINES:
                        if final_cnt[ch]:
                            eng.wait_ge(sems[ch], final_cnt[ch])
            return body

        for en in ENGINES:
            getattr(block, en)(runner(en))
    sem_stack.close()

    stack.close()
    return nc


def _consts():
    oh = np.zeros((16, 16 * 128), dtype=np.float32)
    for n in range(16):
        oh[n, n * 128:(n + 1) * 128] = 1.0
    return {
        "identf": np.eye(128, dtype=np.float32),
        "identb": np.eye(128).astype(ml_dtypes.bfloat16),
        "oh16f": oh,
        "iota_cr": np.tile(np.arange(1, CAP + 1, dtype=np.float32), (128, 1)).copy(),
        "iota_cc": np.stack([np.arange(1, 129, dtype=np.float32),
                             np.arange(129, 257, dtype=np.float32)], axis=1).copy(),
        "ones1f": np.ones((1, 128), dtype=np.float32),
        "onescb": np.ones((128, 1)).astype(ml_dtypes.bfloat16),
    }


def _in_map(inputs, core):
    a = np.ascontiguousarray
    m = {
        "x": a(inputs["L0_attn_x"][core]),
        "W_in": a(inputs["L0_attn_W_in"]),
        "conv_w": a(inputs["L0_attn_conv_w"].reshape(16, 128, 4).transpose(1, 0, 2)),
        "conv_b": a(inputs["L0_attn_conv_b"].reshape(16, 128).T),
        "W_x": a(inputs["L0_attn_W_x"]),
        "W_dt": a(inputs["L0_attn_W_dt"].reshape(16, 128, R).transpose(1, 0, 2)),
        "b_dt": a(inputs["L0_attn_b_dt"].reshape(16, 128).T),
        "A_log": a(inputs["L0_attn_A_log"].reshape(16, 128, NS).transpose(1, 0, 2)),
        "Dp": a(inputs["L0_attn_Dp"].reshape(16, 128).T),
        "W_out1": a(inputs["L0_attn_W_out1"]),
        "W_out2": a(inputs["L0_attn_W_out2"]),
        "router_w1": a(inputs["L0_ffn_router_w1"].reshape(8, 128, RH).transpose(1, 0, 2)),
        "router_w2": a(inputs["L0_ffn_router_w2"]),
        "moe_w1": a(inputs["L0_ffn_w1"]),
        "moe_w2": a(inputs["L0_ffn_w2"]),
        "moe_w3": a(inputs["L0_ffn_w3"]),
        "l1_lnw": a(inputs["L1_attn_ln_w"][3]),
        "l1_lnb": a(inputs["L1_attn_ln_b"][3]),
        "l1_ip": a(inputs["L1_attn_in_proj_w"][3]),
        "l1_cw": a(inputs["L1_attn_conv_w"][3].reshape(16, 128, 4).transpose(1, 0, 2)),
        "l1_cb": a(inputs["L1_attn_conv_b"][3].reshape(16, 128).T),
        "l1_xp": a(inputs["L1_attn_x_proj_w"][3]),
        "l1_dtp": a(inputs["L1_attn_dt_proj_w"][3].reshape(16, 128, R).transpose(1, 0, 2)),
        "l1_dtb": a(inputs["L1_attn_dt_proj_b"][3].reshape(16, 128).T),
        "l1_Alog": a(inputs["L1_attn_A_log"][3].reshape(16, 128, NS).transpose(1, 0, 2)),
        "l1_D": a(inputs["L1_attn_D"][3].reshape(16, 128).T),
        "l1_op": a(inputs["L1_attn_out_proj_w"][3]),
        "w_up": a(inputs["L1_ffn_w_up"]),
        "w_down": a(inputs["L1_ffn_w_down"]),
    }
    m.update(_consts())
    return m


def _run(inputs, taps=()):
    nc = build_program(taps=taps)
    in_maps = [_in_map(inputs, core) for core in range(8)]
    res = run_bass_kernel_spmd(nc, in_maps, core_ids=list(range(8)))
    return res


def kernel(**inputs):
    res = _run(inputs)
    out = np.stack([np.asarray(r["y"], dtype=np.float32) for r in res.results], axis=0)
    return out


# version 17
# speedup vs baseline: 1.0022x; 1.0022x over previous
import numpy as np
import ml_dtypes
import concourse.bass as bass
import concourse.mybir as mybir
from concourse.alu_op_type import AluOpType
from concourse.bass_utils import run_bass_kernel_spmd
from contextlib import ExitStack

F32 = mybir.dt.float32
BF16 = mybir.dt.bfloat16
AF = mybir.ActivationFunctionType
OP = AluOpType

L, D, DI, NS, R = 2048, 1024, 2048, 16, 64
E, CAP, DFF, F1, RH = 16, 256, 2048, 4096, 128
T = 256
NCH = L // T
NLT = L // 128
LN_EPS = 1e-5
ENGINES = ("sync", "scalar", "vector", "tensor", "gpsimd")


def ts(i, n):
    return slice(i * n, (i + 1) * n)


_PSUM_NAMES = frozenset(["P0", "P1", "P2", "P3", "P4", "P5", "P6", "PB"])


def _ap_range(ap):
    if ap.__class__.__name__.endswith("TensorHandle"):
        ap = ap[:]
    name = ap.tensor.name
    if name in _PSUM_NAMES:
        return name, 0, 1 << 30
    off = ap.offset
    lo = off
    hi = off
    for step, count in ap.ap:
        if count <= 1:
            continue
        d = step * (count - 1)
        if d >= 0:
            hi += d
        else:
            lo += d
    return name, lo, hi + 1


class Prog:

    def __init__(self):
        self.ops = {e: [] for e in ENGINES}
        self.cnt = {e: 0 for e in ENGINES}
        self.clock = {e: {c: 0 for c in ENGINES} for e in ENGINES}
        self.writers = {}
        self.readers = {}
        self.pending = {}
        self.n_sync_edges = 0

    def _deps_from(self, table, name, lo, hi, deps):
        for (l2, h2, ch, val) in table.get(name, ()):
            if l2 < hi and lo < h2:
                if deps.get(ch, 0) < val:
                    deps[ch] = val

    def emit(self, eng, fn, reads=(), writes=(), dma=False, dma_fence=False):
        deps = dict(self.pending.pop(eng, ()))
        rr = [_ap_range(a) for a in reads if a is not None]
        ww = [_ap_range(a) for a in writes if a is not None]
        for (name, lo, hi) in rr:
            self._deps_from(self.writers, name, lo, hi, deps)
        for (name, lo, hi) in ww:
            self._deps_from(self.writers, name, lo, hi, deps)
            self._deps_from(self.readers, name, lo, hi, deps)
        if eng == "tensor" and not dma:
            deps.pop(eng, None)
        kl = self.clock[eng]
        waits = {}
        for ch, val in deps.items():
            if val > kl[ch]:
                waits[ch] = val
                kl[ch] = val
        inc = 16 if dma else 1
        self.cnt[eng] += inc
        self.n_sync_edges += len(waits)
        self.ops[eng].append((fn, waits, inc))
        for (name, lo, hi) in ww:
            self.writers.setdefault(name, []).append((lo, hi, eng, self.cnt[eng]))
            wl = self.writers[name]
            if len(wl) > 96:
                del wl[:32]
        for (name, lo, hi) in rr:
            self.readers.setdefault(name, []).append((lo, hi, eng, self.cnt[eng]))
            rl = self.readers[name]
            if len(rl) > 96:
                del rl[:32]

    def barrier(self):
        snap = dict(self.cnt)
        self.writers = {}
        self.readers = {}
        self.pending = {e: dict(snap) for e in ENGINES}

    def finalize(self):
        return dict(self.cnt)


def build_program(taps=()):
    nc = bass.Bass("TRN2", target_bir_lowering=False)
    prog = Prog()
    em = prog.emit

    def din(name, shape, dt=F32):
        return nc.dram_tensor(name, list(shape), dt, kind="ExternalInput").ap()

    x0 = din("x", (L, D))
    W_in = din("W_in", (2 * DI, D))
    conv_w = din("conv_w", (128, 16, 4))
    conv_b = din("conv_b", (128, 16))
    W_x = din("W_x", (R + 2 * NS, DI))
    W_dt = din("W_dt", (128, 16, R))
    b_dt = din("b_dt", (128, 16))
    A_log = din("A_log", (128, 16, NS))
    Dp = din("Dp", (128, 16))
    W_out1 = din("W_out1", (DI, 2 * DI))
    W_out2 = din("W_out2", (D, DI))
    r_w1 = din("router_w1", (128, 8, RH))
    r_w2 = din("router_w2", (RH, E))
    moe_w1 = din("moe_w1", (E, D, DFF))
    moe_w2 = din("moe_w2", (E, D, DFF))
    moe_w3 = din("moe_w3", (E, DFF, D))
    l1_lnw = din("l1_lnw", (D,))
    l1_lnb = din("l1_lnb", (D,))
    l1_ip = din("l1_ip", (2 * DI, D))
    l1_cw = din("l1_cw", (128, 16, 4))
    l1_cb = din("l1_cb", (128, 16))
    l1_xp = din("l1_xp", (R + 2 * NS, DI))
    l1_dtp = din("l1_dtp", (128, 16, R))
    l1_dtb = din("l1_dtb", (128, 16))
    l1_Alog = din("l1_Alog", (128, 16, NS))
    l1_D = din("l1_D", (128, 16))
    l1_op = din("l1_op", (D, DI))
    w_up = din("w_up", (F1, D))
    w_down = din("w_down", (D, F1))
    identf = din("identf", (128, 128))
    identb = din("identb", (128, 128), BF16)
    oh16f_in = din("oh16f", (16, 16 * 128))
    iota_cr = din("iota_cr", (128, CAP))
    iota_cc = din("iota_cc", (128, 2))
    ones1f = din("ones1f", (1, 128))
    onescb = din("onescb", (128, 1), BF16)

    y_out = nc.dram_tensor("y", [L, D], F32, kind="ExternalOutput").ap()

    def dscr(name, shape, dt=F32):
        return nc.dram_tensor(name, list(shape), dt, kind="Internal").ap()

    xres1 = dscr("xres1", (L, D))
    xres2 = dscr("xres2", (L, D))
    xres3 = dscr("xres3", (L, D))
    ycat_s = dscr("ycat_s", (32, 128, L), BF16)
    hn_s = dscr("hn_s", (32, 128, L), BF16)
    s_WinT = dscr("s_WinT", (D, 2 * DI), BF16)
    s_Wo1T = dscr("s_Wo1T", (2 * DI, DI), BF16)
    s_Wo2T = dscr("s_Wo2T", (DI, D), BF16)
    s_ipT = dscr("s_ipT", (D, 2 * DI), BF16)
    s_opT = dscr("s_opT", (DI, D), BF16)

    tap_t = {}

    def tap(name, shape, dt=F32):
        if name not in taps:
            return None
        if name not in tap_t:
            tap_t[name] = nc.dram_tensor("tap_" + name, list(shape), dt,
                                         kind="ExternalOutput").ap()
        return tap_t[name]

    stack = ExitStack()

    def sbp(name, shape, dt=F32):
        return stack.enter_context(nc.sbuf_tensor(name, list(shape), dt))

    ident_f = sbp("ident_f", (128, 128))
    ident_b = sbp("ident_b", (128, 128), BF16)
    ones1_f = sbp("ones1_f", (1, 128))
    ones_cb = sbp("ones_cb", (128, 1), BF16)
    oh16_f = sbp("oh16_f", (16, 16 * 128))
    carry = sbp("carry", (128, 16 * NS))
    stat = sbp("stat", (128, 20))
    srow = sbp("srow", (16, 192))

    psum = [stack.enter_context(nc.psum_tensor("P%d" % i, [128, 512], F32)) for i in range(7)]
    psum_b = stack.enter_context(nc.psum_tensor("PB", [128, 1024], BF16))

    def _ap(x):
        if x is not None and x.__class__.__name__.endswith("TensorHandle"):
            return x[:]
        return x

    def E_dma(eng, out, in_):
        out, in_ = _ap(out), _ap(in_)
        em(eng, (lambda e, out=out, in_=in_: e.dma_start(out=out, in_=in_)), dma=True,
           reads=[in_], writes=[out])

    def E_dmaT(out, in_):
        out, in_ = _ap(out), _ap(in_)
        em("sync", (lambda e, out=out, in_=in_: e.dma_start(out=out, in_=in_, transpose=True)),
           dma=True, reads=[in_], writes=[out])

    def E_act(out, in_, func, bias=None, scale=None, eng="scalar"):
        out, in_ = _ap(out), _ap(in_)
        kw = {}
        rds = [in_]
        if bias is not None:
            kw["bias"] = _ap(bias)
            if not isinstance(bias, float):
                rds.append(kw["bias"])
        if scale is not None:
            kw["scale"] = _ap(scale)
            if not isinstance(scale, float):
                rds.append(kw["scale"])
        em(eng, (lambda e, out=out, in_=in_, func=func, kw=kw:
                 e.activation(out=out, in_=in_, func=func, **kw)), reads=rds, writes=[out])

    def E_tt(out, in0, in1, op, eng="vector"):
        out, in0, in1 = _ap(out), _ap(in0), _ap(in1)
        em(eng, (lambda e, out=out, in0=in0, in1=in1, op=op:
                 e.tensor_tensor(out=out, in0=in0, in1=in1, op=op)),
           reads=[in0, in1], writes=[out])

    def E_ts(out, in0, s1, op0, s2=None, op1=None, accum_out=None, eng="vector"):
        out, in0, s1, s2 = _ap(out), _ap(in0), _ap(s1), _ap(s2)
        kw = {}
        rds = [in0]
        wrs = [out]
        for s in (s1, s2):
            if s is not None and not isinstance(s, float):
                rds.append(s)
        if op1 is not None:
            kw["op1"] = op1
        if accum_out is not None:
            kw["accum_out"] = _ap(accum_out)
            wrs.append(kw["accum_out"])
        em(eng, (lambda e, out=out, in0=in0, s1=s1, s2=s2, op0=op0, kw=kw:
                 e.tensor_scalar(out=out, in0=in0, scalar1=s1, scalar2=s2, op0=op0, **kw)),
           reads=rds, writes=wrs)

    def E_stt(out, in0, scalar, in1, op0, op1, eng="vector"):
        out, in0, scalar, in1 = _ap(out), _ap(in0), _ap(scalar), _ap(in1)
        rds = [in0, in1]
        if not isinstance(scalar, float):
            rds.append(scalar)
        em(eng, (lambda e, out=out, in0=in0, sc=scalar, in1=in1, op0=op0, op1=op1:
                 e.scalar_tensor_tensor(out=out, in0=in0, scalar=sc, in1=in1, op0=op0, op1=op1)),
           reads=rds, writes=[out])

    def E_red(out, in_, op, absval=False, eng="vector"):
        out, in_ = _ap(out), _ap(in_)
        kw = {"apply_absolute_value": True} if absval else {}
        em(eng, (lambda e, out=out, in_=in_, op=op, kw=kw:
                 e.tensor_reduce(out=out, in_=in_, axis=mybir.AxisListType.X, op=op, **kw)),
           reads=[in_], writes=[out])

    def E_scan(out, d0, d1, initial):
        out, d0, d1, initial = _ap(out), _ap(d0), _ap(d1), _ap(initial)
        rds = [d0, d1]
        if not isinstance(initial, float):
            rds.append(initial)
        em("vector", (lambda e, out=out, d0=d0, d1=d1, init=initial:
                      e.tensor_tensor_scan(out=out, data0=d0, data1=d1, initial=init,
                                           op0=OP.mult, op1=OP.add)),
           reads=rds, writes=[out])

    def E_mm(out, lhsT, rhs, start, stop):
        out, lhsT, rhs = _ap(out), _ap(lhsT), _ap(rhs)
        em("tensor", (lambda e, out=out, l=lhsT, r=rhs, st=start, sp=stop:
                      e.matmul(out=out, lhsT=l, rhs=r, start=st, stop=sp)),
           reads=[lhsT, rhs], writes=[out])

    def E_tr(out, in_, identity):
        out, in_, identity = _ap(out), _ap(in_), _ap(identity)
        k = in_.shape[0]
        if identity.shape[0] != k:
            identity = identity[0:k, 0:k]
        em("tensor", (lambda e, out=out, in_=in_, idn=identity:
                      e.transpose(out=out, in_=in_, identity=idn)),
           reads=[in_, identity], writes=[out])

    def E_copy(out, in_, eng="scalar"):
        out, in_ = _ap(out), _ap(in_)
        if eng == "scalar":
            E_act(out, in_, AF.Copy)
        else:
            em(eng, (lambda e, out=out, in_=in_: e.tensor_copy(out=out, in_=in_)),
               reads=[in_], writes=[out])

    def E_memset(ap, v, eng="vector"):
        ap = _ap(ap)
        em(eng, (lambda e, ap=ap, v=v: e.memset(ap, v)), writes=[ap])

    def E_recip(out, in_):
        out, in_ = _ap(out), _ap(in_)
        em("vector", (lambda e, out=out, in_=in_: e.reciprocal(out=out, in_=in_)),
           reads=[in_], writes=[out])

    def rsqrt2(out, v, t1):
        E_recip(t1, v)
        E_act(out, t1, AF.Sqrt)

    def E_bnstats(out, in_):
        out, in_ = _ap(out), _ap(in_)
        em("vector", (lambda e, out=out, in_=in_: e.bn_stats(out=out, in_=in_)),
           reads=[in_], writes=[out])

    def E_bnaggr(out, in_):
        out, in_ = _ap(out), _ap(in_)
        em("vector", (lambda e, out=out, in_=in_: e.bn_aggr(out=out, in_=in_)),
           reads=[in_], writes=[out])

    def ln_stats(x_t, sq, Dd=D, eps=LN_EPS):
        m, var, rs = stat[:, 0:1], stat[:, 1:2], stat[:, 2:3]
        t1 = stat[:, 3:4]
        nh = Dd // 512
        for h in range(nh):
            E_bnstats(stat[:, 4 + 6 * h:10 + 6 * h], x_t[:, ts(h, 512)])
        E_bnaggr(stat[:, 0:2], stat[:, 4:4 + 6 * nh])
        E_ts(t1, var, 1.0, OP.mult, eps, OP.add)
        rsqrt2(rs, t1, stat[:, 3:4]) if False else None
        E_recip(stat[:, 3:4], t1)
        E_act(rs, stat[:, 3:4], AF.Sqrt)

    def prep_T(w_ap, o_dim, i_dim, dst, st_nat, st_row):
        OSUP = 1024
        k = [0]
        for ib in range(i_dim // 128):
            rn = ib % 2
            for osb in range(o_dim // OSUP):
                src = w_ap[ts(osb, OSUP), ts(ib, 128)].rearrange("(a p) i -> p a i", p=128)
                E_dma("gpsimd", st_nat[:, rn, :, :], src)
                for a in range(OSUP // 128):
                    o0 = osb * OSUP + a * 128
                    pmt = psum[k[0] % 4][:, 0:128]
                    k[0] += 1
                    E_tr(pmt, st_nat[:, rn, a, :], ident_f[:])
                    E_copy(st_row[:, rn, o0:o0 + 128], pmt)
            E_dma("scalar", dst[ts(ib, 128), :], st_row[:, rn, 0:o_dim])

    E_dma("sync", ident_f[:], identf)
    with nc.sbuf_tensor("pr_nat", [128, 2, 8, 128], F32) as st_nat, \
         nc.sbuf_tensor("pr_row", [128, 2, 4096], BF16) as st_row:
        prep_T(W_in, 2 * DI, D, s_WinT, st_nat, st_row)
        prep_T(W_out1, DI, 2 * DI, s_Wo1T, st_nat, st_row)
        prep_T(W_out2, D, DI, s_Wo2T, st_nat, st_row)
        prep_T(l1_ip, 2 * DI, D, s_ipT, st_nat, st_row)
        prep_T(l1_op, D, DI, s_opT, st_nat, st_row)
    prog.barrier()

    E_dma("sync", ident_b[:], identb)
    E_dma("sync", ones1_f[:], ones1f)
    E_dma("sync", ones_cb[:], onescb)
    E_dma("sync", oh16_f[:], oh16f_in)

    def mamba_block(which):
        l0 = which == 0
        sfx = "_m%d" % which
        x_in = x0 if l0 else xres2
        winT_scr = s_WinT if l0 else s_ipT
        w_x_in = W_x if l0 else l1_xp
        w_dt_in = W_dt if l0 else l1_dtp
        cw_in = conv_w if l0 else l1_cw
        cb_in = conv_b if l0 else l1_cb
        bdt_in = b_dt if l0 else l1_dtb
        alog_in = A_log if l0 else l1_Alog
        dp_in = Dp if l0 else l1_D

        es = ExitStack()

        def s(name, shape, dt=F32):
            return es.enter_context(nc.sbuf_tensor(name + sfx, list(shape), dt))

        cwc = s("cwc", (128, 16, 4))
        cbc = s("cbc", (128, 16))
        bdtc = s("bdtc", (128, 16))
        dpc = s("dpc", (128, 16))
        ac = s("ac", (128, 16 * NS))
        wxT = s("wxT", (128, 16, 96), BF16)
        wdtT = s("wdtT", (64, DI), BF16)
        E_dma("sync", cwc[:], cw_in)
        E_dma("sync", cbc[:], cb_in)
        E_dma("sync", bdtc[:], bdt_in)
        E_dma("sync", dpc[:], dp_in)
        E_dma("sync", ac[:].rearrange("p (a n) -> p a n", n=NS), alog_in)
        E_act(ac[:], ac[:], AF.Exp)
        E_ts(ac[:], ac[:], -1.0, OP.mult)
        if l0:
            dpb = s("dpb", (128, 16), BF16)
            atb = s("atb", (16, DI), BF16)
            E_copy(dpb[:], dpc[:])
            for db in range(16):
                E_tr(psum[6][0:NS, 0:128], ac[:, ts(db, NS)], ident_f[:])
                E_copy(atb[:, ts(db, 128)], psum[6][0:NS, 0:128])
        else:
            lnw_bc = s("lnwbc", (128, D), BF16)
            lnb_bc = s("lnbbc", (128, D), BF16)
            with nc.sbuf_tensor("lnr" + sfx, [1, 2 * D], F32) as lnr:
                E_dma("sync", lnr[:, 0:D], l1_lnw.rearrange("(a d) -> a d", a=1))
                E_dma("sync", lnr[:, D:2 * D], l1_lnb.rearrange("(a d) -> a d", a=1))
                for i, dstt in ((0, lnw_bc), (1, lnb_bc)):
                    for j in range(2):
                        E_mm(psum[6][:, 0:512], ones1_f[:],
                             lnr[:, i * D + j * 512: i * D + (j + 1) * 512], start=True, stop=True)
                        E_copy(dstt[:, ts(j, 512)], psum[6][:, 0:512])
            prog.barrier()
        with nc.sbuf_tensor("smw" + sfx, [96, DI], BF16) as smw:
            E_dma("gpsimd", smw[:], w_x_in)
            for ib in range(16):
                E_tr(psum_b[:, 0:96], smw[:, ts(ib, 128)], ident_b[:])
                E_copy(wxT[:, ib, :], psum_b[:, 0:96])
        prog.barrier()
        with nc.sbuf_tensor("smd" + sfx, [128, 16, R], BF16) as smd:
            E_dma("gpsimd", smd[:], w_dt_in)
            for ib in range(16):
                E_tr(psum_b[0:R, 0:128], smd[:, ib, :], ident_b[:])
                E_copy(wdtT[:, ts(ib, 128)], psum_b[0:R, 0:128])
        prog.barrier()

        xf = s("xf", (128, D))
        sq = s("sq", (128, D))
        xT = s("xT", (128, 8, T), BF16)
        winP = s("winP", (128, 8, 512), BF16)
        xih = s("xih", (128, 16, T + 3))
        halo = s("halo", (128, 16, 3))
        u_f = s("u", (128, 16, T))
        ub = s("ub", (128, 16, T), BF16) if l0 else s("ub", (128, 2, T), BF16)
        sz = s("sz", (128, 2, 16, T), BF16)
        dtrb = s("dtrb", (64, T), BF16)
        btf = s("btf", (16, T))
        ctf = s("ctf", (16, T))
        btil = s("btil", (128, NS, T))
        ctil = s("ctil", (128, NS, T), BF16)
        dl_all = s("dl_all", (128, 16, T))
        sp_s = s("sp_s", (128, 2, T))
        dA_all = s("dA_all", (128, NS, T))
        bsc_all = s("bsc_all", (128, NS, T))
        du = s("du", (128, T))
        s_all = s("s_all", (128, NS, T))
        sC = s("sC", (128, NS, T), BF16)
        t1 = s("t1", (128, T))
        ycb = s("ycb", (128, 2, T), BF16)
        if l0:
            ctb = s("ctb", (16, T), BF16)
            sdu = btf[0:1, :]
        else:
            ytb = s("ytb", (128, 16, T), BF16)
            opt_t = s("optt", (128, 1, D), BF16)

        tp_xn = tap("xn%d" % which, (L, D))
        tp_u = tap("u%d" % which, (128, 16, T))
        tp_dl = tap("dl%d" % which, (128, 16, T))
        tp_bt = tap("bt%d" % which, (16, T))
        tp_sz = tap("sz%d" % which, (128, 16, T))
        tp_s0 = tap("s%d" % which, (128, NS, T))

        E_memset(carry[:], 0.0)

        def ln_xz(c):
            t0 = c * T
            for lt in range(2):
                row0 = t0 + lt * 128
                E_dma("sync", xf[:], x_in[row0:row0 + 128, :])
                ln_stats(xf, sq)
                if not l0:
                    var = stat[:, 1:2]
                    t1c, t2c, vt = stat[:, 10:11], stat[:, 11:12], stat[:, 12:13]
                    E_ts(t1c, var, 1.0, OP.mult, LN_EPS, OP.add)
                    E_recip(t2c, t1c)
                    E_tt(vt, var, t2c, OP.mult)
                    E_ts(vt, vt, 25.0, OP.mult, LN_EPS, OP.add)
                    rsqrt2(t2c, vt, stat[:, 13:14])
                    E_ts(t2c, t2c, 5.0, OP.mult)
                    E_tt(stat[:, 2:3], stat[:, 2:3], t2c, OP.mult)
                E_ts(sq[:, 0:D], xf, stat[:, 0:1], OP.subtract, stat[:, 2:3], OP.mult)
                if not l0:
                    E_tt(sq[:, 0:D], sq[:, 0:D], lnw_bc[:], OP.mult)
                    E_tt(sq[:, 0:D], sq[:, 0:D], lnb_bc[:], OP.add)
                if tp_xn is not None:
                    E_dma("gpsimd", tp_xn[row0:row0 + 128, :], sq[:, 0:D])
                for ib in range(8):
                    pmt = psum[4 + ib % 2][:, 0:128]
                    E_tr(pmt, sq[:, ts(ib, 128)], ident_f[:])
                    E_copy(xT[:, ib, lt * 128:(lt + 1) * 128], pmt)
            for q in range(8):
                for ib in range(8):
                    E_dma("scalar", winP[:, ib, :], winT_scr[ts(ib, 128), q * 512:(q + 1) * 512])
                for eq in range(4):
                    et = q * 4 + eq
                    pm = psum[eq % 2][:, 0:T]
                    for ib in range(8):
                        E_mm(pm, winP[:, ib, ts(eq, 128)], xT[:, ib, :],
                             start=(ib == 0), stop=(ib == 7))
                    if et < 16:
                        if c == 0:
                            E_memset(xih[:, et, 0:3], 0.0)
                        else:
                            E_copy(xih[:, et, 0:3], halo[:, et, :])
                        E_copy(xih[:, et, 3:T + 3], pm)
                    else:
                        E_act(sz[:, c % 2, et - 16, :], pm, AF.Silu)

        ln_xz(0)
        for c in range(NCH):
            t0 = c * T
            for k in range(4):
                for et in range(16):
                    tconv = dA_all[:, et, :]
                    if k == 0:
                        E_ts(tconv, xih[:, et, 0:T], cwc[:, et, 0:1], OP.mult)
                    else:
                        E_stt(tconv, xih[:, et, k:k + T], cwc[:, et, k:k + 1], tconv,
                              OP.mult, OP.add)
            for et in range(16):
                E_copy(halo[:, et, :], xih[:, et, T:T + 3])
                E_act(u_f[:, et, :], dA_all[:, et, :], AF.Silu, bias=cbc[:, et:et + 1])
                if l0:
                    E_copy(ub[:, et, :], u_f[:, et, :])
            if c == 0 and tp_u is not None:
                E_dma("gpsimd", tp_u, u_f[:])
            if c == 0 and tp_sz is not None:
                E_dma("gpsimd", tp_sz, sz[:, 0])
            for ib in range(16):
                if l0:
                    ubt = ub[:, ib, :]
                else:
                    ubt = ub[:, ib % 2, :]
                    E_copy(ubt, u_f[:, ib, :])
                E_mm(psum[0][0:64, 0:T], wxT[:, ib, 0:64], ubt, start=(ib == 0), stop=(ib == 15))
                E_mm(psum[1][0:16, 0:T], wxT[:, ib, 64:80], ubt, start=(ib == 0), stop=(ib == 15))
                E_mm(psum[2][0:16, 0:T], wxT[:, ib, 80:96], ubt, start=(ib == 0), stop=(ib == 15))
            E_copy(dtrb[:], psum[0][0:64, 0:T])
            E_copy(btf[:], psum[1][0:16, 0:T])
            E_copy(ctf[:], psum[2][0:16, 0:T])
            if l0:
                E_copy(ctb[:], psum[2][0:16, 0:T])
            if c == 0 and tp_bt is not None:
                E_dma("gpsimd", tp_bt, btf[:])
            for n in range(NS):
                pm = psum[0 + n % 2][:, 0:T]
                E_mm(pm, oh16_f[:, ts(n, 128)], btf[:], start=True, stop=True)
                E_copy(btil[:, n, :], pm)
                pm2 = psum[2 + n % 2][:, 0:T]
                E_mm(pm2, oh16_f[:, ts(n, 128)], ctf[:], start=True, stop=True)
                E_copy(ctil[:, n, :], pm2, eng="vector")
            for db in range(16):
                pm = psum[4 + db % 2][:, 0:T]
                sp = sp_s[:, db % 2, :]
                dl = dl_all[:, db, :]
                E_mm(pm, wdtT[:, ts(db, 128)], dtrb[:], start=True, stop=True)
                E_act(sp, pm, AF.Abs, bias=bdtc[:, db:db + 1])
                E_act(sp, sp, AF.Exp, scale=-1.0)
                E_act(sp, sp, AF.Ln, bias=1.0)
                E_act(dl, pm, AF.Relu, bias=bdtc[:, db:db + 1])
                E_tt(dl, dl, sp, OP.add, eng="gpsimd")
            if c == 0 and tp_dl is not None:
                E_dma("gpsimd", tp_dl, dl_all[:])
            if c + 1 < NCH:
                ln_xz(c + 1)
            def tail(db):
                r = db % 2
                py = psum[2 + r][:, 0:T]
                E_stt(t1[:], u_f[:, db, :], dpc[:, db:db + 1], py, OP.mult, OP.add)
                E_tt(ycb[:, r, :], t1[:], sz[:, c % 2, db, :], OP.mult)
                if l0:
                    E_dma("sync", ycat_s[db, :, t0:t0 + T], ycb[:, r, :])
                else:
                    E_copy(ytb[:, db, :], ycb[:, r, :], eng="vector")

            for db in range(16):
                r = db % 2
                dl = dl_all[:, db, :]
                dub = du[:]
                E_tt(dub, dl, u_f[:, db, :], OP.mult, eng="gpsimd")
                dub_b = dub.rearrange("p (a t) -> p a t", a=1).broadcast_to([128, 4, T])
                for g in range(4):
                    em("gpsimd", (lambda e, out=bsc_all[:, 4 * g:4 * g + 4, :], in0=dub_b,
                                  in1=btil[:, 4 * g:4 * g + 4, :]:
                                  e.tensor_tensor(out=out, in0=in0, in1=in1, op=OP.mult)),
                       reads=[dub, btil[:, 4 * g:4 * g + 4, :]],
                       writes=[bsc_all[:, 4 * g:4 * g + 4, :]])
                py = psum[2 + r][:, 0:T]
                for n in range(NS):
                    E_act(dA_all[:, n, :], dl, AF.Exp, scale=ac[:, db * NS + n:db * NS + n + 1])
                for n in range(NS):
                    E_scan(s_all[:, n, :], dA_all[:, n, :], bsc_all[:, n, :],
                           carry[:, db * NS + n:db * NS + n + 1])
                E_copy(carry[:, ts(db, NS)], s_all[:, :, T - 1], eng="vector")
                E_tt(sC[:], s_all[:], ctil[:], OP.mult)
                for n in range(NS):
                    E_mm(py, ident_b[:], sC[:, n, :], start=(n == 0), stop=(n == NS - 1))
                if c == 0 and db == 0 and tp_s0 is not None:
                    E_dma("gpsimd", tp_s0, s_all[:])
                if db > 0:
                    tail(db - 1)
            tail(15)
            if l0:
                psd = psum[6][0:1, 0:T]
                for db in range(16):
                    E_mm(psd, dpb[:, db:db + 1], ub[:, db, :], start=(db == 0), stop=(db == 15))
                E_copy(sdu[:], psd)
                for db2 in range(16):
                    pm = psum[2 + db2 % 2][:, 0:T]
                    E_mm(pm, atb[:, ts(db2, 128)], ctb[:], start=True, stop=False)
                    E_mm(pm, ones1_f[:], sdu[:], start=False, stop=True)
                    E_copy(ycb[:, db2 % 2, :], pm)
                    E_dma("sync", ycat_s[16 + db2, :, t0:t0 + T], ycb[:, db2 % 2, :])
            else:
                pm00 = psum[2][:, 0:512]
                pm01 = psum[3][:, 0:512]
                pm10 = psum[6][:, 0:512]
                pm11 = psum[5][:, 0:512]
                for db in range(16):
                    E_dma("sync", opt_t[:, 0, :], s_opT[ts(db, 128), :])
                    st = (db == 0)
                    sp2 = (db == 15)
                    E_mm(pm00, ytb[:, db, 0:128], opt_t[:, 0, 0:512], start=st, stop=sp2)
                    E_mm(pm01, ytb[:, db, 0:128], opt_t[:, 0, 512:1024], start=st, stop=sp2)
                    E_mm(pm10, ytb[:, db, 128:256], opt_t[:, 0, 0:512], start=st, stop=sp2)
                    E_mm(pm11, ytb[:, db, 128:256], opt_t[:, 0, 512:1024], start=st, stop=sp2)
                for lsub in range(2):
                    row0 = t0 + lsub * 128
                    for dh in range(2):
                        pm = (pm00, pm01, pm10, pm11)[lsub * 2 + dh]
                        rb = xf[:, ts(dh, 512)]
                        E_dma("sync", rb, xres2[row0:row0 + 128, ts(dh, 512)])
                        E_tt(rb, rb, pm, OP.add)
                        E_dma("sync", xres3[row0:row0 + 128, ts(dh, 512)], rb)
        es.close()
        prog.barrier()

    def l0_tail():
        es = ExitStack()

        def s(name, shape, dt=F32):
            return es.enter_context(nc.sbuf_tensor(name, list(shape), dt))

        h1T = s("t_h1T", (128, 16, L), BF16)
        wo2 = s("t_wo2", (128, 16, D), BF16)
        wo1 = s("t_wo1", (128, 32, 1024), BF16)
        yc = s("t_yc", (128, 2, 32, 256), BF16)
        xs = s("t_xs", (128, 512))
        o2 = s("t_o2", (128, 512))
        for ib in range(16):
            E_dma("sync", wo2[:, ib, :], s_Wo2T[ts(ib, 128), :])
        for fh in range(2):
            for cb in range(32):
                E_dma("sync", wo1[:, cb, :], s_Wo1T[ts(cb, 128), fh * 1024:(fh + 1) * 1024])
            for lsl in range(8):
                for cb in range(32):
                    E_dma("sync", yc[:, lsl % 2, cb, :], ycat_s[cb, :, lsl * 256:(lsl + 1) * 256])
                for ft in range(8):
                    pm = psum[ft % 2][:, 0:256]
                    for cb in range(32):
                        E_mm(pm, wo1[:, cb, ts(ft, 128)], yc[:, lsl % 2, cb, :],
                             start=(cb == 0), stop=(cb == 31))
                    E_act(h1T[:, fh * 8 + ft, lsl * 256:(lsl + 1) * 256], pm, AF.Relu)
        for lt in range(NLT):
            for dh in range(2):
                pm = psum[2 + dh][:, 0:512]
                for ft in range(16):
                    E_mm(pm, h1T[:, ft, lt * 128:(lt + 1) * 128], wo2[:, ft, ts(dh, 512)],
                         start=(ft == 0), stop=(ft == 15))
                E_dma("sync", xs[:], x0[ts(lt, 128), ts(dh, 512)])
                E_tt(o2[:], xs[:], pm, OP.add)
                E_dma("sync", xres1[ts(lt, 128), ts(dh, 512)], o2[:])
        es.close()
        prog.barrier()

    def moe_block():
        es = ExitStack()

        def s(name, shape, dt=F32):
            return es.enter_context(nc.sbuf_tensor(name, list(shape), dt))

        iotacr = s("m_iotacr", (128, CAP))
        iotacc = s("m_iotacc", (128, 2))
        E_dma("sync", iotacr[:], iota_cr)
        E_dma("sync", iotacc[:], iota_cc)
        xn_tok = s("m_xn", (128, 16, D), BF16)
        xf = s("m_xf", (128, D))
        rankm = s("m_rankm", (16, L))
        rmT = s("m_rmT", (128, 16, 16))
        pT = s("m_pT", (128, 16, 16), BF16)
        routing_es = ExitStack()
        probs = routing_es.enter_context(nc.sbuf_tensor("m_probs", [16, L], F32))
        cmp_s = routing_es.enter_context(nc.sbuf_tensor("m_cmp", [16, L], F32))
        ones_r = routing_es.enter_context(nc.sbuf_tensor("m_ones", [16, L], F32))
        lo, hi, th = srow[:, 0:1], srow[:, 1:2], srow[:, 2:3]
        cntc, pred, dnm = srow[:, 3:4], srow[:, 4:5], srow[:, 5:6]
        mx, sm = srow[:, 6:7], srow[:, 7:8]
        tp_pr = tap("probs", (16, L))
        tp_rk = tap("rankm", (16, L))
        tp_ein = tap("ein", (128, 8, CAP), BF16)
        tp_gv = tap("gv", (128, 16, CAP), BF16)

        with nc.sbuf_tensor("m_xnT", [128, 8, L], BF16) as xnT, \
             nc.sbuf_tensor("m_hrT", [128, L], BF16) as hrT, \
             nc.sbuf_tensor("m_sq", [128, D], F32) as sq, \
             nc.sbuf_tensor("m_rw1", [128, 8, RH], BF16) as rw1b, \
             nc.sbuf_tensor("m_rw2", [128, E], BF16) as rw2b:
            E_dma("gpsimd", rw1b[:], r_w1)
            E_dma("gpsimd", rw2b[:], r_w2)
            for lt in range(NLT):
                E_dma("sync", xf[:], xres1[ts(lt, 128), :])
                ln_stats(xf, sq)
                E_ts(xn_tok[:, lt, :], xf, stat[:, 0:1], OP.subtract, stat[:, 2:3], OP.mult)
                E_copy(sq[:, 0:D], xn_tok[:, lt, :])
                for ib in range(8):
                    E_tr(psum[6][:, 0:128], sq[:, ts(ib, 128)], ident_f[:])
                    E_copy(xnT[:, ib, ts(lt, 128)], psum[6][:, 0:128])
            for j in range(4):
                pm = psum[j % 2][:, 0:512]
                for ib in range(8):
                    E_mm(pm, rw1b[:, ib, :], xnT[:, ib, ts(j, 512)], start=(ib == 0), stop=(ib == 7))
                E_act(hrT[:, ts(j, 512)], pm, AF.Silu)
            for j in range(4):
                pm = psum[2 + j % 2][0:16, 0:512]
                E_mm(pm, rw2b[:], hrT[:, ts(j, 512)], start=True, stop=True)
                E_copy(probs[:, ts(j, 512)], pm)
        prog.barrier()
        E_red(mx, probs[:], OP.max)
        E_ts(mx, mx, -1.0, OP.mult)
        E_act(probs[:], probs[:], AF.Exp, bias=mx)
        E_red(sm, probs[:], OP.add)
        E_recip(sm, sm)
        E_ts(probs[:], probs[:], sm, OP.mult)
        if tp_pr is not None:
            E_dma("gpsimd", tp_pr, probs[:])
        E_memset(lo, 0.0)
        E_memset(hi, 1.1)
        tb1, tb2 = srow[:, 8:9], srow[:, 9:10]
        for _ in range(44):
            E_tt(th, lo, hi, OP.add)
            E_ts(th, th, 0.5, OP.mult)
            E_ts(cmp_s[:], probs[:], th, OP.is_ge, op1=OP.add, accum_out=cntc)
            E_ts(pred, cntc, float(CAP) - 0.5, OP.is_ge)
            E_tt(tb1, th, lo, OP.subtract)
            E_tt(tb2, hi, th, OP.subtract)
            E_stt(lo, tb1, pred, lo, OP.mult, OP.add)
            E_stt(hi, tb2, pred, th, OP.mult, OP.add)
        E_ts(cmp_s[:], probs[:], lo, OP.is_ge)
        E_memset(ones_r[:], 1.0)
        E_scan(rankm[:], ones_r[:], cmp_s[:], 0.0)
        E_tt(rankm[:], rankm[:], cmp_s[:], OP.mult)
        E_ts(cmp_s[:], rankm[:], float(CAP) + 0.5, OP.is_le)
        E_tt(rankm[:], rankm[:], cmp_s[:], OP.mult)
        if tp_rk is not None:
            E_dma("gpsimd", tp_rk, rankm[:])
        E_ts(cmp_s[:], rankm[:], 0.5, OP.is_ge)
        E_tt(cmp_s[:], cmp_s[:], probs[:], OP.mult)
        E_red(dnm, cmp_s[:], OP.add)
        E_ts(dnm, dnm, 1.0, OP.mult, 1e-9, OP.add)
        E_recip(dnm, dnm)
        for lt in range(NLT):
            E_tr(psum[6][:, 0:16], rankm[:, ts(lt, 128)], ident_f[:])
            E_copy(rmT[:, lt, :], psum[6][:, 0:16])
            E_tr(psum[6][:, 0:16], probs[:, ts(lt, 128)], ident_f[:])
            E_copy(pT[:, lt, :], psum[6][:, 0:16])
        routing_es.close()
        prog.barrier()

        wgt = s("m_wgt", (128, 16, D), BF16)
        geT = s("m_geT", (128, 16, CAP), BF16)
        einT = s("m_einT", (128, 8, CAP), BF16)
        w12 = s("m_w12", (128, 16, DFF), BF16)
        w3f = s("m_w3f", (128, 16, D), BF16)
        gvb = s("m_gv", (128, 16, CAP), BF16)
        sg = s("m_sg", (128, CAP))
        psel = sg[0:16, :]
        wcol = s("m_wcol", (128, 2 * E))
        gsc = s("m_gsc", (128, 2, 128), BF16)
        for grp in range(2):
            for ei in range(8):
                e = grp * 8 + ei
                E_dma("gpsimd", w12[:, 0:8, :], moe_w1[e].rearrange("(a p) f -> p a f", p=128))
                E_dma("gpsimd", w12[:, 8:16, :], moe_w2[e].rearrange("(a p) f -> p a f", p=128))
                E_dma("gpsimd", w3f[:], moe_w3[e].rearrange("(a p) d -> p a d", p=128))
                for lt in range(NLT):
                    E_ts(geT[:, lt, :], iotacr[:], rmT[:, lt, e:e + 1], OP.is_equal)
                for dt8 in range(8):
                    pm = psum[dt8 % 2][:, 0:CAP]
                    for lt in range(NLT):
                        E_mm(pm, xn_tok[:, lt, ts(dt8, 128)], geT[:, lt, :],
                             start=(lt == 0), stop=(lt == 15))
                    E_copy(einT[:, dt8, :], pm)
                if tp_ein is not None and e == 0:
                    E_dma("gpsimd", tp_ein, einT[:])
                pm = psum[2][0:16, 0:CAP]
                for lt in range(NLT):
                    E_mm(pm, pT[:, lt, :], geT[:, lt, :], start=(lt == 0), stop=(lt == 15))
                E_copy(psel[:], pm)
                E_ts(psel[:], psel[:], dnm, OP.mult)
                for ch in range(2):
                    E_tr(psum[6][:, 0:16], psel[:, ts(ch, 128)], ident_f[:])
                    E_copy(wcol[:, 2 * e + ch:2 * e + ch + 1], psum[6][:, e:e + 1])
                for ft in range(16):
                    pg = psum[2 + ft % 2][:, 0:CAP]
                    pv = psum[4 + ft % 2][:, 0:CAP]
                    for dt8 in range(8):
                        E_mm(pg, w12[:, dt8, ts(ft, 128)], einT[:, dt8, :],
                             start=(dt8 == 0), stop=(dt8 == 7))
                    for dt8 in range(8):
                        E_mm(pv, w12[:, 8 + dt8, ts(ft, 128)], einT[:, dt8, :],
                             start=(dt8 == 0), stop=(dt8 == 7))
                    E_act(sg[:], pg, AF.Silu)
                    E_tt(gvb[:, ft, :], sg[:], pv, OP.mult)
                if tp_gv is not None and e == 0:
                    E_dma("gpsimd", tp_gv, gvb[:])
                for ct in range(2):
                    pm0 = psum[2 + ct][:, 0:512]
                    pm1 = psum[4 + ct][:, 0:512]
                    for ft in range(16):
                        E_mm(pm0, gvb[:, ft, ts(ct, 128)], w3f[:, ft, 0:512],
                             start=(ft == 0), stop=(ft == 15))
                        E_mm(pm1, gvb[:, ft, ts(ct, 128)], w3f[:, ft, 512:1024],
                             start=(ft == 0), stop=(ft == 15))
                    E_ts(wgt[:, 2 * ei + ct, 0:512], pm0, wcol[:, 2 * e + ct:2 * e + ct + 1], OP.mult)
                    E_ts(wgt[:, 2 * ei + ct, 512:1024], pm1, wcol[:, 2 * e + ct:2 * e + ct + 1], OP.mult)
            src = xres1 if grp == 0 else xres2
            for st in range(NLT):
                pm = [psum[0][:, 0:512], psum[1][:, 0:512]]
                for ei in range(8):
                    e = grp * 8 + ei
                    prb = psum[5][:, 0:128]
                    E_mm(prb, oh16_f[:, ts(e, 128)], rankm[:, ts(st, 128)], start=True, stop=True)
                    for ch in range(2):
                        E_ts(gsc[:, ch, :], prb, iotacc[:, ch:ch + 1], OP.is_equal)
                    for ch in range(2):
                        for dh in range(2):
                            E_mm(pm[dh], gsc[:, ch, :], wgt[:, 2 * ei + ch, ts(dh, 512)],
                                 start=(ei == 0 and ch == 0), stop=(ei == 7 and ch == 1))
                E_dma("sync", xf[:], src[ts(st, 128), :])
                for dh in range(2):
                    E_tt(xf[:, ts(dh, 512)], xf[:, ts(dh, 512)], pm[dh], OP.add)
                    E_dma("sync", xres2[ts(st, 128), ts(dh, 512)], xf[:, ts(dh, 512)])
        es.close()
        prog.barrier()

    def l1_ffn():
        es = ExitStack()

        def s(name, shape, dt=F32):
            return es.enter_context(nc.sbuf_tensor(name, list(shape), dt))

        wtile = s("f_wt", (128, 2, F1))
        accw = s("f_acc", (128, 2))
        ahalf = s("f_ah", (128, 2))
        for wi, w_ap, nrow, ncol in ((0, w_up, F1, D), (1, w_down, D, F1)):
            E_memset(accw[:, wi:wi + 1], 0.0)
            for ob in range(nrow // 128):
                E_dma("sync", wtile[:, ob % 2, 0:ncol], w_ap[ts(ob, 128), :])
                E_red(stat[:, ob % 2:ob % 2 + 1], wtile[:, ob % 2, 0:ncol], OP.add, absval=True)
                E_tt(accw[:, wi:wi + 1], accw[:, wi:wi + 1], stat[:, ob % 2:ob % 2 + 1], OP.add)
        for wi in range(2):
            E_tr(psum[6][0:1, 0:128], accw[:, wi:wi + 1], ident_f[:])
            E_copy(srow[0:1, 0:128], psum[6][0:1, 0:128])
            E_red(srow[0:1, 128:129], srow[0:1, 0:128], OP.add)
            E_ts(srow[0:1, 129:130], srow[0:1, 128:129], 0.5 / float(F1 * D), OP.mult)
            E_mm(psum[6][:, 0:1], ones1_f[:], srow[0:1, 129:130], start=True, stop=True)
            E_copy(ahalf[:, wi:wi + 1], psum[6][:, 0:1])

        sgn = s("f_sgn", (128, 2, 128))
        msk = s("f_msk", (128, 2, 128))
        natf = s("f_natf", (128, 4, 128))

        def ternarize_T(w_ap, wi, nrow, ncol, dstT):
            kk = [0]
            for ob in range(nrow // 128):
                E_dma("sync", wtile[:, ob % 2, 0:ncol], w_ap[ts(ob, 128), :])
                for ib in range(ncol // 128):
                    r2 = ib % 2
                    E_act(sgn[:, r2, :], wtile[:, ob % 2, ts(ib, 128)], AF.Sign)
                    E_act(msk[:, r2, :], wtile[:, ob % 2, ts(ib, 128)], AF.Abs)
                    E_ts(msk[:, r2, :], msk[:, r2, :], ahalf[:, wi:wi + 1], OP.is_gt)
                    E_tt(natf[:, ib % 4, :], sgn[:, r2, :], msk[:, r2, :], OP.mult)
                    pmt = psum[kk[0] % 4][:, 0:128]
                    kk[0] += 1
                    E_tr(pmt, natf[:, ib % 4, :], ident_f[:])
                    E_copy(dstT[:, ib, ts(ob, 128)], pmt)

        xn4T = s("f_xn4T", (128, 8, L), BF16)
        xf = s("f_xf", (128, D))
        sq = s("f_sq", (128, D))
        for lt in range(NLT):
            E_dma("sync", xf[:], xres3[ts(lt, 128), :])
            ln_stats(xf, sq)
            var = stat[:, 1:2]
            t1c, t2c, vt = stat[:, 10:11], stat[:, 11:12], stat[:, 12:13]
            E_ts(t1c, var, 1.0, OP.mult, LN_EPS, OP.add)
            E_recip(t2c, t1c)
            E_tt(vt, var, t2c, OP.mult)
            E_recip(t1c, vt)
            E_act(t1c, t1c, AF.Sqrt)
            E_recip(t2c, t1c)
            E_ts(t2c, t2c, 1.0, OP.mult, 1e-8, OP.add)
            E_recip(t1c, t2c)
            E_tt(stat[:, 2:3], stat[:, 2:3], t1c, OP.mult)
            E_ts(sq[:, 0:D], xf, stat[:, 0:1], OP.subtract, stat[:, 2:3], OP.mult)
            for ib in range(8):
                E_tr(psum[6][:, 0:128], sq[:, ts(ib, 128)], ident_f[:])
                E_copy(xn4T[:, ib, ts(lt, 128)], psum[6][:, 0:128])
        LS = 256
        with nc.sbuf_tensor("f_upT", [128, 8, F1], BF16) as w_upT, \
             nc.sbuf_tensor("f_hg", [128, 32, LS], F32) as hg, \
             nc.sbuf_tensor("f_h2b", [128, 2, LS], BF16) as h2b, \
             nc.sbuf_tensor("f_hsc", [128, 2, LS], F32) as hsc, \
             nc.sbuf_tensor("f_hnb", [128, LS], BF16) as hnb, \
             nc.sbuf_tensor("f_r2r", [1, LS], F32) as r2r, \
             nc.sbuf_tensor("f_srt", [1, 3 * LS], F32) as srt:
            ternarize_T(w_up, 0, F1, D, w_upT)
            for lsl in range(L // LS):
                ph2 = psum[4][0:1, 0:LS]
                for ft in range(32):
                    pm = psum[ft % 2][:, 0:LS]
                    for ib in range(8):
                        E_mm(pm, w_upT[:, ib, ts(ft, 128)], xn4T[:, ib, ts(lsl, LS)],
                             start=(ib == 0), stop=(ib == 7))
                    E_act(hg[:, ft, :], pm, AF.Gelu)
                    E_tt(hsc[:, ft % 2, :], hg[:, ft, :], hg[:, ft, :], OP.mult)
                    E_copy(h2b[:, ft % 2, :], hsc[:, ft % 2, :])
                    E_mm(ph2, ones_cb[:], h2b[:, ft % 2, :], start=(ft == 0), stop=(ft == 31))
                E_copy(r2r[:], ph2)
                E_ts(r2r[:], r2r[:], 1.0 / float(F1), OP.mult)
                E_act(srt[:, 0:LS], r2r[:], AF.Sqrt)
                E_recip(srt[:, LS:2 * LS], srt[:, 0:LS])
                E_tt(srt[:, 2 * LS:3 * LS], srt[:, LS:2 * LS], r2r[:], OP.mult)
                E_tt(srt[:, 0:LS], srt[:, 0:LS], srt[:, 2 * LS:3 * LS], OP.add)
                E_ts(srt[:, 0:LS], srt[:, 0:LS], 0.5, OP.mult, 1e-8, OP.add)
                E_recip(r2r[:], srt[:, 0:LS])
                pmb = psum[5][:, 0:LS]
                E_mm(pmb, ones1_f[:], r2r[:], start=True, stop=True)
                for ft in range(32):
                    E_tt(hnb[:], hg[:, ft, :], pmb, OP.mult)
                    E_dma("sync", hn_s[ft, :, ts(lsl, LS)], hnb[:])
        prog.barrier()
        with nc.sbuf_tensor("f_dnT", [128, 32, D], BF16) as w_dnT, \
             nc.sbuf_tensor("f_hn", [128, 2, 32, 128], BF16) as hnt, \
             nc.sbuf_tensor("f_o", [128, 512], F32) as of:
            ternarize_T(w_down, 1, D, F1, w_dnT)
            for lt in range(NLT):
                for ft in range(32):
                    E_dma("sync", hnt[:, lt % 2, ft, :], hn_s[ft, :, ts(lt, 128)])
                for dh in range(2):
                    pm = psum[2 + dh][:, 0:512]
                    for ft in range(32):
                        E_mm(pm, hnt[:, lt % 2, ft, :], w_dnT[:, ft, ts(dh, 512)],
                             start=(ft == 0), stop=(ft == 31))
                    E_dma("sync", xf[:, 0:512], xres3[ts(lt, 128), ts(dh, 512)])
                    E_tt(of[:], xf[:, 0:512], pm, OP.add)
                    E_dma("sync", y_out[ts(lt, 128), ts(dh, 512)], of[:])
        es.close()
        prog.barrier()

    mamba_block(0)
    l0_tail()
    moe_block()
    mamba_block(1)
    l1_ffn()

    for nm, src_t, shape, dt in (("x1", xres1, (L, D), F32), ("x2", xres2, (L, D), F32),
                                 ("x3", xres3, (L, D), F32),
                                 ("ycat", ycat_s, (32, 128, L), BF16)):
        tt = tap(nm, shape, dt)
        if tt is not None:
            E_dma("gpsimd", tt, src_t)

    final_cnt = prog.finalize()

    sem_stack = ExitStack()
    sems = {e: sem_stack.enter_context(nc.semaphore("sem_" + e)) for e in ENGINES}
    with nc.Block() as block:
        def runner(eng_name):
            def body(eng):
                for fn, waits, inc in prog.ops[eng_name]:
                    for ch, val in waits.items():
                        eng.wait_ge(sems[ch], val)
                    ins = fn(eng)
                    ins.then_inc(sems[eng_name], inc)
                if eng_name == "sync":
                    for ch in ENGINES:
                        if final_cnt[ch]:
                            eng.wait_ge(sems[ch], final_cnt[ch])
            return body

        for en in ENGINES:
            getattr(block, en)(runner(en))
    sem_stack.close()

    stack.close()
    return nc


def _consts():
    oh = np.zeros((16, 16 * 128), dtype=np.float32)
    for n in range(16):
        oh[n, n * 128:(n + 1) * 128] = 1.0
    return {
        "identf": np.eye(128, dtype=np.float32),
        "identb": np.eye(128).astype(ml_dtypes.bfloat16),
        "oh16f": oh,
        "iota_cr": np.tile(np.arange(1, CAP + 1, dtype=np.float32), (128, 1)).copy(),
        "iota_cc": np.stack([np.arange(1, 129, dtype=np.float32),
                             np.arange(129, 257, dtype=np.float32)], axis=1).copy(),
        "ones1f": np.ones((1, 128), dtype=np.float32),
        "onescb": np.ones((128, 1)).astype(ml_dtypes.bfloat16),
    }


def _in_map(inputs, core):
    a = np.ascontiguousarray
    m = {
        "x": a(inputs["L0_attn_x"][core]),
        "W_in": a(inputs["L0_attn_W_in"]),
        "conv_w": a(inputs["L0_attn_conv_w"].reshape(16, 128, 4).transpose(1, 0, 2)),
        "conv_b": a(inputs["L0_attn_conv_b"].reshape(16, 128).T),
        "W_x": a(inputs["L0_attn_W_x"]),
        "W_dt": a(inputs["L0_attn_W_dt"].reshape(16, 128, R).transpose(1, 0, 2)),
        "b_dt": a(inputs["L0_attn_b_dt"].reshape(16, 128).T),
        "A_log": a(inputs["L0_attn_A_log"].reshape(16, 128, NS).transpose(1, 0, 2)),
        "Dp": a(inputs["L0_attn_Dp"].reshape(16, 128).T),
        "W_out1": a(inputs["L0_attn_W_out1"]),
        "W_out2": a(inputs["L0_attn_W_out2"]),
        "router_w1": a(inputs["L0_ffn_router_w1"].reshape(8, 128, RH).transpose(1, 0, 2)),
        "router_w2": a(inputs["L0_ffn_router_w2"]),
        "moe_w1": a(inputs["L0_ffn_w1"]),
        "moe_w2": a(inputs["L0_ffn_w2"]),
        "moe_w3": a(inputs["L0_ffn_w3"]),
        "l1_lnw": a(inputs["L1_attn_ln_w"][3]),
        "l1_lnb": a(inputs["L1_attn_ln_b"][3]),
        "l1_ip": a(inputs["L1_attn_in_proj_w"][3]),
        "l1_cw": a(inputs["L1_attn_conv_w"][3].reshape(16, 128, 4).transpose(1, 0, 2)),
        "l1_cb": a(inputs["L1_attn_conv_b"][3].reshape(16, 128).T),
        "l1_xp": a(inputs["L1_attn_x_proj_w"][3]),
        "l1_dtp": a(inputs["L1_attn_dt_proj_w"][3].reshape(16, 128, R).transpose(1, 0, 2)),
        "l1_dtb": a(inputs["L1_attn_dt_proj_b"][3].reshape(16, 128).T),
        "l1_Alog": a(inputs["L1_attn_A_log"][3].reshape(16, 128, NS).transpose(1, 0, 2)),
        "l1_D": a(inputs["L1_attn_D"][3].reshape(16, 128).T),
        "l1_op": a(inputs["L1_attn_out_proj_w"][3]),
        "w_up": a(inputs["L1_ffn_w_up"]),
        "w_down": a(inputs["L1_ffn_w_down"]),
    }
    m.update(_consts())
    return m


def _run(inputs, taps=()):
    nc = build_program(taps=taps)
    in_maps = [_in_map(inputs, core) for core in range(8)]
    res = run_bass_kernel_spmd(nc, in_maps, core_ids=list(range(8)))
    return res


def kernel(**inputs):
    res = _run(inputs)
    out = np.stack([np.asarray(r["y"], dtype=np.float32) for r in res.results], axis=0)
    return out


# version 19
# speedup vs baseline: 1.0050x; 1.0028x over previous
import numpy as np
import ml_dtypes
import concourse.bass as bass
import concourse.mybir as mybir
from concourse.alu_op_type import AluOpType
from concourse.bass_utils import run_bass_kernel_spmd
from contextlib import ExitStack

F32 = mybir.dt.float32
BF16 = mybir.dt.bfloat16
AF = mybir.ActivationFunctionType
OP = AluOpType

L, D, DI, NS, R = 2048, 1024, 2048, 16, 64
E, CAP, DFF, F1, RH = 16, 256, 2048, 4096, 128
T = 256
NCH = L // T
NLT = L // 128
LN_EPS = 1e-5
ENGINES = ("sync", "scalar", "vector", "tensor", "gpsimd")


def ts(i, n):
    return slice(i * n, (i + 1) * n)


_PSUM_NAMES = frozenset(["P0", "P1", "P2", "P3", "P4", "P5", "P6", "PB"])


def _ap_range(ap):
    if ap.__class__.__name__.endswith("TensorHandle"):
        ap = ap[:]
    name = ap.tensor.name
    if name in _PSUM_NAMES:
        return name, 0, 1 << 30
    off = ap.offset
    lo = off
    hi = off
    for step, count in ap.ap:
        if count <= 1:
            continue
        d = step * (count - 1)
        if d >= 0:
            hi += d
        else:
            lo += d
    return name, lo, hi + 1


class Prog:

    def __init__(self):
        self.ops = {e: [] for e in ENGINES}
        self.writers = {}
        self.readers = {}
        self.pending = {}

    def _deps_from(self, table, name, lo, hi, deps):
        for (l2, h2, ch, idx) in table.get(name, ()):
            if l2 < hi and lo < h2:
                if deps.get(ch, -1) < idx:
                    deps[ch] = idx

    def emit(self, eng, fn, reads=(), writes=(), dma=False, dma_fence=False):
        deps = dict(self.pending.pop(eng, ()))
        rr = [_ap_range(a) for a in reads if a is not None]
        ww = [_ap_range(a) for a in writes if a is not None]
        for (name, lo, hi) in rr:
            self._deps_from(self.writers, name, lo, hi, deps)
        for (name, lo, hi) in ww:
            self._deps_from(self.writers, name, lo, hi, deps)
            self._deps_from(self.readers, name, lo, hi, deps)
        if eng == "tensor" and not dma:
            deps.pop(eng, None)
        idx = len(self.ops[eng])
        self.ops[eng].append((fn, deps, dma))
        for (name, lo, hi) in ww:
            self.writers.setdefault(name, []).append((lo, hi, eng, idx))
            wl = self.writers[name]
            if len(wl) > 96:
                del wl[:32]
        for (name, lo, hi) in rr:
            self.readers.setdefault(name, []).append((lo, hi, eng, idx))
            rl = self.readers[name]
            if len(rl) > 96:
                del rl[:32]

    def barrier(self):
        snap = {e: len(self.ops[e]) - 1 for e in ENGINES if self.ops[e]}
        self.writers = {}
        self.readers = {}
        self.pending = {e: dict(snap) for e in ENGINES}

    def finalize(self):
        needed = {e: set() for e in ENGINES}
        for eng in ENGINES:
            for (fn, deps, dma) in self.ops[eng]:
                for ch, j in deps.items():
                    needed[ch].add(j)
        for e in ENGINES:
            if self.ops[e]:
                needed[e].add(len(self.ops[e]) - 1)
        val_after = {}
        incs = {}
        for e in ENGINES:
            v = 0
            arr = []
            ia = []
            for i, (fn, deps, dma) in enumerate(self.ops[e]):
                if dma:
                    v += 16
                    ia.append(16)
                elif i in needed[e]:
                    v += 1
                    ia.append(1)
                else:
                    ia.append(0)
                arr.append(v)
            val_after[e] = arr
            incs[e] = ia
        resolved = {e: [] for e in ENGINES}
        clock = {e: {c: 0 for c in ENGINES} for e in ENGINES}
        for e in ENGINES:
            kl = clock[e]
            for i, (fn, deps, dma) in enumerate(self.ops[e]):
                waits = {}
                for ch, j in deps.items():
                    val = val_after[ch][j]
                    if val > kl[ch]:
                        waits[ch] = val
                        kl[ch] = val
                resolved[e].append((fn, waits, incs[e][i]))
        finals = {e: (val_after[e][-1] if val_after[e] else 0) for e in ENGINES}
        return resolved, finals


def build_program(taps=()):
    nc = bass.Bass("TRN2", target_bir_lowering=False)
    prog = Prog()
    em = prog.emit

    def din(name, shape, dt=F32):
        return nc.dram_tensor(name, list(shape), dt, kind="ExternalInput").ap()

    x0 = din("x", (L, D))
    W_in = din("W_in", (2 * DI, D))
    conv_w = din("conv_w", (128, 16, 4))
    conv_b = din("conv_b", (128, 16))
    W_x = din("W_x", (R + 2 * NS, DI))
    W_dt = din("W_dt", (128, 16, R))
    b_dt = din("b_dt", (128, 16))
    A_log = din("A_log", (128, 16, NS))
    Dp = din("Dp", (128, 16))
    W_out1 = din("W_out1", (DI, 2 * DI))
    W_out2 = din("W_out2", (D, DI))
    r_w1 = din("router_w1", (128, 8, RH))
    r_w2 = din("router_w2", (RH, E))
    moe_w1 = din("moe_w1", (E, D, DFF))
    moe_w2 = din("moe_w2", (E, D, DFF))
    moe_w3 = din("moe_w3", (E, DFF, D))
    l1_lnw = din("l1_lnw", (D,))
    l1_lnb = din("l1_lnb", (D,))
    l1_ip = din("l1_ip", (2 * DI, D))
    l1_cw = din("l1_cw", (128, 16, 4))
    l1_cb = din("l1_cb", (128, 16))
    l1_xp = din("l1_xp", (R + 2 * NS, DI))
    l1_dtp = din("l1_dtp", (128, 16, R))
    l1_dtb = din("l1_dtb", (128, 16))
    l1_Alog = din("l1_Alog", (128, 16, NS))
    l1_D = din("l1_D", (128, 16))
    l1_op = din("l1_op", (D, DI))
    w_up = din("w_up", (F1, D))
    w_down = din("w_down", (D, F1))
    identf = din("identf", (128, 128))
    identb = din("identb", (128, 128), BF16)
    oh16f_in = din("oh16f", (16, 16 * 128))
    iota_cr = din("iota_cr", (128, CAP))
    iota_cc = din("iota_cc", (128, 2))
    ones1f = din("ones1f", (1, 128))
    onescb = din("onescb", (128, 1), BF16)

    y_out = nc.dram_tensor("y", [L, D], F32, kind="ExternalOutput").ap()

    def dscr(name, shape, dt=F32):
        return nc.dram_tensor(name, list(shape), dt, kind="Internal").ap()

    xres1 = dscr("xres1", (L, D))
    xres2 = dscr("xres2", (L, D))
    xres3 = dscr("xres3", (L, D))
    ycat_s = dscr("ycat_s", (32, 128, L), BF16)
    hn_s = dscr("hn_s", (32, 128, L), BF16)
    s_WinT = dscr("s_WinT", (D, 2 * DI), BF16)
    s_Wo1T = dscr("s_Wo1T", (2 * DI, DI), BF16)
    s_Wo2T = dscr("s_Wo2T", (DI, D), BF16)
    s_ipT = dscr("s_ipT", (D, 2 * DI), BF16)
    s_opT = dscr("s_opT", (DI, D), BF16)

    tap_t = {}

    def tap(name, shape, dt=F32):
        if name not in taps:
            return None
        if name not in tap_t:
            tap_t[name] = nc.dram_tensor("tap_" + name, list(shape), dt,
                                         kind="ExternalOutput").ap()
        return tap_t[name]

    stack = ExitStack()

    def sbp(name, shape, dt=F32):
        return stack.enter_context(nc.sbuf_tensor(name, list(shape), dt))

    ident_f = sbp("ident_f", (128, 128))
    ident_b = sbp("ident_b", (128, 128), BF16)
    ones1_f = sbp("ones1_f", (1, 128))
    ones_cb = sbp("ones_cb", (128, 1), BF16)
    oh16_f = sbp("oh16_f", (16, 16 * 128))
    carry = sbp("carry", (128, 16 * NS))
    stat = sbp("stat", (128, 20))
    srow = sbp("srow", (16, 192))

    psum = [stack.enter_context(nc.psum_tensor("P%d" % i, [128, 512], F32)) for i in range(7)]
    psum_b = stack.enter_context(nc.psum_tensor("PB", [128, 1024], BF16))

    def _ap(x):
        if x is not None and x.__class__.__name__.endswith("TensorHandle"):
            return x[:]
        return x

    def E_dma(eng, out, in_):
        out, in_ = _ap(out), _ap(in_)
        em(eng, (lambda e, out=out, in_=in_: e.dma_start(out=out, in_=in_)), dma=True,
           reads=[in_], writes=[out])

    def E_dmaT(out, in_):
        out, in_ = _ap(out), _ap(in_)
        em("sync", (lambda e, out=out, in_=in_: e.dma_start(out=out, in_=in_, transpose=True)),
           dma=True, reads=[in_], writes=[out])

    def E_act(out, in_, func, bias=None, scale=None, eng="scalar"):
        out, in_ = _ap(out), _ap(in_)
        kw = {}
        rds = [in_]
        if bias is not None:
            kw["bias"] = _ap(bias)
            if not isinstance(bias, float):
                rds.append(kw["bias"])
        if scale is not None:
            kw["scale"] = _ap(scale)
            if not isinstance(scale, float):
                rds.append(kw["scale"])
        em(eng, (lambda e, out=out, in_=in_, func=func, kw=kw:
                 e.activation(out=out, in_=in_, func=func, **kw)), reads=rds, writes=[out])

    def E_tt(out, in0, in1, op, eng="vector"):
        out, in0, in1 = _ap(out), _ap(in0), _ap(in1)
        em(eng, (lambda e, out=out, in0=in0, in1=in1, op=op:
                 e.tensor_tensor(out=out, in0=in0, in1=in1, op=op)),
           reads=[in0, in1], writes=[out])

    def E_ts(out, in0, s1, op0, s2=None, op1=None, accum_out=None, eng="vector"):
        out, in0, s1, s2 = _ap(out), _ap(in0), _ap(s1), _ap(s2)
        kw = {}
        rds = [in0]
        wrs = [out]
        for s in (s1, s2):
            if s is not None and not isinstance(s, float):
                rds.append(s)
        if op1 is not None:
            kw["op1"] = op1
        if accum_out is not None:
            kw["accum_out"] = _ap(accum_out)
            wrs.append(kw["accum_out"])
        em(eng, (lambda e, out=out, in0=in0, s1=s1, s2=s2, op0=op0, kw=kw:
                 e.tensor_scalar(out=out, in0=in0, scalar1=s1, scalar2=s2, op0=op0, **kw)),
           reads=rds, writes=wrs)

    def E_stt(out, in0, scalar, in1, op0, op1, eng="vector"):
        out, in0, scalar, in1 = _ap(out), _ap(in0), _ap(scalar), _ap(in1)
        rds = [in0, in1]
        if not isinstance(scalar, float):
            rds.append(scalar)
        em(eng, (lambda e, out=out, in0=in0, sc=scalar, in1=in1, op0=op0, op1=op1:
                 e.scalar_tensor_tensor(out=out, in0=in0, scalar=sc, in1=in1, op0=op0, op1=op1)),
           reads=rds, writes=[out])

    def E_red(out, in_, op, absval=False, eng="vector"):
        out, in_ = _ap(out), _ap(in_)
        kw = {"apply_absolute_value": True} if absval else {}
        em(eng, (lambda e, out=out, in_=in_, op=op, kw=kw:
                 e.tensor_reduce(out=out, in_=in_, axis=mybir.AxisListType.X, op=op, **kw)),
           reads=[in_], writes=[out])

    def E_scan(out, d0, d1, initial):
        out, d0, d1, initial = _ap(out), _ap(d0), _ap(d1), _ap(initial)
        rds = [d0, d1]
        if not isinstance(initial, float):
            rds.append(initial)
        em("vector", (lambda e, out=out, d0=d0, d1=d1, init=initial:
                      e.tensor_tensor_scan(out=out, data0=d0, data1=d1, initial=init,
                                           op0=OP.mult, op1=OP.add)),
           reads=rds, writes=[out])

    def E_mm(out, lhsT, rhs, start, stop):
        out, lhsT, rhs = _ap(out), _ap(lhsT), _ap(rhs)
        em("tensor", (lambda e, out=out, l=lhsT, r=rhs, st=start, sp=stop:
                      e.matmul(out=out, lhsT=l, rhs=r, start=st, stop=sp)),
           reads=[lhsT, rhs], writes=[out])

    def E_tr(out, in_, identity):
        out, in_, identity = _ap(out), _ap(in_), _ap(identity)
        k = in_.shape[0]
        if identity.shape[0] != k:
            identity = identity[0:k, 0:k]
        em("tensor", (lambda e, out=out, in_=in_, idn=identity:
                      e.transpose(out=out, in_=in_, identity=idn)),
           reads=[in_, identity], writes=[out])

    def E_copy(out, in_, eng="scalar"):
        out, in_ = _ap(out), _ap(in_)
        if eng == "scalar":
            E_act(out, in_, AF.Copy)
        else:
            em(eng, (lambda e, out=out, in_=in_: e.tensor_copy(out=out, in_=in_)),
               reads=[in_], writes=[out])

    def E_memset(ap, v, eng="vector"):
        ap = _ap(ap)
        em(eng, (lambda e, ap=ap, v=v: e.memset(ap, v)), writes=[ap])

    def E_recip(out, in_):
        out, in_ = _ap(out), _ap(in_)
        em("vector", (lambda e, out=out, in_=in_: e.reciprocal(out=out, in_=in_)),
           reads=[in_], writes=[out])

    def rsqrt2(out, v, t1):
        E_recip(t1, v)
        E_act(out, t1, AF.Sqrt)

    def E_bnstats(out, in_):
        out, in_ = _ap(out), _ap(in_)
        em("vector", (lambda e, out=out, in_=in_: e.bn_stats(out=out, in_=in_)),
           reads=[in_], writes=[out])

    def E_bnaggr(out, in_):
        out, in_ = _ap(out), _ap(in_)
        em("vector", (lambda e, out=out, in_=in_: e.bn_aggr(out=out, in_=in_)),
           reads=[in_], writes=[out])

    def ln_stats(x_t, sq, Dd=D, eps=LN_EPS):
        m, var, rs = stat[:, 0:1], stat[:, 1:2], stat[:, 2:3]
        t1 = stat[:, 3:4]
        nh = Dd // 512
        for h in range(nh):
            E_bnstats(stat[:, 4 + 6 * h:10 + 6 * h], x_t[:, ts(h, 512)])
        E_bnaggr(stat[:, 0:2], stat[:, 4:4 + 6 * nh])
        E_ts(t1, var, 1.0, OP.mult, eps, OP.add)
        rsqrt2(rs, t1, stat[:, 3:4]) if False else None
        E_recip(stat[:, 3:4], t1)
        E_act(rs, stat[:, 3:4], AF.Sqrt)

    def prep_T(w_ap, o_dim, i_dim, dst, st_nat, st_row):
        OSUP = 1024
        k = [0]
        for ib in range(i_dim // 128):
            rn = ib % 2
            for osb in range(o_dim // OSUP):
                src = w_ap[ts(osb, OSUP), ts(ib, 128)].rearrange("(a p) i -> p a i", p=128)
                E_dma("gpsimd", st_nat[:, rn, :, :], src)
                for a in range(OSUP // 128):
                    o0 = osb * OSUP + a * 128
                    pmt = psum[k[0] % 4][:, 0:128]
                    k[0] += 1
                    E_tr(pmt, st_nat[:, rn, a, :], ident_f[:])
                    E_copy(st_row[:, rn, o0:o0 + 128], pmt)
            E_dma("scalar", dst[ts(ib, 128), :], st_row[:, rn, 0:o_dim])

    E_dma("sync", ident_f[:], identf)
    with nc.sbuf_tensor("pr_nat", [128, 2, 8, 128], F32) as st_nat, \
         nc.sbuf_tensor("pr_row", [128, 2, 4096], BF16) as st_row:
        prep_T(W_in, 2 * DI, D, s_WinT, st_nat, st_row)
        prep_T(W_out1, DI, 2 * DI, s_Wo1T, st_nat, st_row)
        prep_T(W_out2, D, DI, s_Wo2T, st_nat, st_row)
        prep_T(l1_ip, 2 * DI, D, s_ipT, st_nat, st_row)
        prep_T(l1_op, D, DI, s_opT, st_nat, st_row)
    prog.barrier()

    E_dma("sync", ident_b[:], identb)
    E_dma("sync", ones1_f[:], ones1f)
    E_dma("sync", ones_cb[:], onescb)
    E_dma("sync", oh16_f[:], oh16f_in)

    def mamba_block(which):
        l0 = which == 0
        sfx = "_m%d" % which
        x_in = x0 if l0 else xres2
        winT_scr = s_WinT if l0 else s_ipT
        w_x_in = W_x if l0 else l1_xp
        w_dt_in = W_dt if l0 else l1_dtp
        cw_in = conv_w if l0 else l1_cw
        cb_in = conv_b if l0 else l1_cb
        bdt_in = b_dt if l0 else l1_dtb
        alog_in = A_log if l0 else l1_Alog
        dp_in = Dp if l0 else l1_D

        es = ExitStack()

        def s(name, shape, dt=F32):
            return es.enter_context(nc.sbuf_tensor(name + sfx, list(shape), dt))

        cwc = s("cwc", (128, 16, 4))
        cbc = s("cbc", (128, 16))
        bdtc = s("bdtc", (128, 16))
        dpc = s("dpc", (128, 16))
        ac = s("ac", (128, 16 * NS))
        wxT = s("wxT", (128, 16, 96), BF16)
        wdtT = s("wdtT", (64, DI), BF16)
        E_dma("sync", cwc[:], cw_in)
        E_dma("sync", cbc[:], cb_in)
        E_dma("sync", bdtc[:], bdt_in)
        E_dma("sync", dpc[:], dp_in)
        E_dma("sync", ac[:].rearrange("p (a n) -> p a n", n=NS), alog_in)
        E_act(ac[:], ac[:], AF.Exp)
        E_ts(ac[:], ac[:], -1.0, OP.mult)
        if l0:
            dpb = s("dpb", (128, 16), BF16)
            atb = s("atb", (16, DI), BF16)
            E_copy(dpb[:], dpc[:])
            for db in range(16):
                E_tr(psum[6][0:NS, 0:128], ac[:, ts(db, NS)], ident_f[:])
                E_copy(atb[:, ts(db, 128)], psum[6][0:NS, 0:128])
        else:
            lnw_bc = s("lnwbc", (128, D), BF16)
            lnb_bc = s("lnbbc", (128, D), BF16)
            with nc.sbuf_tensor("lnr" + sfx, [1, 2 * D], F32) as lnr:
                E_dma("sync", lnr[:, 0:D], l1_lnw.rearrange("(a d) -> a d", a=1))
                E_dma("sync", lnr[:, D:2 * D], l1_lnb.rearrange("(a d) -> a d", a=1))
                for i, dstt in ((0, lnw_bc), (1, lnb_bc)):
                    for j in range(2):
                        E_mm(psum[6][:, 0:512], ones1_f[:],
                             lnr[:, i * D + j * 512: i * D + (j + 1) * 512], start=True, stop=True)
                        E_copy(dstt[:, ts(j, 512)], psum[6][:, 0:512])
            prog.barrier()
        with nc.sbuf_tensor("smw" + sfx, [96, DI], BF16) as smw:
            E_dma("gpsimd", smw[:], w_x_in)
            for ib in range(16):
                E_tr(psum_b[:, 0:96], smw[:, ts(ib, 128)], ident_b[:])
                E_copy(wxT[:, ib, :], psum_b[:, 0:96])
        prog.barrier()
        with nc.sbuf_tensor("smd" + sfx, [128, 16, R], BF16) as smd:
            E_dma("gpsimd", smd[:], w_dt_in)
            for ib in range(16):
                E_tr(psum_b[0:R, 0:128], smd[:, ib, :], ident_b[:])
                E_copy(wdtT[:, ts(ib, 128)], psum_b[0:R, 0:128])
        prog.barrier()

        xf = s("xf", (128, D))
        sq = s("sq", (128, D))
        xT = s("xT", (128, 8, T), BF16)
        winP = s("winP", (128, 8, 512), BF16)
        xih = s("xih", (128, 16, T + 3))
        halo = s("halo", (128, 16, 3))
        u_f = s("u", (128, 16, T))
        ub = s("ub", (128, 16, T), BF16) if l0 else s("ub", (128, 2, T), BF16)
        sz = s("sz", (128, 2, 16, T), BF16)
        dtrb = s("dtrb", (64, T), BF16)
        btf = s("btf", (16, T))
        ctf = s("ctf", (16, T))
        btil = s("btil", (128, NS, T))
        ctil = s("ctil", (128, NS, T), BF16)
        dl_all = s("dl_all", (128, 16, T))
        sp_s = s("sp_s", (128, 2, T))
        dA_all = s("dA_all", (128, NS, T))
        bsc_all = s("bsc_all", (128, NS, T))
        du = s("du", (128, T))
        s_all = s("s_all", (128, NS, T))
        sC = s("sC", (128, NS, T), BF16)
        t1 = s("t1", (128, T))
        ycb = s("ycb", (128, 2, T), BF16)
        if l0:
            ctb = s("ctb", (16, T), BF16)
            sdu = btf[0:1, :]
        else:
            ytb = s("ytb", (128, 16, T), BF16)
            opt_t = s("optt", (128, 1, D), BF16)

        tp_xn = tap("xn%d" % which, (L, D))
        tp_u = tap("u%d" % which, (128, 16, T))
        tp_dl = tap("dl%d" % which, (128, 16, T))
        tp_bt = tap("bt%d" % which, (16, T))
        tp_sz = tap("sz%d" % which, (128, 16, T))
        tp_s0 = tap("s%d" % which, (128, NS, T))

        E_memset(carry[:], 0.0)

        def ln_xz(c):
            t0 = c * T
            for lt in range(2):
                row0 = t0 + lt * 128
                E_dma("sync", xf[:], x_in[row0:row0 + 128, :])
                ln_stats(xf, sq)
                if not l0:
                    var = stat[:, 1:2]
                    t1c, t2c, vt = stat[:, 10:11], stat[:, 11:12], stat[:, 12:13]
                    E_ts(t1c, var, 1.0, OP.mult, LN_EPS, OP.add)
                    E_recip(t2c, t1c)
                    E_tt(vt, var, t2c, OP.mult)
                    E_ts(vt, vt, 25.0, OP.mult, LN_EPS, OP.add)
                    rsqrt2(t2c, vt, stat[:, 13:14])
                    E_ts(t2c, t2c, 5.0, OP.mult)
                    E_tt(stat[:, 2:3], stat[:, 2:3], t2c, OP.mult)
                E_ts(sq[:, 0:D], xf, stat[:, 0:1], OP.subtract, stat[:, 2:3], OP.mult)
                if not l0:
                    E_tt(sq[:, 0:D], sq[:, 0:D], lnw_bc[:], OP.mult)
                    E_tt(sq[:, 0:D], sq[:, 0:D], lnb_bc[:], OP.add)
                if tp_xn is not None:
                    E_dma("gpsimd", tp_xn[row0:row0 + 128, :], sq[:, 0:D])
                for ib in range(8):
                    pmt = psum[4 + ib % 2][:, 0:128]
                    E_tr(pmt, sq[:, ts(ib, 128)], ident_f[:])
                    E_copy(xT[:, ib, lt * 128:(lt + 1) * 128], pmt)
            for q in range(8):
                for ib in range(8):
                    E_dma("scalar", winP[:, ib, :], winT_scr[ts(ib, 128), q * 512:(q + 1) * 512])
                for eq in range(4):
                    et = q * 4 + eq
                    pm = psum[eq % 2][:, 0:T]
                    for ib in range(8):
                        E_mm(pm, winP[:, ib, ts(eq, 128)], xT[:, ib, :],
                             start=(ib == 0), stop=(ib == 7))
                    if et < 16:
                        if c == 0:
                            E_memset(xih[:, et, 0:3], 0.0)
                        else:
                            E_copy(xih[:, et, 0:3], halo[:, et, :])
                        E_copy(xih[:, et, 3:T + 3], pm)
                    else:
                        E_act(sz[:, c % 2, et - 16, :], pm, AF.Silu)

        ln_xz(0)
        for c in range(NCH):
            t0 = c * T
            for k in range(4):
                for et in range(16):
                    tconv = dA_all[:, et, :]
                    if k == 0:
                        E_ts(tconv, xih[:, et, 0:T], cwc[:, et, 0:1], OP.mult)
                    else:
                        E_stt(tconv, xih[:, et, k:k + T], cwc[:, et, k:k + 1], tconv,
                              OP.mult, OP.add)
            for et in range(16):
                E_copy(halo[:, et, :], xih[:, et, T:T + 3])
                E_act(u_f[:, et, :], dA_all[:, et, :], AF.Silu, bias=cbc[:, et:et + 1])
                if l0:
                    E_copy(ub[:, et, :], u_f[:, et, :])
            if c == 0 and tp_u is not None:
                E_dma("gpsimd", tp_u, u_f[:])
            if c == 0 and tp_sz is not None:
                E_dma("gpsimd", tp_sz, sz[:, 0])
            for ib in range(16):
                if l0:
                    ubt = ub[:, ib, :]
                else:
                    ubt = ub[:, ib % 2, :]
                    E_copy(ubt, u_f[:, ib, :])
                E_mm(psum[0][0:64, 0:T], wxT[:, ib, 0:64], ubt, start=(ib == 0), stop=(ib == 15))
                E_mm(psum[1][0:16, 0:T], wxT[:, ib, 64:80], ubt, start=(ib == 0), stop=(ib == 15))
                E_mm(psum[2][0:16, 0:T], wxT[:, ib, 80:96], ubt, start=(ib == 0), stop=(ib == 15))
            E_copy(dtrb[:], psum[0][0:64, 0:T])
            E_copy(btf[:], psum[1][0:16, 0:T])
            E_copy(ctf[:], psum[2][0:16, 0:T])
            if l0:
                E_copy(ctb[:], psum[2][0:16, 0:T])
            if c == 0 and tp_bt is not None:
                E_dma("gpsimd", tp_bt, btf[:])
            for n in range(NS):
                pm = psum[0 + n % 2][:, 0:T]
                E_mm(pm, oh16_f[:, ts(n, 128)], btf[:], start=True, stop=True)
                E_copy(btil[:, n, :], pm)
                pm2 = psum[2 + n % 2][:, 0:T]
                E_mm(pm2, oh16_f[:, ts(n, 128)], ctf[:], start=True, stop=True)
                E_copy(ctil[:, n, :], pm2, eng="vector")
            for db in range(16):
                pm = psum[4 + db % 2][:, 0:T]
                sp = sp_s[:, db % 2, :]
                dl = dl_all[:, db, :]
                E_mm(pm, wdtT[:, ts(db, 128)], dtrb[:], start=True, stop=True)
                E_act(sp, pm, AF.Abs, bias=bdtc[:, db:db + 1])
                E_act(sp, sp, AF.Exp, scale=-1.0)
                E_act(sp, sp, AF.Ln, bias=1.0)
                E_act(dl, pm, AF.Relu, bias=bdtc[:, db:db + 1])
                E_tt(dl, dl, sp, OP.add, eng="gpsimd")
            if c == 0 and tp_dl is not None:
                E_dma("gpsimd", tp_dl, dl_all[:])
            if c + 1 < NCH:
                ln_xz(c + 1)
            def tail(db):
                r = db % 2
                py = psum[2 + r][:, 0:T]
                E_stt(t1[:], u_f[:, db, :], dpc[:, db:db + 1], py, OP.mult, OP.add)
                E_tt(ycb[:, r, :], t1[:], sz[:, c % 2, db, :], OP.mult)
                if l0:
                    E_dma("sync", ycat_s[db, :, t0:t0 + T], ycb[:, r, :])
                else:
                    E_copy(ytb[:, db, :], ycb[:, r, :], eng="vector")

            for db in range(16):
                r = db % 2
                dl = dl_all[:, db, :]
                dub = du[:]
                E_tt(dub, dl, u_f[:, db, :], OP.mult, eng="gpsimd")
                dub_b = dub.rearrange("p (a t) -> p a t", a=1).broadcast_to([128, 4, T])
                for g in range(4):
                    em("gpsimd", (lambda e, out=bsc_all[:, 4 * g:4 * g + 4, :], in0=dub_b,
                                  in1=btil[:, 4 * g:4 * g + 4, :]:
                                  e.tensor_tensor(out=out, in0=in0, in1=in1, op=OP.mult)),
                       reads=[dub, btil[:, 4 * g:4 * g + 4, :]],
                       writes=[bsc_all[:, 4 * g:4 * g + 4, :]])
                py = psum[2 + r][:, 0:T]
                for n in range(NS):
                    E_act(dA_all[:, n, :], dl, AF.Exp, scale=ac[:, db * NS + n:db * NS + n + 1])
                for n in range(NS):
                    E_scan(s_all[:, n, :], dA_all[:, n, :], bsc_all[:, n, :],
                           carry[:, db * NS + n:db * NS + n + 1])
                E_copy(carry[:, ts(db, NS)], s_all[:, :, T - 1], eng="vector")
                E_tt(sC[:], s_all[:], ctil[:], OP.mult)
                for n in range(NS):
                    E_mm(py, ident_b[:], sC[:, n, :], start=(n == 0), stop=(n == NS - 1))
                if c == 0 and db == 0 and tp_s0 is not None:
                    E_dma("gpsimd", tp_s0, s_all[:])
                if db > 0:
                    tail(db - 1)
            tail(15)
            if l0:
                psd = psum[6][0:1, 0:T]
                for db in range(16):
                    E_mm(psd, dpb[:, db:db + 1], ub[:, db, :], start=(db == 0), stop=(db == 15))
                E_copy(sdu[:], psd)
                for db2 in range(16):
                    pm = psum[2 + db2 % 2][:, 0:T]
                    E_mm(pm, atb[:, ts(db2, 128)], ctb[:], start=True, stop=False)
                    E_mm(pm, ones1_f[:], sdu[:], start=False, stop=True)
                    E_copy(ycb[:, db2 % 2, :], pm)
                    E_dma("sync", ycat_s[16 + db2, :, t0:t0 + T], ycb[:, db2 % 2, :])
            else:
                pm00 = psum[2][:, 0:512]
                pm01 = psum[3][:, 0:512]
                pm10 = psum[6][:, 0:512]
                pm11 = psum[5][:, 0:512]
                for db in range(16):
                    E_dma("sync", opt_t[:, 0, :], s_opT[ts(db, 128), :])
                    st = (db == 0)
                    sp2 = (db == 15)
                    E_mm(pm00, ytb[:, db, 0:128], opt_t[:, 0, 0:512], start=st, stop=sp2)
                    E_mm(pm01, ytb[:, db, 0:128], opt_t[:, 0, 512:1024], start=st, stop=sp2)
                    E_mm(pm10, ytb[:, db, 128:256], opt_t[:, 0, 0:512], start=st, stop=sp2)
                    E_mm(pm11, ytb[:, db, 128:256], opt_t[:, 0, 512:1024], start=st, stop=sp2)
                for lsub in range(2):
                    row0 = t0 + lsub * 128
                    for dh in range(2):
                        pm = (pm00, pm01, pm10, pm11)[lsub * 2 + dh]
                        rb = xf[:, ts(dh, 512)]
                        E_dma("sync", rb, xres2[row0:row0 + 128, ts(dh, 512)])
                        E_tt(rb, rb, pm, OP.add)
                        E_dma("sync", xres3[row0:row0 + 128, ts(dh, 512)], rb)
        es.close()
        prog.barrier()

    def l0_tail():
        es = ExitStack()

        def s(name, shape, dt=F32):
            return es.enter_context(nc.sbuf_tensor(name, list(shape), dt))

        h1T = s("t_h1T", (128, 16, L), BF16)
        wo2 = s("t_wo2", (128, 16, D), BF16)
        wo1 = s("t_wo1", (128, 32, 1024), BF16)
        yc = s("t_yc", (128, 2, 32, 256), BF16)
        xs = s("t_xs", (128, 512))
        o2 = s("t_o2", (128, 512))
        for ib in range(16):
            E_dma("sync", wo2[:, ib, :], s_Wo2T[ts(ib, 128), :])
        for fh in range(2):
            for cb in range(32):
                E_dma("sync", wo1[:, cb, :], s_Wo1T[ts(cb, 128), fh * 1024:(fh + 1) * 1024])
            for lsl in range(8):
                for cb in range(32):
                    E_dma("sync", yc[:, lsl % 2, cb, :], ycat_s[cb, :, lsl * 256:(lsl + 1) * 256])
                for ft in range(8):
                    pm = psum[ft % 2][:, 0:256]
                    for cb in range(32):
                        E_mm(pm, wo1[:, cb, ts(ft, 128)], yc[:, lsl % 2, cb, :],
                             start=(cb == 0), stop=(cb == 31))
                    E_act(h1T[:, fh * 8 + ft, lsl * 256:(lsl + 1) * 256], pm, AF.Relu)
        for lt in range(NLT):
            for dh in range(2):
                pm = psum[2 + dh][:, 0:512]
                for ft in range(16):
                    E_mm(pm, h1T[:, ft, lt * 128:(lt + 1) * 128], wo2[:, ft, ts(dh, 512)],
                         start=(ft == 0), stop=(ft == 15))
                E_dma("sync", xs[:], x0[ts(lt, 128), ts(dh, 512)])
                E_tt(o2[:], xs[:], pm, OP.add)
                E_dma("sync", xres1[ts(lt, 128), ts(dh, 512)], o2[:])
        es.close()
        prog.barrier()

    def moe_block():
        es = ExitStack()

        def s(name, shape, dt=F32):
            return es.enter_context(nc.sbuf_tensor(name, list(shape), dt))

        iotacr = s("m_iotacr", (128, CAP))
        iotacc = s("m_iotacc", (128, 2))
        E_dma("sync", iotacr[:], iota_cr)
        E_dma("sync", iotacc[:], iota_cc)
        xn_tok = s("m_xn", (128, 16, D), BF16)
        xf = s("m_xf", (128, D))
        rankm = s("m_rankm", (16, L))
        rmT = s("m_rmT", (128, 16, 16))
        pT = s("m_pT", (128, 16, 16), BF16)
        routing_es = ExitStack()
        probs = routing_es.enter_context(nc.sbuf_tensor("m_probs", [16, L], F32))
        cmp_s = routing_es.enter_context(nc.sbuf_tensor("m_cmp", [16, L], F32))
        ones_r = routing_es.enter_context(nc.sbuf_tensor("m_ones", [16, L], F32))
        lo, hi, th = srow[:, 0:1], srow[:, 1:2], srow[:, 2:3]
        cntc, pred, dnm = srow[:, 3:4], srow[:, 4:5], srow[:, 5:6]
        mx, sm = srow[:, 6:7], srow[:, 7:8]
        tp_pr = tap("probs", (16, L))
        tp_rk = tap("rankm", (16, L))
        tp_ein = tap("ein", (128, 8, CAP), BF16)
        tp_gv = tap("gv", (128, 16, CAP), BF16)

        with nc.sbuf_tensor("m_xnT", [128, 8, L], BF16) as xnT, \
             nc.sbuf_tensor("m_hrT", [128, L], BF16) as hrT, \
             nc.sbuf_tensor("m_sq", [128, D], F32) as sq, \
             nc.sbuf_tensor("m_rw1", [128, 8, RH], BF16) as rw1b, \
             nc.sbuf_tensor("m_rw2", [128, E], BF16) as rw2b:
            E_dma("gpsimd", rw1b[:], r_w1)
            E_dma("gpsimd", rw2b[:], r_w2)
            for lt in range(NLT):
                E_dma("sync", xf[:], xres1[ts(lt, 128), :])
                ln_stats(xf, sq)
                E_ts(xn_tok[:, lt, :], xf, stat[:, 0:1], OP.subtract, stat[:, 2:3], OP.mult)
                E_copy(sq[:, 0:D], xn_tok[:, lt, :])
                for ib in range(8):
                    E_tr(psum[6][:, 0:128], sq[:, ts(ib, 128)], ident_f[:])
                    E_copy(xnT[:, ib, ts(lt, 128)], psum[6][:, 0:128])
            for j in range(4):
                pm = psum[j % 2][:, 0:512]
                for ib in range(8):
                    E_mm(pm, rw1b[:, ib, :], xnT[:, ib, ts(j, 512)], start=(ib == 0), stop=(ib == 7))
                E_act(hrT[:, ts(j, 512)], pm, AF.Silu)
            for j in range(4):
                pm = psum[2 + j % 2][0:16, 0:512]
                E_mm(pm, rw2b[:], hrT[:, ts(j, 512)], start=True, stop=True)
                E_copy(probs[:, ts(j, 512)], pm)
        prog.barrier()
        E_red(mx, probs[:], OP.max)
        E_ts(mx, mx, -1.0, OP.mult)
        E_act(probs[:], probs[:], AF.Exp, bias=mx)
        E_red(sm, probs[:], OP.add)
        E_recip(sm, sm)
        E_ts(probs[:], probs[:], sm, OP.mult)
        if tp_pr is not None:
            E_dma("gpsimd", tp_pr, probs[:])
        E_memset(lo, 0.0)
        E_memset(hi, 1.1)
        tb1, tb2 = srow[:, 8:9], srow[:, 9:10]
        for _ in range(44):
            E_tt(th, lo, hi, OP.add)
            E_ts(th, th, 0.5, OP.mult)
            E_ts(cmp_s[:], probs[:], th, OP.is_ge, op1=OP.add, accum_out=cntc)
            E_ts(pred, cntc, float(CAP) - 0.5, OP.is_ge)
            E_tt(tb1, th, lo, OP.subtract)
            E_tt(tb2, hi, th, OP.subtract)
            E_stt(lo, tb1, pred, lo, OP.mult, OP.add)
            E_stt(hi, tb2, pred, th, OP.mult, OP.add)
        E_ts(cmp_s[:], probs[:], lo, OP.is_ge)
        E_memset(ones_r[:], 1.0)
        E_scan(rankm[:], ones_r[:], cmp_s[:], 0.0)
        E_tt(rankm[:], rankm[:], cmp_s[:], OP.mult)
        E_ts(cmp_s[:], rankm[:], float(CAP) + 0.5, OP.is_le)
        E_tt(rankm[:], rankm[:], cmp_s[:], OP.mult)
        if tp_rk is not None:
            E_dma("gpsimd", tp_rk, rankm[:])
        E_ts(cmp_s[:], rankm[:], 0.5, OP.is_ge)
        E_tt(cmp_s[:], cmp_s[:], probs[:], OP.mult)
        E_red(dnm, cmp_s[:], OP.add)
        E_ts(dnm, dnm, 1.0, OP.mult, 1e-9, OP.add)
        E_recip(dnm, dnm)
        for lt in range(NLT):
            E_tr(psum[6][:, 0:16], rankm[:, ts(lt, 128)], ident_f[:])
            E_copy(rmT[:, lt, :], psum[6][:, 0:16])
            E_tr(psum[6][:, 0:16], probs[:, ts(lt, 128)], ident_f[:])
            E_copy(pT[:, lt, :], psum[6][:, 0:16])
        routing_es.close()
        prog.barrier()

        wgt = s("m_wgt", (128, 16, D), BF16)
        geT = s("m_geT", (128, 16, CAP), BF16)
        einT = s("m_einT", (128, 8, CAP), BF16)
        w12 = s("m_w12", (128, 16, DFF), BF16)
        w3f = s("m_w3f", (128, 16, D), BF16)
        gvb = s("m_gv", (128, 16, CAP), BF16)
        sg = s("m_sg", (128, CAP))
        psel = sg[0:16, :]
        wcol = s("m_wcol", (128, 2 * E))
        gsc = s("m_gsc", (128, 2, 128), BF16)
        for grp in range(2):
            for ei in range(8):
                e = grp * 8 + ei
                for lt in range(NLT):
                    E_ts(geT[:, lt, :], iotacr[:], rmT[:, lt, e:e + 1], OP.is_equal)
                for dt8 in range(8):
                    pm = psum[dt8 % 2][:, 0:CAP]
                    for lt in range(NLT):
                        E_mm(pm, xn_tok[:, lt, ts(dt8, 128)], geT[:, lt, :],
                             start=(lt == 0), stop=(lt == 15))
                    E_copy(einT[:, dt8, :], pm)
                if tp_ein is not None and e == 0:
                    E_dma("gpsimd", tp_ein, einT[:])
                pm = psum[2][0:16, 0:CAP]
                for lt in range(NLT):
                    E_mm(pm, pT[:, lt, :], geT[:, lt, :], start=(lt == 0), stop=(lt == 15))
                E_copy(psel[:], pm)
                E_ts(psel[:], psel[:], dnm, OP.mult)
                for ch in range(2):
                    E_tr(psum[6][:, 0:16], psel[:, ts(ch, 128)], ident_f[:])
                    E_copy(wcol[:, 2 * e + ch:2 * e + ch + 1], psum[6][:, e:e + 1])
                E_dma("gpsimd", w12[:, 0:8, :], moe_w1[e].rearrange("(a p) f -> p a f", p=128))
                E_dma("gpsimd", w12[:, 8:16, :], moe_w2[e].rearrange("(a p) f -> p a f", p=128))
                E_dma("gpsimd", w3f[:], moe_w3[e].rearrange("(a p) d -> p a d", p=128))
                for ft in range(16):
                    pg = psum[2 + ft % 2][:, 0:CAP]
                    pv = psum[4 + ft % 2][:, 0:CAP]
                    for dt8 in range(8):
                        E_mm(pg, w12[:, dt8, ts(ft, 128)], einT[:, dt8, :],
                             start=(dt8 == 0), stop=(dt8 == 7))
                    for dt8 in range(8):
                        E_mm(pv, w12[:, 8 + dt8, ts(ft, 128)], einT[:, dt8, :],
                             start=(dt8 == 0), stop=(dt8 == 7))
                    E_act(sg[:], pg, AF.Silu)
                    E_tt(gvb[:, ft, :], sg[:], pv, OP.mult)
                if tp_gv is not None and e == 0:
                    E_dma("gpsimd", tp_gv, gvb[:])
                for ct in range(2):
                    pm0 = psum[2 + ct][:, 0:512]
                    pm1 = psum[4 + ct][:, 0:512]
                    for ft in range(16):
                        E_mm(pm0, gvb[:, ft, ts(ct, 128)], w3f[:, ft, 0:512],
                             start=(ft == 0), stop=(ft == 15))
                        E_mm(pm1, gvb[:, ft, ts(ct, 128)], w3f[:, ft, 512:1024],
                             start=(ft == 0), stop=(ft == 15))
                    E_ts(wgt[:, 2 * ei + ct, 0:512], pm0, wcol[:, 2 * e + ct:2 * e + ct + 1], OP.mult)
                    E_ts(wgt[:, 2 * ei + ct, 512:1024], pm1, wcol[:, 2 * e + ct:2 * e + ct + 1], OP.mult)
            src = xres1 if grp == 0 else xres2
            for st in range(NLT):
                pm = [psum[0][:, 0:512], psum[1][:, 0:512]]
                for ei in range(8):
                    e = grp * 8 + ei
                    prb = psum[5][:, 0:128]
                    E_mm(prb, oh16_f[:, ts(e, 128)], rankm[:, ts(st, 128)], start=True, stop=True)
                    for ch in range(2):
                        E_ts(gsc[:, ch, :], prb, iotacc[:, ch:ch + 1], OP.is_equal)
                    for ch in range(2):
                        for dh in range(2):
                            E_mm(pm[dh], gsc[:, ch, :], wgt[:, 2 * ei + ch, ts(dh, 512)],
                                 start=(ei == 0 and ch == 0), stop=(ei == 7 and ch == 1))
                E_dma("sync", xf[:], src[ts(st, 128), :])
                for dh in range(2):
                    E_tt(xf[:, ts(dh, 512)], xf[:, ts(dh, 512)], pm[dh], OP.add)
                    E_dma("sync", xres2[ts(st, 128), ts(dh, 512)], xf[:, ts(dh, 512)])
        es.close()
        prog.barrier()

    def l1_ffn():
        es = ExitStack()

        def s(name, shape, dt=F32):
            return es.enter_context(nc.sbuf_tensor(name, list(shape), dt))

        wtile = s("f_wt", (128, 2, F1))
        accw = s("f_acc", (128, 2))
        ahalf = s("f_ah", (128, 2))
        for wi, w_ap, nrow, ncol in ((0, w_up, F1, D), (1, w_down, D, F1)):
            E_memset(accw[:, wi:wi + 1], 0.0)
            for ob in range(nrow // 128):
                E_dma("sync", wtile[:, ob % 2, 0:ncol], w_ap[ts(ob, 128), :])
                E_red(stat[:, ob % 2:ob % 2 + 1], wtile[:, ob % 2, 0:ncol], OP.add, absval=True)
                E_tt(accw[:, wi:wi + 1], accw[:, wi:wi + 1], stat[:, ob % 2:ob % 2 + 1], OP.add)
        for wi in range(2):
            E_tr(psum[6][0:1, 0:128], accw[:, wi:wi + 1], ident_f[:])
            E_copy(srow[0:1, 0:128], psum[6][0:1, 0:128])
            E_red(srow[0:1, 128:129], srow[0:1, 0:128], OP.add)
            E_ts(srow[0:1, 129:130], srow[0:1, 128:129], 0.5 / float(F1 * D), OP.mult)
            E_mm(psum[6][:, 0:1], ones1_f[:], srow[0:1, 129:130], start=True, stop=True)
            E_copy(ahalf[:, wi:wi + 1], psum[6][:, 0:1])

        sgn = s("f_sgn", (128, 2, 128))
        msk = s("f_msk", (128, 2, 128))
        natf = s("f_natf", (128, 4, 128))

        def ternarize_T(w_ap, wi, nrow, ncol, dstT):
            kk = [0]
            for ob in range(nrow // 128):
                E_dma("sync", wtile[:, ob % 2, 0:ncol], w_ap[ts(ob, 128), :])
                for ib in range(ncol // 128):
                    r2 = ib % 2
                    E_act(sgn[:, r2, :], wtile[:, ob % 2, ts(ib, 128)], AF.Sign)
                    E_act(msk[:, r2, :], wtile[:, ob % 2, ts(ib, 128)], AF.Abs)
                    E_ts(msk[:, r2, :], msk[:, r2, :], ahalf[:, wi:wi + 1], OP.is_gt)
                    E_tt(natf[:, ib % 4, :], sgn[:, r2, :], msk[:, r2, :], OP.mult)
                    pmt = psum[kk[0] % 4][:, 0:128]
                    kk[0] += 1
                    E_tr(pmt, natf[:, ib % 4, :], ident_f[:])
                    E_copy(dstT[:, ib, ts(ob, 128)], pmt)

        xn4T = s("f_xn4T", (128, 8, L), BF16)
        xf = s("f_xf", (128, D))
        sq = s("f_sq", (128, D))
        for lt in range(NLT):
            E_dma("sync", xf[:], xres3[ts(lt, 128), :])
            ln_stats(xf, sq)
            var = stat[:, 1:2]
            t1c, t2c, vt = stat[:, 10:11], stat[:, 11:12], stat[:, 12:13]
            E_ts(t1c, var, 1.0, OP.mult, LN_EPS, OP.add)
            E_recip(t2c, t1c)
            E_tt(vt, var, t2c, OP.mult)
            E_recip(t1c, vt)
            E_act(t1c, t1c, AF.Sqrt)
            E_recip(t2c, t1c)
            E_ts(t2c, t2c, 1.0, OP.mult, 1e-8, OP.add)
            E_recip(t1c, t2c)
            E_tt(stat[:, 2:3], stat[:, 2:3], t1c, OP.mult)
            E_ts(sq[:, 0:D], xf, stat[:, 0:1], OP.subtract, stat[:, 2:3], OP.mult)
            for ib in range(8):
                E_tr(psum[6][:, 0:128], sq[:, ts(ib, 128)], ident_f[:])
                E_copy(xn4T[:, ib, ts(lt, 128)], psum[6][:, 0:128])
        LS = 256
        with nc.sbuf_tensor("f_upT", [128, 8, F1], BF16) as w_upT, \
             nc.sbuf_tensor("f_hg", [128, 32, LS], F32) as hg, \
             nc.sbuf_tensor("f_h2b", [128, 2, LS], BF16) as h2b, \
             nc.sbuf_tensor("f_hsc", [128, 2, LS], F32) as hsc, \
             nc.sbuf_tensor("f_hnb", [128, LS], BF16) as hnb, \
             nc.sbuf_tensor("f_r2r", [1, LS], F32) as r2r, \
             nc.sbuf_tensor("f_srt", [1, 3 * LS], F32) as srt:
            ternarize_T(w_up, 0, F1, D, w_upT)
            for lsl in range(L // LS):
                ph2 = psum[4][0:1, 0:LS]
                for ft in range(32):
                    pm = psum[ft % 2][:, 0:LS]
                    for ib in range(8):
                        E_mm(pm, w_upT[:, ib, ts(ft, 128)], xn4T[:, ib, ts(lsl, LS)],
                             start=(ib == 0), stop=(ib == 7))
                    E_act(hg[:, ft, :], pm, AF.Gelu)
                    E_tt(hsc[:, ft % 2, :], hg[:, ft, :], hg[:, ft, :], OP.mult)
                    E_copy(h2b[:, ft % 2, :], hsc[:, ft % 2, :])
                    E_mm(ph2, ones_cb[:], h2b[:, ft % 2, :], start=(ft == 0), stop=(ft == 31))
                E_copy(r2r[:], ph2)
                E_ts(r2r[:], r2r[:], 1.0 / float(F1), OP.mult)
                E_act(srt[:, 0:LS], r2r[:], AF.Sqrt)
                E_recip(srt[:, LS:2 * LS], srt[:, 0:LS])
                E_tt(srt[:, 2 * LS:3 * LS], srt[:, LS:2 * LS], r2r[:], OP.mult)
                E_tt(srt[:, 0:LS], srt[:, 0:LS], srt[:, 2 * LS:3 * LS], OP.add)
                E_ts(srt[:, 0:LS], srt[:, 0:LS], 0.5, OP.mult, 1e-8, OP.add)
                E_recip(r2r[:], srt[:, 0:LS])
                pmb = psum[5][:, 0:LS]
                E_mm(pmb, ones1_f[:], r2r[:], start=True, stop=True)
                for ft in range(32):
                    E_tt(hnb[:], hg[:, ft, :], pmb, OP.mult)
                    E_dma("sync", hn_s[ft, :, ts(lsl, LS)], hnb[:])
        prog.barrier()
        with nc.sbuf_tensor("f_dnT", [128, 32, D], BF16) as w_dnT, \
             nc.sbuf_tensor("f_hn", [128, 2, 32, 128], BF16) as hnt, \
             nc.sbuf_tensor("f_o", [128, 512], F32) as of:
            ternarize_T(w_down, 1, D, F1, w_dnT)
            for lt in range(NLT):
                for ft in range(32):
                    E_dma("sync", hnt[:, lt % 2, ft, :], hn_s[ft, :, ts(lt, 128)])
                for dh in range(2):
                    pm = psum[2 + dh][:, 0:512]
                    for ft in range(32):
                        E_mm(pm, hnt[:, lt % 2, ft, :], w_dnT[:, ft, ts(dh, 512)],
                             start=(ft == 0), stop=(ft == 31))
                    E_dma("sync", xf[:, 0:512], xres3[ts(lt, 128), ts(dh, 512)])
                    E_tt(of[:], xf[:, 0:512], pm, OP.add)
                    E_dma("sync", y_out[ts(lt, 128), ts(dh, 512)], of[:])
        es.close()
        prog.barrier()

    mamba_block(0)
    l0_tail()
    moe_block()
    mamba_block(1)
    l1_ffn()

    for nm, src_t, shape, dt in (("x1", xres1, (L, D), F32), ("x2", xres2, (L, D), F32),
                                 ("x3", xres3, (L, D), F32),
                                 ("ycat", ycat_s, (32, 128, L), BF16)):
        tt = tap(nm, shape, dt)
        if tt is not None:
            E_dma("gpsimd", tt, src_t)

    resolved_ops, final_cnt = prog.finalize()

    sem_stack = ExitStack()
    sems = {e: sem_stack.enter_context(nc.semaphore("sem_" + e)) for e in ENGINES}
    with nc.Block() as block:
        def runner(eng_name):
            def body(eng):
                for fn, waits, inc in resolved_ops[eng_name]:
                    for ch, val in waits.items():
                        eng.wait_ge(sems[ch], val)
                    ins = fn(eng)
                    if inc:
                        ins.then_inc(sems[eng_name], inc)
                if eng_name == "sync":
                    for ch in ENGINES:
                        if final_cnt[ch]:
                            eng.wait_ge(sems[ch], final_cnt[ch])
            return body

        for en in ENGINES:
            getattr(block, en)(runner(en))
    sem_stack.close()

    stack.close()
    return nc


def _consts():
    oh = np.zeros((16, 16 * 128), dtype=np.float32)
    for n in range(16):
        oh[n, n * 128:(n + 1) * 128] = 1.0
    return {
        "identf": np.eye(128, dtype=np.float32),
        "identb": np.eye(128).astype(ml_dtypes.bfloat16),
        "oh16f": oh,
        "iota_cr": np.tile(np.arange(1, CAP + 1, dtype=np.float32), (128, 1)).copy(),
        "iota_cc": np.stack([np.arange(1, 129, dtype=np.float32),
                             np.arange(129, 257, dtype=np.float32)], axis=1).copy(),
        "ones1f": np.ones((1, 128), dtype=np.float32),
        "onescb": np.ones((128, 1)).astype(ml_dtypes.bfloat16),
    }


def _in_map(inputs, core):
    a = np.ascontiguousarray
    m = {
        "x": a(inputs["L0_attn_x"][core]),
        "W_in": a(inputs["L0_attn_W_in"]),
        "conv_w": a(inputs["L0_attn_conv_w"].reshape(16, 128, 4).transpose(1, 0, 2)),
        "conv_b": a(inputs["L0_attn_conv_b"].reshape(16, 128).T),
        "W_x": a(inputs["L0_attn_W_x"]),
        "W_dt": a(inputs["L0_attn_W_dt"].reshape(16, 128, R).transpose(1, 0, 2)),
        "b_dt": a(inputs["L0_attn_b_dt"].reshape(16, 128).T),
        "A_log": a(inputs["L0_attn_A_log"].reshape(16, 128, NS).transpose(1, 0, 2)),
        "Dp": a(inputs["L0_attn_Dp"].reshape(16, 128).T),
        "W_out1": a(inputs["L0_attn_W_out1"]),
        "W_out2": a(inputs["L0_attn_W_out2"]),
        "router_w1": a(inputs["L0_ffn_router_w1"].reshape(8, 128, RH).transpose(1, 0, 2)),
        "router_w2": a(inputs["L0_ffn_router_w2"]),
        "moe_w1": a(inputs["L0_ffn_w1"]),
        "moe_w2": a(inputs["L0_ffn_w2"]),
        "moe_w3": a(inputs["L0_ffn_w3"]),
        "l1_lnw": a(inputs["L1_attn_ln_w"][3]),
        "l1_lnb": a(inputs["L1_attn_ln_b"][3]),
        "l1_ip": a(inputs["L1_attn_in_proj_w"][3]),
        "l1_cw": a(inputs["L1_attn_conv_w"][3].reshape(16, 128, 4).transpose(1, 0, 2)),
        "l1_cb": a(inputs["L1_attn_conv_b"][3].reshape(16, 128).T),
        "l1_xp": a(inputs["L1_attn_x_proj_w"][3]),
        "l1_dtp": a(inputs["L1_attn_dt_proj_w"][3].reshape(16, 128, R).transpose(1, 0, 2)),
        "l1_dtb": a(inputs["L1_attn_dt_proj_b"][3].reshape(16, 128).T),
        "l1_Alog": a(inputs["L1_attn_A_log"][3].reshape(16, 128, NS).transpose(1, 0, 2)),
        "l1_D": a(inputs["L1_attn_D"][3].reshape(16, 128).T),
        "l1_op": a(inputs["L1_attn_out_proj_w"][3]),
        "w_up": a(inputs["L1_ffn_w_up"]),
        "w_down": a(inputs["L1_ffn_w_down"]),
    }
    m.update(_consts())
    return m


def _run(inputs, taps=()):
    nc = build_program(taps=taps)
    in_maps = [_in_map(inputs, core) for core in range(8)]
    res = run_bass_kernel_spmd(nc, in_maps, core_ids=list(range(8)))
    return res


def kernel(**inputs):
    res = _run(inputs)
    out = np.stack([np.asarray(r["y"], dtype=np.float32) for r in res.results], axis=0)
    return out


# version 20 (final)
# speedup vs baseline: 1.0372x; 1.0321x over previous
import numpy as np
import ml_dtypes
import concourse.bass as bass
import concourse.mybir as mybir
from concourse.alu_op_type import AluOpType
from concourse.bass_utils import run_bass_kernel_spmd
from contextlib import ExitStack

F32 = mybir.dt.float32
BF16 = mybir.dt.bfloat16
AF = mybir.ActivationFunctionType
OP = AluOpType

L, D, DI, NS, R = 2048, 1024, 2048, 16, 64
E, CAP, DFF, F1, RH = 16, 256, 2048, 4096, 128
T = 256
NCH = L // T
NLT = L // 128
LN_EPS = 1e-5
ENGINES = ("sync", "scalar", "vector", "tensor", "gpsimd")


def ts(i, n):
    return slice(i * n, (i + 1) * n)


_PSUM_NAMES = frozenset(["P0", "P1", "P2", "P3", "P4", "P5", "P6", "PB"])


def _ap_range(ap):
    if ap.__class__.__name__.endswith("TensorHandle"):
        ap = ap[:]
    name = ap.tensor.name
    if name in _PSUM_NAMES:
        return name, 0, 1 << 30
    off = ap.offset
    lo = off
    hi = off
    for step, count in ap.ap:
        if count <= 1:
            continue
        d = step * (count - 1)
        if d >= 0:
            hi += d
        else:
            lo += d
    return name, lo, hi + 1


class Prog:

    def __init__(self):
        self.ops = {e: [] for e in ENGINES}
        self.cnt = {e: 0 for e in ENGINES}
        self.clock = {e: {c: 0 for c in ENGINES} for e in ENGINES}
        self.writers = {}
        self.readers = {}
        self.pending = {}
        self.n_sync_edges = 0

    def _deps_from(self, table, name, lo, hi, deps):
        for (l2, h2, ch, val) in table.get(name, ()):
            if l2 < hi and lo < h2:
                if deps.get(ch, 0) < val:
                    deps[ch] = val

    def emit(self, eng, fn, reads=(), writes=(), dma=False, dma_fence=False):
        deps = dict(self.pending.pop(eng, ()))
        rr = [_ap_range(a) for a in reads if a is not None]
        ww = [_ap_range(a) for a in writes if a is not None]
        for (name, lo, hi) in rr:
            self._deps_from(self.writers, name, lo, hi, deps)
        for (name, lo, hi) in ww:
            self._deps_from(self.writers, name, lo, hi, deps)
            self._deps_from(self.readers, name, lo, hi, deps)
        if eng == "tensor" and not dma:
            deps.pop(eng, None)
        kl = self.clock[eng]
        waits = {}
        for ch, val in deps.items():
            if val > kl[ch]:
                waits[ch] = val
                kl[ch] = val
        inc = 16 if dma else 1
        self.cnt[eng] += inc
        self.n_sync_edges += len(waits)
        self.ops[eng].append((fn, waits, inc))
        for (name, lo, hi) in ww:
            self.writers.setdefault(name, []).append((lo, hi, eng, self.cnt[eng]))
            wl = self.writers[name]
            if len(wl) > 96:
                del wl[:32]
        for (name, lo, hi) in rr:
            self.readers.setdefault(name, []).append((lo, hi, eng, self.cnt[eng]))
            rl = self.readers[name]
            if len(rl) > 96:
                del rl[:32]

    def barrier(self):
        snap = dict(self.cnt)
        self.writers = {}
        self.readers = {}
        self.pending = {e: dict(snap) for e in ENGINES}

    def finalize(self):
        return dict(self.cnt)


def build_program(taps=()):
    nc = bass.Bass("TRN2", target_bir_lowering=False)
    prog = Prog()
    em = prog.emit

    def din(name, shape, dt=F32):
        return nc.dram_tensor(name, list(shape), dt, kind="ExternalInput").ap()

    x0 = din("x", (L, D))
    W_in = din("W_in", (2 * DI, D))
    conv_w = din("conv_w", (128, 16, 4))
    conv_b = din("conv_b", (128, 16))
    W_x = din("W_x", (R + 2 * NS, DI))
    W_dt = din("W_dt", (128, 16, R))
    b_dt = din("b_dt", (128, 16))
    A_log = din("A_log", (128, 16, NS))
    Dp = din("Dp", (128, 16))
    W_out1 = din("W_out1", (DI, 2 * DI))
    W_out2 = din("W_out2", (D, DI))
    r_w1 = din("router_w1", (128, 8, RH))
    r_w2 = din("router_w2", (RH, E))
    moe_w1 = din("moe_w1", (E, D, DFF))
    moe_w2 = din("moe_w2", (E, D, DFF))
    moe_w3 = din("moe_w3", (E, DFF, D))
    l1_lnw = din("l1_lnw", (D,))
    l1_lnb = din("l1_lnb", (D,))
    l1_ip = din("l1_ip", (2 * DI, D))
    l1_cw = din("l1_cw", (128, 16, 4))
    l1_cb = din("l1_cb", (128, 16))
    l1_xp = din("l1_xp", (R + 2 * NS, DI))
    l1_dtp = din("l1_dtp", (128, 16, R))
    l1_dtb = din("l1_dtb", (128, 16))
    l1_Alog = din("l1_Alog", (128, 16, NS))
    l1_D = din("l1_D", (128, 16))
    l1_op = din("l1_op", (D, DI))
    w_up = din("w_up", (F1, D))
    w_down = din("w_down", (D, F1))
    identf = din("identf", (128, 128))
    identb = din("identb", (128, 128), BF16)
    oh16f_in = din("oh16f", (16, 16 * 128))
    iota_cr = din("iota_cr", (128, CAP))
    iota_cc = din("iota_cc", (128, 2))
    ones1f = din("ones1f", (1, 128))
    onescb = din("onescb", (128, 1), BF16)

    y_out = nc.dram_tensor("y", [L, D], F32, kind="ExternalOutput").ap()

    def dscr(name, shape, dt=F32):
        return nc.dram_tensor(name, list(shape), dt, kind="Internal").ap()

    xres1 = dscr("xres1", (L, D))
    xres2 = dscr("xres2", (L, D))
    xres3 = dscr("xres3", (L, D))
    ycat_s = dscr("ycat_s", (32, 128, L), BF16)
    hn_s = dscr("hn_s", (32, 128, L), BF16)
    s_WinT = dscr("s_WinT", (D, 2 * DI), BF16)
    s_Wo1T = dscr("s_Wo1T", (2 * DI, DI), BF16)
    s_Wo2T = dscr("s_Wo2T", (DI, D), BF16)
    s_ipT = dscr("s_ipT", (D, 2 * DI), BF16)
    s_opT = dscr("s_opT", (DI, D), BF16)

    tap_t = {}

    def tap(name, shape, dt=F32):
        if name not in taps:
            return None
        if name not in tap_t:
            tap_t[name] = nc.dram_tensor("tap_" + name, list(shape), dt,
                                         kind="ExternalOutput").ap()
        return tap_t[name]

    stack = ExitStack()

    def sbp(name, shape, dt=F32):
        return stack.enter_context(nc.sbuf_tensor(name, list(shape), dt))

    ident_f = sbp("ident_f", (128, 128))
    ident_b = sbp("ident_b", (128, 128), BF16)
    ones1_f = sbp("ones1_f", (1, 128))
    ones_cb = sbp("ones_cb", (128, 1), BF16)
    oh16_f = sbp("oh16_f", (16, 16 * 128))
    carry = sbp("carry", (128, 16 * NS))
    stat = sbp("stat", (128, 20))
    srow = sbp("srow", (16, 192))

    psum = [stack.enter_context(nc.psum_tensor("P%d" % i, [128, 512], F32)) for i in range(7)]
    psum_b = stack.enter_context(nc.psum_tensor("PB", [128, 1024], BF16))

    def _ap(x):
        if x is not None and x.__class__.__name__.endswith("TensorHandle"):
            return x[:]
        return x

    def E_dma(eng, out, in_):
        out, in_ = _ap(out), _ap(in_)
        em(eng, (lambda e, out=out, in_=in_: e.dma_start(out=out, in_=in_)), dma=True,
           reads=[in_], writes=[out])

    def E_dmaT(out, in_):
        out, in_ = _ap(out), _ap(in_)
        em("sync", (lambda e, out=out, in_=in_: e.dma_start(out=out, in_=in_, transpose=True)),
           dma=True, reads=[in_], writes=[out])

    def E_act(out, in_, func, bias=None, scale=None, eng="scalar"):
        out, in_ = _ap(out), _ap(in_)
        kw = {}
        rds = [in_]
        if bias is not None:
            kw["bias"] = _ap(bias)
            if not isinstance(bias, float):
                rds.append(kw["bias"])
        if scale is not None:
            kw["scale"] = _ap(scale)
            if not isinstance(scale, float):
                rds.append(kw["scale"])
        em(eng, (lambda e, out=out, in_=in_, func=func, kw=kw:
                 e.activation(out=out, in_=in_, func=func, **kw)), reads=rds, writes=[out])

    def E_tt(out, in0, in1, op, eng="vector"):
        out, in0, in1 = _ap(out), _ap(in0), _ap(in1)
        em(eng, (lambda e, out=out, in0=in0, in1=in1, op=op:
                 e.tensor_tensor(out=out, in0=in0, in1=in1, op=op)),
           reads=[in0, in1], writes=[out])

    def E_ts(out, in0, s1, op0, s2=None, op1=None, accum_out=None, eng="vector"):
        out, in0, s1, s2 = _ap(out), _ap(in0), _ap(s1), _ap(s2)
        kw = {}
        rds = [in0]
        wrs = [out]
        for s in (s1, s2):
            if s is not None and not isinstance(s, float):
                rds.append(s)
        if op1 is not None:
            kw["op1"] = op1
        if accum_out is not None:
            kw["accum_out"] = _ap(accum_out)
            wrs.append(kw["accum_out"])
        em(eng, (lambda e, out=out, in0=in0, s1=s1, s2=s2, op0=op0, kw=kw:
                 e.tensor_scalar(out=out, in0=in0, scalar1=s1, scalar2=s2, op0=op0, **kw)),
           reads=rds, writes=wrs)

    def E_stt(out, in0, scalar, in1, op0, op1, eng="vector"):
        out, in0, scalar, in1 = _ap(out), _ap(in0), _ap(scalar), _ap(in1)
        rds = [in0, in1]
        if not isinstance(scalar, float):
            rds.append(scalar)
        em(eng, (lambda e, out=out, in0=in0, sc=scalar, in1=in1, op0=op0, op1=op1:
                 e.scalar_tensor_tensor(out=out, in0=in0, scalar=sc, in1=in1, op0=op0, op1=op1)),
           reads=rds, writes=[out])

    def E_red(out, in_, op, absval=False, eng="vector"):
        out, in_ = _ap(out), _ap(in_)
        kw = {"apply_absolute_value": True} if absval else {}
        em(eng, (lambda e, out=out, in_=in_, op=op, kw=kw:
                 e.tensor_reduce(out=out, in_=in_, axis=mybir.AxisListType.X, op=op, **kw)),
           reads=[in_], writes=[out])

    def E_scan(out, d0, d1, initial):
        out, d0, d1, initial = _ap(out), _ap(d0), _ap(d1), _ap(initial)
        rds = [d0, d1]
        if not isinstance(initial, float):
            rds.append(initial)
        em("vector", (lambda e, out=out, d0=d0, d1=d1, init=initial:
                      e.tensor_tensor_scan(out=out, data0=d0, data1=d1, initial=init,
                                           op0=OP.mult, op1=OP.add)),
           reads=rds, writes=[out])

    def E_mm(out, lhsT, rhs, start, stop):
        out, lhsT, rhs = _ap(out), _ap(lhsT), _ap(rhs)
        em("tensor", (lambda e, out=out, l=lhsT, r=rhs, st=start, sp=stop:
                      e.matmul(out=out, lhsT=l, rhs=r, start=st, stop=sp)),
           reads=[lhsT, rhs], writes=[out])

    def E_tr(out, in_, identity):
        out, in_, identity = _ap(out), _ap(in_), _ap(identity)
        k = in_.shape[0]
        if identity.shape[0] != k:
            identity = identity[0:k, 0:k]
        em("tensor", (lambda e, out=out, in_=in_, idn=identity:
                      e.transpose(out=out, in_=in_, identity=idn)),
           reads=[in_, identity], writes=[out])

    def E_copy(out, in_, eng="scalar"):
        out, in_ = _ap(out), _ap(in_)
        if eng == "scalar":
            E_act(out, in_, AF.Copy)
        else:
            em(eng, (lambda e, out=out, in_=in_: e.tensor_copy(out=out, in_=in_)),
               reads=[in_], writes=[out])

    def E_memset(ap, v, eng="vector"):
        ap = _ap(ap)
        em(eng, (lambda e, ap=ap, v=v: e.memset(ap, v)), writes=[ap])

    def E_recip(out, in_):
        out, in_ = _ap(out), _ap(in_)
        em("vector", (lambda e, out=out, in_=in_: e.reciprocal(out=out, in_=in_)),
           reads=[in_], writes=[out])

    def rsqrt2(out, v, t1):
        E_recip(t1, v)
        E_act(out, t1, AF.Sqrt)

    def E_bnstats(out, in_):
        out, in_ = _ap(out), _ap(in_)
        em("vector", (lambda e, out=out, in_=in_: e.bn_stats(out=out, in_=in_)),
           reads=[in_], writes=[out])

    def E_bnaggr(out, in_):
        out, in_ = _ap(out), _ap(in_)
        em("vector", (lambda e, out=out, in_=in_: e.bn_aggr(out=out, in_=in_)),
           reads=[in_], writes=[out])

    def ln_stats(x_t, sq, Dd=D, eps=LN_EPS):
        m, var, rs = stat[:, 0:1], stat[:, 1:2], stat[:, 2:3]
        t1 = stat[:, 3:4]
        nh = Dd // 512
        for h in range(nh):
            E_bnstats(stat[:, 4 + 6 * h:10 + 6 * h], x_t[:, ts(h, 512)])
        E_bnaggr(stat[:, 0:2], stat[:, 4:4 + 6 * nh])
        E_ts(t1, var, 1.0, OP.mult, eps, OP.add)
        rsqrt2(rs, t1, stat[:, 3:4]) if False else None
        E_recip(stat[:, 3:4], t1)
        E_act(rs, stat[:, 3:4], AF.Sqrt)

    def prep_T(w_ap, o_dim, i_dim, dst, st_nat, st_row):
        OSUP = 1024
        k = [0]
        for ib in range(i_dim // 128):
            rn = ib % 2
            for osb in range(o_dim // OSUP):
                src = w_ap[ts(osb, OSUP), ts(ib, 128)].rearrange("(a p) i -> p a i", p=128)
                E_dma("gpsimd", st_nat[:, rn, :, :], src)
                for a in range(OSUP // 128):
                    o0 = osb * OSUP + a * 128
                    pmt = psum[k[0] % 4][:, 0:128]
                    k[0] += 1
                    E_tr(pmt, st_nat[:, rn, a, :], ident_f[:])
                    E_copy(st_row[:, rn, o0:o0 + 128], pmt)
            E_dma("scalar", dst[ts(ib, 128), :], st_row[:, rn, 0:o_dim])

    E_dma("sync", ident_f[:], identf)
    with nc.sbuf_tensor("pr_nat", [128, 2, 8, 128], F32) as st_nat, \
         nc.sbuf_tensor("pr_row", [128, 2, 4096], BF16) as st_row:
        prep_T(W_in, 2 * DI, D, s_WinT, st_nat, st_row)
        prep_T(W_out1, DI, 2 * DI, s_Wo1T, st_nat, st_row)
        prep_T(W_out2, D, DI, s_Wo2T, st_nat, st_row)
        prep_T(l1_ip, 2 * DI, D, s_ipT, st_nat, st_row)
        prep_T(l1_op, D, DI, s_opT, st_nat, st_row)
    prog.barrier()

    E_dma("sync", ident_b[:], identb)
    E_dma("sync", ones1_f[:], ones1f)
    E_dma("sync", ones_cb[:], onescb)
    E_dma("sync", oh16_f[:], oh16f_in)

    def mamba_block(which):
        l0 = which == 0
        sfx = "_m%d" % which
        x_in = x0 if l0 else xres2
        winT_scr = s_WinT if l0 else s_ipT
        w_x_in = W_x if l0 else l1_xp
        w_dt_in = W_dt if l0 else l1_dtp
        cw_in = conv_w if l0 else l1_cw
        cb_in = conv_b if l0 else l1_cb
        bdt_in = b_dt if l0 else l1_dtb
        alog_in = A_log if l0 else l1_Alog
        dp_in = Dp if l0 else l1_D

        es = ExitStack()

        def s(name, shape, dt=F32):
            return es.enter_context(nc.sbuf_tensor(name + sfx, list(shape), dt))

        cwc = s("cwc", (128, 16, 4))
        cbc = s("cbc", (128, 16))
        bdtc = s("bdtc", (128, 16))
        dpc = s("dpc", (128, 16))
        ac = s("ac", (128, 16 * NS))
        wxT = s("wxT", (128, 16, 96), BF16)
        wdtT = s("wdtT", (64, DI), BF16)
        E_dma("sync", cwc[:], cw_in)
        E_dma("sync", cbc[:], cb_in)
        E_dma("sync", bdtc[:], bdt_in)
        E_dma("sync", dpc[:], dp_in)
        E_dma("sync", ac[:].rearrange("p (a n) -> p a n", n=NS), alog_in)
        E_act(ac[:], ac[:], AF.Exp)
        E_ts(ac[:], ac[:], -1.0, OP.mult)
        if l0:
            dpb = s("dpb", (128, 16), BF16)
            atb = s("atb", (16, DI), BF16)
            E_copy(dpb[:], dpc[:])
            for db in range(16):
                E_tr(psum[6][0:NS, 0:128], ac[:, ts(db, NS)], ident_f[:])
                E_copy(atb[:, ts(db, 128)], psum[6][0:NS, 0:128])
        else:
            lnw_bc = s("lnwbc", (128, D), BF16)
            lnb_bc = s("lnbbc", (128, D), BF16)
            with nc.sbuf_tensor("lnr" + sfx, [1, 2 * D], F32) as lnr:
                E_dma("sync", lnr[:, 0:D], l1_lnw.rearrange("(a d) -> a d", a=1))
                E_dma("sync", lnr[:, D:2 * D], l1_lnb.rearrange("(a d) -> a d", a=1))
                for i, dstt in ((0, lnw_bc), (1, lnb_bc)):
                    for j in range(2):
                        E_mm(psum[6][:, 0:512], ones1_f[:],
                             lnr[:, i * D + j * 512: i * D + (j + 1) * 512], start=True, stop=True)
                        E_copy(dstt[:, ts(j, 512)], psum[6][:, 0:512])
            prog.barrier()
        with nc.sbuf_tensor("smw" + sfx, [96, DI], BF16) as smw:
            E_dma("gpsimd", smw[:], w_x_in)
            for ib in range(16):
                E_tr(psum_b[:, 0:96], smw[:, ts(ib, 128)], ident_b[:])
                E_copy(wxT[:, ib, :], psum_b[:, 0:96])
        prog.barrier()
        with nc.sbuf_tensor("smd" + sfx, [128, 16, R], BF16) as smd:
            E_dma("gpsimd", smd[:], w_dt_in)
            for ib in range(16):
                E_tr(psum_b[0:R, 0:128], smd[:, ib, :], ident_b[:])
                E_copy(wdtT[:, ts(ib, 128)], psum_b[0:R, 0:128])
        prog.barrier()

        xf = s("xf", (128, D))
        sq = s("sq", (128, D))
        xT = s("xT", (128, 8, T), BF16)
        winP = s("winP", (128, 8, 512), BF16)
        xih = s("xih", (128, 16, T + 3))
        halo = s("halo", (128, 16, 3))
        u_f = s("u", (128, 16, T))
        ub = s("ub", (128, 16, T), BF16) if l0 else s("ub", (128, 2, T), BF16)
        sz = s("sz", (128, 2, 16, T), BF16)
        dtrb = s("dtrb", (64, T), BF16)
        btf = s("btf", (16, T))
        ctf = s("ctf", (16, T))
        btil = s("btil", (128, NS, T))
        ctil = s("ctil", (128, NS, T), BF16)
        dl_all = s("dl_all", (128, 16, T))
        sp_s = s("sp_s", (128, 2, T))
        fix16 = s("fix16", (128, 16))
        dA_all = s("dA_all", (128, NS, T))
        bsc_all = s("bsc_all", (128, NS, T))
        du = s("du", (128, T))
        s_all = s("s_all", (128, NS, T))
        sC = s("sC", (128, NS, T), BF16)
        t1 = s("t1", (128, T))
        ycb = s("ycb", (128, 2, T), BF16)
        if l0:
            ctb = s("ctb", (16, T), BF16)
            sdu = btf[0:1, :]
        else:
            ytb = s("ytb", (128, 16, T), BF16)
            opt_t = s("optt", (128, 1, D), BF16)

        tp_xn = tap("xn%d" % which, (L, D))
        tp_u = tap("u%d" % which, (128, 16, T))
        tp_dl = tap("dl%d" % which, (128, 16, T))
        tp_bt = tap("bt%d" % which, (16, T))
        tp_sz = tap("sz%d" % which, (128, 16, T))
        tp_s0 = tap("s%d" % which, (128, NS, T))

        E_memset(carry[:], 0.0)

        def ln_xz(c):
            t0 = c * T
            for lt in range(2):
                row0 = t0 + lt * 128
                E_dma("sync", xf[:], x_in[row0:row0 + 128, :])
                ln_stats(xf, sq)
                if not l0:
                    var = stat[:, 1:2]
                    t1c, t2c, vt = stat[:, 10:11], stat[:, 11:12], stat[:, 12:13]
                    E_ts(t1c, var, 1.0, OP.mult, LN_EPS, OP.add)
                    E_recip(t2c, t1c)
                    E_tt(vt, var, t2c, OP.mult)
                    E_ts(vt, vt, 25.0, OP.mult, LN_EPS, OP.add)
                    rsqrt2(t2c, vt, stat[:, 13:14])
                    E_ts(t2c, t2c, 5.0, OP.mult)
                    E_tt(stat[:, 2:3], stat[:, 2:3], t2c, OP.mult)
                E_ts(sq[:, 0:D], xf, stat[:, 0:1], OP.subtract, stat[:, 2:3], OP.mult)
                if not l0:
                    E_tt(sq[:, 0:D], sq[:, 0:D], lnw_bc[:], OP.mult)
                    E_tt(sq[:, 0:D], sq[:, 0:D], lnb_bc[:], OP.add)
                if tp_xn is not None:
                    E_dma("gpsimd", tp_xn[row0:row0 + 128, :], sq[:, 0:D])
                for ib in range(8):
                    pmt = psum[4 + ib % 2][:, 0:128]
                    E_tr(pmt, sq[:, ts(ib, 128)], ident_f[:])
                    E_copy(xT[:, ib, lt * 128:(lt + 1) * 128], pmt)
            for q in range(8):
                for ib in range(8):
                    E_dma("scalar", winP[:, ib, :], winT_scr[ts(ib, 128), q * 512:(q + 1) * 512])
                for eq in range(4):
                    et = q * 4 + eq
                    pm = psum[eq % 2][:, 0:T]
                    for ib in range(8):
                        E_mm(pm, winP[:, ib, ts(eq, 128)], xT[:, ib, :],
                             start=(ib == 0), stop=(ib == 7))
                    if et < 16:
                        if c == 0:
                            E_memset(xih[:, et, 0:3], 0.0)
                        else:
                            E_copy(xih[:, et, 0:3], halo[:, et, :])
                        E_copy(xih[:, et, 3:T + 3], pm)
                    else:
                        E_act(sz[:, c % 2, et - 16, :], pm, AF.Silu)

        ln_xz(0)
        for c in range(NCH):
            t0 = c * T
            for k in range(4):
                for et in range(16):
                    tconv = dA_all[:, et, :]
                    if k == 0:
                        E_ts(tconv, xih[:, et, 0:T], cwc[:, et, 0:1], OP.mult)
                    else:
                        E_stt(tconv, xih[:, et, k:k + T], cwc[:, et, k:k + 1], tconv,
                              OP.mult, OP.add)
            for et in range(16):
                E_copy(halo[:, et, :], xih[:, et, T:T + 3])
                E_act(u_f[:, et, :], dA_all[:, et, :], AF.Silu, bias=cbc[:, et:et + 1])
                if l0:
                    E_copy(ub[:, et, :], u_f[:, et, :])
            if c == 0 and tp_u is not None:
                E_dma("gpsimd", tp_u, u_f[:])
            if c == 0 and tp_sz is not None:
                E_dma("gpsimd", tp_sz, sz[:, 0])
            for ib in range(16):
                if l0:
                    ubt = ub[:, ib, :]
                else:
                    ubt = ub[:, ib % 2, :]
                    E_copy(ubt, u_f[:, ib, :])
                E_mm(psum[0][0:64, 0:T], wxT[:, ib, 0:64], ubt, start=(ib == 0), stop=(ib == 15))
                E_mm(psum[1][0:16, 0:T], wxT[:, ib, 64:80], ubt, start=(ib == 0), stop=(ib == 15))
                E_mm(psum[2][0:16, 0:T], wxT[:, ib, 80:96], ubt, start=(ib == 0), stop=(ib == 15))
            E_copy(dtrb[:], psum[0][0:64, 0:T])
            E_copy(btf[:], psum[1][0:16, 0:T])
            E_copy(ctf[:], psum[2][0:16, 0:T])
            if l0:
                E_copy(ctb[:], psum[2][0:16, 0:T])
            if c == 0 and tp_bt is not None:
                E_dma("gpsimd", tp_bt, btf[:])
            for n in range(NS):
                pm = psum[0 + n % 2][:, 0:T]
                E_mm(pm, oh16_f[:, ts(n, 128)], btf[:], start=True, stop=True)
                E_copy(btil[:, n, :], pm)
                pm2 = psum[2 + n % 2][:, 0:T]
                E_mm(pm2, oh16_f[:, ts(n, 128)], ctf[:], start=True, stop=True)
                E_copy(ctil[:, n, :], pm2, eng="vector")
            for db in range(16):
                pm = psum[4 + db % 2][:, 0:T]
                sp = sp_s[:, db % 2, :]
                dl = dl_all[:, db, :]
                E_mm(pm, wdtT[:, ts(db, 128)], dtrb[:], start=True, stop=True)
                E_act(sp, pm, AF.Abs, bias=bdtc[:, db:db + 1])
                E_act(sp, sp, AF.Exp, scale=-1.0)
                E_act(sp, sp, AF.Ln, bias=1.0)
                E_act(dl, pm, AF.Relu, bias=bdtc[:, db:db + 1])
                E_tt(dl, dl, sp, OP.add, eng="gpsimd")
            if c == 0 and tp_dl is not None:
                E_dma("gpsimd", tp_dl, dl_all[:])
            if c + 1 < NCH:
                ln_xz(c + 1)
            def tail(db):
                r = db % 2
                py = psum[2 + r][:, 0:T]
                E_stt(t1[:], u_f[:, db, :], dpc[:, db:db + 1], py, OP.mult, OP.add)
                E_tt(ycb[:, r, :], t1[:], sz[:, c % 2, db, :], OP.mult)
                if l0:
                    E_dma("sync", ycat_s[db, :, t0:t0 + T], ycb[:, r, :])
                else:
                    E_copy(ytb[:, db, :], ycb[:, r, :], eng="vector")

            for db in range(16):
                r = db % 2
                dl = dl_all[:, db, :]
                dub = du[:]
                E_tt(dub, dl, u_f[:, db, :], OP.mult, eng="gpsimd")
                dub_b = dub.rearrange("p (a t) -> p a t", a=1).broadcast_to([128, 4, T])
                for g in range(4):
                    em("gpsimd", (lambda e, out=bsc_all[:, 4 * g:4 * g + 4, :], in0=dub_b,
                                  in1=btil[:, 4 * g:4 * g + 4, :]:
                                  e.tensor_tensor(out=out, in0=in0, in1=in1, op=OP.mult)),
                       reads=[dub, btil[:, 4 * g:4 * g + 4, :]],
                       writes=[bsc_all[:, 4 * g:4 * g + 4, :]])
                py = psum[2 + r][:, 0:T]
                for n in range(NS):
                    E_act(dA_all[:, n, :], dl, AF.Exp, scale=ac[:, db * NS + n:db * NS + n + 1])
                E_tt(fix16[:], dA_all[:, :, 0], carry[:, ts(db, NS)], OP.mult)
                E_tt(bsc_all[:, :, 0], bsc_all[:, :, 0], fix16[:], OP.add)
                E_memset(dA_all[:, :, 0], 0.0)
                E_scan(s_all[:].rearrange("p n t -> p (n t)"),
                       dA_all[:].rearrange("p n t -> p (n t)"),
                       bsc_all[:].rearrange("p n t -> p (n t)"), 0.0)
                E_copy(carry[:, ts(db, NS)], s_all[:, :, T - 1], eng="vector")
                E_tt(sC[:], s_all[:], ctil[:], OP.mult)
                for n in range(NS):
                    E_mm(py, ident_b[:], sC[:, n, :], start=(n == 0), stop=(n == NS - 1))
                if c == 0 and db == 0 and tp_s0 is not None:
                    E_dma("gpsimd", tp_s0, s_all[:])
                if db > 0:
                    tail(db - 1)
            tail(15)
            if l0:
                psd = psum[6][0:1, 0:T]
                for db in range(16):
                    E_mm(psd, dpb[:, db:db + 1], ub[:, db, :], start=(db == 0), stop=(db == 15))
                E_copy(sdu[:], psd)
                for db2 in range(16):
                    pm = psum[2 + db2 % 2][:, 0:T]
                    E_mm(pm, atb[:, ts(db2, 128)], ctb[:], start=True, stop=False)
                    E_mm(pm, ones1_f[:], sdu[:], start=False, stop=True)
                    E_copy(ycb[:, db2 % 2, :], pm)
                    E_dma("sync", ycat_s[16 + db2, :, t0:t0 + T], ycb[:, db2 % 2, :])
            else:
                pm00 = psum[2][:, 0:512]
                pm01 = psum[3][:, 0:512]
                pm10 = psum[6][:, 0:512]
                pm11 = psum[5][:, 0:512]
                for db in range(16):
                    E_dma("sync", opt_t[:, 0, :], s_opT[ts(db, 128), :])
                    st = (db == 0)
                    sp2 = (db == 15)
                    E_mm(pm00, ytb[:, db, 0:128], opt_t[:, 0, 0:512], start=st, stop=sp2)
                    E_mm(pm01, ytb[:, db, 0:128], opt_t[:, 0, 512:1024], start=st, stop=sp2)
                    E_mm(pm10, ytb[:, db, 128:256], opt_t[:, 0, 0:512], start=st, stop=sp2)
                    E_mm(pm11, ytb[:, db, 128:256], opt_t[:, 0, 512:1024], start=st, stop=sp2)
                for lsub in range(2):
                    row0 = t0 + lsub * 128
                    for dh in range(2):
                        pm = (pm00, pm01, pm10, pm11)[lsub * 2 + dh]
                        rb = xf[:, ts(dh, 512)]
                        E_dma("sync", rb, xres2[row0:row0 + 128, ts(dh, 512)])
                        E_tt(rb, rb, pm, OP.add)
                        E_dma("sync", xres3[row0:row0 + 128, ts(dh, 512)], rb)
        es.close()
        prog.barrier()

    def l0_tail():
        es = ExitStack()

        def s(name, shape, dt=F32):
            return es.enter_context(nc.sbuf_tensor(name, list(shape), dt))

        h1T = s("t_h1T", (128, 16, L), BF16)
        wo2 = s("t_wo2", (128, 16, D), BF16)
        wo1 = s("t_wo1", (128, 32, 1024), BF16)
        yc = s("t_yc", (128, 2, 32, 256), BF16)
        xs = s("t_xs", (128, 512))
        o2 = s("t_o2", (128, 512))
        for ib in range(16):
            E_dma("sync", wo2[:, ib, :], s_Wo2T[ts(ib, 128), :])
        for fh in range(2):
            for cb in range(32):
                E_dma("sync", wo1[:, cb, :], s_Wo1T[ts(cb, 128), fh * 1024:(fh + 1) * 1024])
            for lsl in range(8):
                for cb in range(32):
                    E_dma("sync", yc[:, lsl % 2, cb, :], ycat_s[cb, :, lsl * 256:(lsl + 1) * 256])
                for ft in range(8):
                    pm = psum[ft % 2][:, 0:256]
                    for cb in range(32):
                        E_mm(pm, wo1[:, cb, ts(ft, 128)], yc[:, lsl % 2, cb, :],
                             start=(cb == 0), stop=(cb == 31))
                    E_act(h1T[:, fh * 8 + ft, lsl * 256:(lsl + 1) * 256], pm, AF.Relu)
        for lt in range(NLT):
            for dh in range(2):
                pm = psum[2 + dh][:, 0:512]
                for ft in range(16):
                    E_mm(pm, h1T[:, ft, lt * 128:(lt + 1) * 128], wo2[:, ft, ts(dh, 512)],
                         start=(ft == 0), stop=(ft == 15))
                E_dma("sync", xs[:], x0[ts(lt, 128), ts(dh, 512)])
                E_tt(o2[:], xs[:], pm, OP.add)
                E_dma("sync", xres1[ts(lt, 128), ts(dh, 512)], o2[:])
        es.close()
        prog.barrier()

    def moe_block():
        es = ExitStack()

        def s(name, shape, dt=F32):
            return es.enter_context(nc.sbuf_tensor(name, list(shape), dt))

        iotacr = s("m_iotacr", (128, CAP))
        iotacc = s("m_iotacc", (128, 2))
        E_dma("sync", iotacr[:], iota_cr)
        E_dma("sync", iotacc[:], iota_cc)
        xn_tok = s("m_xn", (128, 16, D), BF16)
        xf = s("m_xf", (128, D))
        rankm = s("m_rankm", (16, L))
        rmT = s("m_rmT", (128, 16, 16))
        pT = s("m_pT", (128, 16, 16), BF16)
        routing_es = ExitStack()
        probs = routing_es.enter_context(nc.sbuf_tensor("m_probs", [16, L], F32))
        cmp_s = routing_es.enter_context(nc.sbuf_tensor("m_cmp", [16, L], F32))
        ones_r = routing_es.enter_context(nc.sbuf_tensor("m_ones", [16, L], F32))
        lo, hi, th = srow[:, 0:1], srow[:, 1:2], srow[:, 2:3]
        cntc, pred, dnm = srow[:, 3:4], srow[:, 4:5], srow[:, 5:6]
        mx, sm = srow[:, 6:7], srow[:, 7:8]
        tp_pr = tap("probs", (16, L))
        tp_rk = tap("rankm", (16, L))
        tp_ein = tap("ein", (128, 8, CAP), BF16)
        tp_gv = tap("gv", (128, 16, CAP), BF16)

        with nc.sbuf_tensor("m_xnT", [128, 8, L], BF16) as xnT, \
             nc.sbuf_tensor("m_hrT", [128, L], BF16) as hrT, \
             nc.sbuf_tensor("m_sq", [128, D], F32) as sq, \
             nc.sbuf_tensor("m_rw1", [128, 8, RH], BF16) as rw1b, \
             nc.sbuf_tensor("m_rw2", [128, E], BF16) as rw2b:
            E_dma("gpsimd", rw1b[:], r_w1)
            E_dma("gpsimd", rw2b[:], r_w2)
            for lt in range(NLT):
                E_dma("sync", xf[:], xres1[ts(lt, 128), :])
                ln_stats(xf, sq)
                E_ts(xn_tok[:, lt, :], xf, stat[:, 0:1], OP.subtract, stat[:, 2:3], OP.mult)
                E_copy(sq[:, 0:D], xn_tok[:, lt, :])
                for ib in range(8):
                    E_tr(psum[6][:, 0:128], sq[:, ts(ib, 128)], ident_f[:])
                    E_copy(xnT[:, ib, ts(lt, 128)], psum[6][:, 0:128])
            for j in range(4):
                pm = psum[j % 2][:, 0:512]
                for ib in range(8):
                    E_mm(pm, rw1b[:, ib, :], xnT[:, ib, ts(j, 512)], start=(ib == 0), stop=(ib == 7))
                E_act(hrT[:, ts(j, 512)], pm, AF.Silu)
            for j in range(4):
                pm = psum[2 + j % 2][0:16, 0:512]
                E_mm(pm, rw2b[:], hrT[:, ts(j, 512)], start=True, stop=True)
                E_copy(probs[:, ts(j, 512)], pm)
        prog.barrier()
        E_red(mx, probs[:], OP.max)
        E_ts(mx, mx, -1.0, OP.mult)
        E_act(probs[:], probs[:], AF.Exp, bias=mx)
        E_red(sm, probs[:], OP.add)
        E_recip(sm, sm)
        E_ts(probs[:], probs[:], sm, OP.mult)
        if tp_pr is not None:
            E_dma("gpsimd", tp_pr, probs[:])
        E_memset(lo, 0.0)
        E_memset(hi, 1.1)
        tb1, tb2 = srow[:, 8:9], srow[:, 9:10]
        for _ in range(44):
            E_tt(th, lo, hi, OP.add)
            E_ts(th, th, 0.5, OP.mult)
            E_ts(cmp_s[:], probs[:], th, OP.is_ge, op1=OP.add, accum_out=cntc)
            E_ts(pred, cntc, float(CAP) - 0.5, OP.is_ge)
            E_tt(tb1, th, lo, OP.subtract)
            E_tt(tb2, hi, th, OP.subtract)
            E_stt(lo, tb1, pred, lo, OP.mult, OP.add)
            E_stt(hi, tb2, pred, th, OP.mult, OP.add)
        E_ts(cmp_s[:], probs[:], lo, OP.is_ge)
        E_memset(ones_r[:], 1.0)
        E_scan(rankm[:], ones_r[:], cmp_s[:], 0.0)
        E_tt(rankm[:], rankm[:], cmp_s[:], OP.mult)
        E_ts(cmp_s[:], rankm[:], float(CAP) + 0.5, OP.is_le)
        E_tt(rankm[:], rankm[:], cmp_s[:], OP.mult)
        if tp_rk is not None:
            E_dma("gpsimd", tp_rk, rankm[:])
        E_ts(cmp_s[:], rankm[:], 0.5, OP.is_ge)
        E_tt(cmp_s[:], cmp_s[:], probs[:], OP.mult)
        E_red(dnm, cmp_s[:], OP.add)
        E_ts(dnm, dnm, 1.0, OP.mult, 1e-9, OP.add)
        E_recip(dnm, dnm)
        for lt in range(NLT):
            E_tr(psum[6][:, 0:16], rankm[:, ts(lt, 128)], ident_f[:])
            E_copy(rmT[:, lt, :], psum[6][:, 0:16])
            E_tr(psum[6][:, 0:16], probs[:, ts(lt, 128)], ident_f[:])
            E_copy(pT[:, lt, :], psum[6][:, 0:16])
        routing_es.close()
        prog.barrier()

        wgt = s("m_wgt", (128, 16, D), BF16)
        geT = s("m_geT", (128, 16, CAP), BF16)
        einT = s("m_einT", (128, 8, CAP), BF16)
        w12 = s("m_w12", (128, 16, DFF), BF16)
        w3f = s("m_w3f", (128, 16, D), BF16)
        gvb = s("m_gv", (128, 16, CAP), BF16)
        sg = s("m_sg", (128, CAP))
        psel = sg[0:16, :]
        wcol = s("m_wcol", (128, 2 * E))
        gsc = s("m_gsc", (128, 2, 128), BF16)
        for grp in range(2):
            for ei in range(8):
                e = grp * 8 + ei
                for lt in range(NLT):
                    E_ts(geT[:, lt, :], iotacr[:], rmT[:, lt, e:e + 1], OP.is_equal)
                for dt8 in range(8):
                    pm = psum[dt8 % 2][:, 0:CAP]
                    for lt in range(NLT):
                        E_mm(pm, xn_tok[:, lt, ts(dt8, 128)], geT[:, lt, :],
                             start=(lt == 0), stop=(lt == 15))
                    E_copy(einT[:, dt8, :], pm)
                if tp_ein is not None and e == 0:
                    E_dma("gpsimd", tp_ein, einT[:])
                pm = psum[2][0:16, 0:CAP]
                for lt in range(NLT):
                    E_mm(pm, pT[:, lt, :], geT[:, lt, :], start=(lt == 0), stop=(lt == 15))
                E_copy(psel[:], pm)
                E_ts(psel[:], psel[:], dnm, OP.mult)
                for ch in range(2):
                    E_tr(psum[6][:, 0:16], psel[:, ts(ch, 128)], ident_f[:])
                    E_copy(wcol[:, 2 * e + ch:2 * e + ch + 1], psum[6][:, e:e + 1])
                E_dma("gpsimd", w12[:, 0:8, :], moe_w1[e].rearrange("(a p) f -> p a f", p=128))
                E_dma("gpsimd", w12[:, 8:16, :], moe_w2[e].rearrange("(a p) f -> p a f", p=128))
                E_dma("gpsimd", w3f[:], moe_w3[e].rearrange("(a p) d -> p a d", p=128))
                for ft in range(16):
                    pg = psum[2 + ft % 2][:, 0:CAP]
                    pv = psum[4 + ft % 2][:, 0:CAP]
                    for dt8 in range(8):
                        E_mm(pg, w12[:, dt8, ts(ft, 128)], einT[:, dt8, :],
                             start=(dt8 == 0), stop=(dt8 == 7))
                    for dt8 in range(8):
                        E_mm(pv, w12[:, 8 + dt8, ts(ft, 128)], einT[:, dt8, :],
                             start=(dt8 == 0), stop=(dt8 == 7))
                    E_act(sg[:], pg, AF.Silu)
                    E_tt(gvb[:, ft, :], sg[:], pv, OP.mult)
                if tp_gv is not None and e == 0:
                    E_dma("gpsimd", tp_gv, gvb[:])
                for ct in range(2):
                    pm0 = psum[2 + ct][:, 0:512]
                    pm1 = psum[4 + ct][:, 0:512]
                    for ft in range(16):
                        E_mm(pm0, gvb[:, ft, ts(ct, 128)], w3f[:, ft, 0:512],
                             start=(ft == 0), stop=(ft == 15))
                        E_mm(pm1, gvb[:, ft, ts(ct, 128)], w3f[:, ft, 512:1024],
                             start=(ft == 0), stop=(ft == 15))
                    E_ts(wgt[:, 2 * ei + ct, 0:512], pm0, wcol[:, 2 * e + ct:2 * e + ct + 1], OP.mult)
                    E_ts(wgt[:, 2 * ei + ct, 512:1024], pm1, wcol[:, 2 * e + ct:2 * e + ct + 1], OP.mult)
            src = xres1 if grp == 0 else xres2
            for st in range(NLT):
                pm = [psum[0][:, 0:512], psum[1][:, 0:512]]
                for ei in range(8):
                    e = grp * 8 + ei
                    prb = psum[5][:, 0:128]
                    E_mm(prb, oh16_f[:, ts(e, 128)], rankm[:, ts(st, 128)], start=True, stop=True)
                    for ch in range(2):
                        E_ts(gsc[:, ch, :], prb, iotacc[:, ch:ch + 1], OP.is_equal)
                    for ch in range(2):
                        for dh in range(2):
                            E_mm(pm[dh], gsc[:, ch, :], wgt[:, 2 * ei + ch, ts(dh, 512)],
                                 start=(ei == 0 and ch == 0), stop=(ei == 7 and ch == 1))
                E_dma("sync", xf[:], src[ts(st, 128), :])
                for dh in range(2):
                    E_tt(xf[:, ts(dh, 512)], xf[:, ts(dh, 512)], pm[dh], OP.add)
                    E_dma("sync", xres2[ts(st, 128), ts(dh, 512)], xf[:, ts(dh, 512)])
        es.close()
        prog.barrier()

    def l1_ffn():
        es = ExitStack()

        def s(name, shape, dt=F32):
            return es.enter_context(nc.sbuf_tensor(name, list(shape), dt))

        wtile = s("f_wt", (128, 2, F1))
        accw = s("f_acc", (128, 2))
        ahalf = s("f_ah", (128, 2))
        for wi, w_ap, nrow, ncol in ((0, w_up, F1, D), (1, w_down, D, F1)):
            E_memset(accw[:, wi:wi + 1], 0.0)
            for ob in range(nrow // 128):
                E_dma("sync", wtile[:, ob % 2, 0:ncol], w_ap[ts(ob, 128), :])
                E_red(stat[:, ob % 2:ob % 2 + 1], wtile[:, ob % 2, 0:ncol], OP.add, absval=True)
                E_tt(accw[:, wi:wi + 1], accw[:, wi:wi + 1], stat[:, ob % 2:ob % 2 + 1], OP.add)
        for wi in range(2):
            E_tr(psum[6][0:1, 0:128], accw[:, wi:wi + 1], ident_f[:])
            E_copy(srow[0:1, 0:128], psum[6][0:1, 0:128])
            E_red(srow[0:1, 128:129], srow[0:1, 0:128], OP.add)
            E_ts(srow[0:1, 129:130], srow[0:1, 128:129], 0.5 / float(F1 * D), OP.mult)
            E_mm(psum[6][:, 0:1], ones1_f[:], srow[0:1, 129:130], start=True, stop=True)
            E_copy(ahalf[:, wi:wi + 1], psum[6][:, 0:1])

        sgn = s("f_sgn", (128, 2, 128))
        msk = s("f_msk", (128, 2, 128))
        natf = s("f_natf", (128, 4, 128))

        def ternarize_T(w_ap, wi, nrow, ncol, dstT):
            kk = [0]
            for ob in range(nrow // 128):
                E_dma("sync", wtile[:, ob % 2, 0:ncol], w_ap[ts(ob, 128), :])
                for ib in range(ncol // 128):
                    r2 = ib % 2
                    E_act(sgn[:, r2, :], wtile[:, ob % 2, ts(ib, 128)], AF.Sign)
                    E_act(msk[:, r2, :], wtile[:, ob % 2, ts(ib, 128)], AF.Abs)
                    E_ts(msk[:, r2, :], msk[:, r2, :], ahalf[:, wi:wi + 1], OP.is_gt)
                    E_tt(natf[:, ib % 4, :], sgn[:, r2, :], msk[:, r2, :], OP.mult)
                    pmt = psum[kk[0] % 4][:, 0:128]
                    kk[0] += 1
                    E_tr(pmt, natf[:, ib % 4, :], ident_f[:])
                    E_copy(dstT[:, ib, ts(ob, 128)], pmt)

        xn4T = s("f_xn4T", (128, 8, L), BF16)
        xf = s("f_xf", (128, D))
        sq = s("f_sq", (128, D))
        for lt in range(NLT):
            E_dma("sync", xf[:], xres3[ts(lt, 128), :])
            ln_stats(xf, sq)
            var = stat[:, 1:2]
            t1c, t2c, vt = stat[:, 10:11], stat[:, 11:12], stat[:, 12:13]
            E_ts(t1c, var, 1.0, OP.mult, LN_EPS, OP.add)
            E_recip(t2c, t1c)
            E_tt(vt, var, t2c, OP.mult)
            E_recip(t1c, vt)
            E_act(t1c, t1c, AF.Sqrt)
            E_recip(t2c, t1c)
            E_ts(t2c, t2c, 1.0, OP.mult, 1e-8, OP.add)
            E_recip(t1c, t2c)
            E_tt(stat[:, 2:3], stat[:, 2:3], t1c, OP.mult)
            E_ts(sq[:, 0:D], xf, stat[:, 0:1], OP.subtract, stat[:, 2:3], OP.mult)
            for ib in range(8):
                E_tr(psum[6][:, 0:128], sq[:, ts(ib, 128)], ident_f[:])
                E_copy(xn4T[:, ib, ts(lt, 128)], psum[6][:, 0:128])
        LS = 256
        with nc.sbuf_tensor("f_upT", [128, 8, F1], BF16) as w_upT, \
             nc.sbuf_tensor("f_hg", [128, 32, LS], F32) as hg, \
             nc.sbuf_tensor("f_h2b", [128, 2, LS], BF16) as h2b, \
             nc.sbuf_tensor("f_hsc", [128, 2, LS], F32) as hsc, \
             nc.sbuf_tensor("f_hnb", [128, LS], BF16) as hnb, \
             nc.sbuf_tensor("f_r2r", [1, LS], F32) as r2r, \
             nc.sbuf_tensor("f_srt", [1, 3 * LS], F32) as srt:
            ternarize_T(w_up, 0, F1, D, w_upT)
            for lsl in range(L // LS):
                ph2 = psum[4][0:1, 0:LS]
                for ft in range(32):
                    pm = psum[ft % 2][:, 0:LS]
                    for ib in range(8):
                        E_mm(pm, w_upT[:, ib, ts(ft, 128)], xn4T[:, ib, ts(lsl, LS)],
                             start=(ib == 0), stop=(ib == 7))
                    E_act(hg[:, ft, :], pm, AF.Gelu)
                    E_tt(hsc[:, ft % 2, :], hg[:, ft, :], hg[:, ft, :], OP.mult)
                    E_copy(h2b[:, ft % 2, :], hsc[:, ft % 2, :])
                    E_mm(ph2, ones_cb[:], h2b[:, ft % 2, :], start=(ft == 0), stop=(ft == 31))
                E_copy(r2r[:], ph2)
                E_ts(r2r[:], r2r[:], 1.0 / float(F1), OP.mult)
                E_act(srt[:, 0:LS], r2r[:], AF.Sqrt)
                E_recip(srt[:, LS:2 * LS], srt[:, 0:LS])
                E_tt(srt[:, 2 * LS:3 * LS], srt[:, LS:2 * LS], r2r[:], OP.mult)
                E_tt(srt[:, 0:LS], srt[:, 0:LS], srt[:, 2 * LS:3 * LS], OP.add)
                E_ts(srt[:, 0:LS], srt[:, 0:LS], 0.5, OP.mult, 1e-8, OP.add)
                E_recip(r2r[:], srt[:, 0:LS])
                pmb = psum[5][:, 0:LS]
                E_mm(pmb, ones1_f[:], r2r[:], start=True, stop=True)
                for ft in range(32):
                    E_tt(hnb[:], hg[:, ft, :], pmb, OP.mult)
                    E_dma("sync", hn_s[ft, :, ts(lsl, LS)], hnb[:])
        prog.barrier()
        with nc.sbuf_tensor("f_dnT", [128, 32, D], BF16) as w_dnT, \
             nc.sbuf_tensor("f_hn", [128, 2, 32, 128], BF16) as hnt, \
             nc.sbuf_tensor("f_o", [128, 512], F32) as of:
            ternarize_T(w_down, 1, D, F1, w_dnT)
            for lt in range(NLT):
                for ft in range(32):
                    E_dma("sync", hnt[:, lt % 2, ft, :], hn_s[ft, :, ts(lt, 128)])
                for dh in range(2):
                    pm = psum[2 + dh][:, 0:512]
                    for ft in range(32):
                        E_mm(pm, hnt[:, lt % 2, ft, :], w_dnT[:, ft, ts(dh, 512)],
                             start=(ft == 0), stop=(ft == 31))
                    E_dma("sync", xf[:, 0:512], xres3[ts(lt, 128), ts(dh, 512)])
                    E_tt(of[:], xf[:, 0:512], pm, OP.add)
                    E_dma("sync", y_out[ts(lt, 128), ts(dh, 512)], of[:])
        es.close()
        prog.barrier()

    mamba_block(0)
    l0_tail()
    moe_block()
    mamba_block(1)
    l1_ffn()

    for nm, src_t, shape, dt in (("x1", xres1, (L, D), F32), ("x2", xres2, (L, D), F32),
                                 ("x3", xres3, (L, D), F32),
                                 ("ycat", ycat_s, (32, 128, L), BF16)):
        tt = tap(nm, shape, dt)
        if tt is not None:
            E_dma("gpsimd", tt, src_t)

    final_cnt = prog.finalize()

    sem_stack = ExitStack()
    sems = {e: sem_stack.enter_context(nc.semaphore("sem_" + e)) for e in ENGINES}
    with nc.Block() as block:
        def runner(eng_name):
            def body(eng):
                for fn, waits, inc in prog.ops[eng_name]:
                    for ch, val in waits.items():
                        eng.wait_ge(sems[ch], val)
                    ins = fn(eng)
                    ins.then_inc(sems[eng_name], inc)
                if eng_name == "sync":
                    for ch in ENGINES:
                        if final_cnt[ch]:
                            eng.wait_ge(sems[ch], final_cnt[ch])
            return body

        for en in ENGINES:
            getattr(block, en)(runner(en))
    sem_stack.close()

    stack.close()
    return nc


def _consts():
    oh = np.zeros((16, 16 * 128), dtype=np.float32)
    for n in range(16):
        oh[n, n * 128:(n + 1) * 128] = 1.0
    return {
        "identf": np.eye(128, dtype=np.float32),
        "identb": np.eye(128).astype(ml_dtypes.bfloat16),
        "oh16f": oh,
        "iota_cr": np.tile(np.arange(1, CAP + 1, dtype=np.float32), (128, 1)).copy(),
        "iota_cc": np.stack([np.arange(1, 129, dtype=np.float32),
                             np.arange(129, 257, dtype=np.float32)], axis=1).copy(),
        "ones1f": np.ones((1, 128), dtype=np.float32),
        "onescb": np.ones((128, 1)).astype(ml_dtypes.bfloat16),
    }


def _in_map(inputs, core):
    a = np.ascontiguousarray
    m = {
        "x": a(inputs["L0_attn_x"][core]),
        "W_in": a(inputs["L0_attn_W_in"]),
        "conv_w": a(inputs["L0_attn_conv_w"].reshape(16, 128, 4).transpose(1, 0, 2)),
        "conv_b": a(inputs["L0_attn_conv_b"].reshape(16, 128).T),
        "W_x": a(inputs["L0_attn_W_x"]),
        "W_dt": a(inputs["L0_attn_W_dt"].reshape(16, 128, R).transpose(1, 0, 2)),
        "b_dt": a(inputs["L0_attn_b_dt"].reshape(16, 128).T),
        "A_log": a(inputs["L0_attn_A_log"].reshape(16, 128, NS).transpose(1, 0, 2)),
        "Dp": a(inputs["L0_attn_Dp"].reshape(16, 128).T),
        "W_out1": a(inputs["L0_attn_W_out1"]),
        "W_out2": a(inputs["L0_attn_W_out2"]),
        "router_w1": a(inputs["L0_ffn_router_w1"].reshape(8, 128, RH).transpose(1, 0, 2)),
        "router_w2": a(inputs["L0_ffn_router_w2"]),
        "moe_w1": a(inputs["L0_ffn_w1"]),
        "moe_w2": a(inputs["L0_ffn_w2"]),
        "moe_w3": a(inputs["L0_ffn_w3"]),
        "l1_lnw": a(inputs["L1_attn_ln_w"][3]),
        "l1_lnb": a(inputs["L1_attn_ln_b"][3]),
        "l1_ip": a(inputs["L1_attn_in_proj_w"][3]),
        "l1_cw": a(inputs["L1_attn_conv_w"][3].reshape(16, 128, 4).transpose(1, 0, 2)),
        "l1_cb": a(inputs["L1_attn_conv_b"][3].reshape(16, 128).T),
        "l1_xp": a(inputs["L1_attn_x_proj_w"][3]),
        "l1_dtp": a(inputs["L1_attn_dt_proj_w"][3].reshape(16, 128, R).transpose(1, 0, 2)),
        "l1_dtb": a(inputs["L1_attn_dt_proj_b"][3].reshape(16, 128).T),
        "l1_Alog": a(inputs["L1_attn_A_log"][3].reshape(16, 128, NS).transpose(1, 0, 2)),
        "l1_D": a(inputs["L1_attn_D"][3].reshape(16, 128).T),
        "l1_op": a(inputs["L1_attn_out_proj_w"][3]),
        "w_up": a(inputs["L1_ffn_w_up"]),
        "w_down": a(inputs["L1_ffn_w_down"]),
    }
    m.update(_consts())
    return m


def _run(inputs, taps=()):
    nc = build_program(taps=taps)
    in_maps = [_in_map(inputs, core) for core in range(8)]
    res = run_bass_kernel_spmd(nc, in_maps, core_ids=list(range(8)))
    return res


def kernel(**inputs):
    res = _run(inputs)
    out = np.stack([np.asarray(r["y"], dtype=np.float32) for r in res.results], axis=0)
    return out
